# version 01
# baseline (speedup 1.0000x reference)
import numpy as np
import ml_dtypes
import concourse.bass as bass
import concourse.mybir as mybir
from concourse.bass_utils import run_bass_kernel_spmd

F32 = mybir.dt.float32
BF16 = mybir.dt.bfloat16
I32 = mybir.dt.int32
OP = mybir.AluOpType
AF = mybir.ActivationFunctionType
AX = mybir.AxisListType

DM = 2048
DIN = 4096
DST = 128
H = 64
PH = 64
CONVD = 4352
DIP = 8512
TOK = 2048
TOKH = 2051
CH = 128
NCH = 16
IF = 1408
NG = 8
E = 64
NEG = -1.0e30
EPS = 1e-5
NM_Z = 32
NM_XBC = 34
M_DT = 66
NM = 67

STAGE = 99
P1CAP = 999


class MultiCtx:

    def __init__(self, *cms):
        self.cms = cms

    def __enter__(self):
        from contextlib import ExitStack
        self.st = ExitStack()
        return tuple(self.st.enter_context(c) for c in self.cms)

    def __exit__(self, *a):
        return self.st.__exit__(*a)


class Sem:
    def __init__(self, nc, name):
        self.h = nc.alloc_semaphore(name)
        self.n = 0

    def inc(self, instr, k=1):
        instr.then_inc(self.h, k)
        self.n += k
        return instr


def build_nc(dbg=()):
    nc = bass.Bass("TRN2", target_bir_lowering=False,
                   detect_race_conditions=False)

    d_uT = nc.dram_tensor("uT", [DM, TOKH], F32, kind="ExternalInput")
    d_winT = nc.dram_tensor("winT", [DM, DIP], F32, kind="ExternalInput")
    d_convw = nc.dram_tensor("convw", [CONVD, 4], F32, kind="ExternalInput")
    d_convb = nc.dram_tensor("convb", [CONVD], F32, kind="ExternalInput")
    d_dtbias = nc.dram_tensor("dtbias", [H], F32, kind="ExternalInput")
    d_alog = nc.dram_tensor("alog", [H], F32, kind="ExternalInput")
    d_drep = nc.dram_tensor("drep", [DIN], F32, kind="ExternalInput")
    d_nw = nc.dram_tensor("nw", [DIN], F32, kind="ExternalInput")
    d_woutT = nc.dram_tensor("woutT", [DIN, DM], F32, kind="ExternalInput")
    d_gatewT = nc.dram_tensor("gatewT", [DM, E], F32, kind="ExternalInput")
    d_invmap = nc.dram_tensor("invmap", [E], I32, kind="ExternalInput")
    d_mupT = nc.dram_tensor("mupT", [DM, NG * 8], F32, kind="ExternalInput")
    d_gpT = nc.dram_tensor("gpT", [NG, DM, IF], F32, kind="ExternalInput")
    d_upT = nc.dram_tensor("upT", [NG, DM, IF], F32, kind="ExternalInput")
    d_dnT = nc.dram_tensor("dnT", [NG, IF, DM], F32, kind="ExternalInput")
    d_mq = nc.dram_tensor("mq", [5, 4], F32, kind="ExternalInput")
    d_ibf = nc.dram_tensor("ibf", [128, 128], BF16, kind="ExternalInput")
    d_i64 = nc.dram_tensor("i64", [64, 64], F32, kind="ExternalInput")
    d_i128 = nc.dram_tensor("i128", [128, 128], F32, kind="ExternalInput")
    d_onesb = nc.dram_tensor("onesb", [128, 1], BF16, kind="ExternalInput")
    d_ones1 = nc.dram_tensor("ones1", [1, 128], F32, kind="ExternalInput")

    d_out = nc.dram_tensor("out", [DM, TOK], F32, kind="ExternalOutput")

    d_zsc = nc.dram_tensor("zsc", [DIN, TOK], F32, kind="Internal")
    d_xsc = nc.dram_tensor("xsc", [NCH, DIN, CH], BF16, kind="Internal")
    d_xdsc = nc.dram_tensor("xdsc", [NCH, DIN, CH], BF16, kind="Internal")
    d_scsc = nc.dram_tensor("scsc", [NCH, 128, DIN], F32, kind="Internal")
    d_acrow = nc.dram_tensor("acrow", [NCH, H * CH], F32, kind="Internal")
    d_dcrow = nc.dram_tensor("dcrow", [NCH, H], F32, kind="Internal")
    d_wrow = nc.dram_tensor("wrow", [4, H], F32, kind="Internal")
    d_xres = nc.dram_tensor("xres", [DM, TOK], F32, kind="Internal")
    d_y2 = nc.dram_tensor("y2", [DM, TOK], F32, kind="Internal")
    d_woutbf = nc.dram_tensor("woutbf", [DIN, DM], BF16, kind="Internal")
    d_agin = nc.dram_tensor("agin", [128 * DIN + 64], F32, kind="Internal")
    d_agout = nc.dram_tensor("agout", [4, 128 * DIN + 64], F32,
                             kind="Internal")
    d_dtsc = nc.dram_tensor("dtsc", [2, H, TOK], F32, kind="Internal")
    d_lnrow = nc.dram_tensor("lnrow", [2, TOK], F32, kind="Internal")
    d_murs = nc.dram_tensor("murs", [2, TOK], F32, kind="Internal")
    d_soft8 = nc.dram_tensor("soft8", [8, NG, TOK], BF16, kind="Internal")
    d_scrow = nc.dram_tensor("scrow", [NG, TOK], F32, kind="Internal")

    dbg_spec = dict(
        z=([DIN, TOK], F32), x=([NCH, DIN, CH], BF16), dt=([H, TOK], F32),
        acum=([H, TOK], F32), sc=([NCH, 128, DIN], F32),
        slocal=([128, DIN], F32), carry=([128, DIN], F32),
        yfm=([128, DIN], F32), ynorm=([128, 32 * 512], BF16),
        xres=([DM, TOK], F32), xp1=([128, 16 * 2048], BF16),
        scoretm=([128, 16 * 64], F32), soft8=([8, NG, TOK], BF16),
        scrow=([NG, TOK], F32), y2=([DM, TOK], F32),
        wt=([128, 32 * CH], BF16), xtm=([128, DIN], BF16),
        yfm5=([128, DIN], F32), woutbf=([DIN, DM], BF16),
        prev5=([128, DIN], BF16), a0=([128, 11 * 2048], BF16),
        gg0=([128, 2048], F32), wpt0=([8, 11 * 128], BF16),
        sgb0=([128, 2048], F32),
    )
    d_dbg = {}
    for name in dbg:
        shape, dt_ = dbg_spec[name]
        d_dbg[name] = nc.dram_tensor("dbg_" + name, shape, dt_,
                                     kind="ExternalOutput")

    ds = Sem(nc, "ds")
    ws = Sem(nc, "ws")
    ps = Sem(nc, "ps")
    es = Sem(nc, "es")
    cs = Sem(nc, "cs")
    zs = Sem(nc, "zs")
    ss = Sem(nc, "ss")
    cc = Sem(nc, "cc")
    ya = Sem(nc, "ya")
    su = Sem(nc, "su")
    ws1 = Sem(nc, "ws1")
    sz0 = Sem(nc, "sz0")
    sz1 = Sem(nc, "sz1")
    sx0 = Sem(nc, "sx0")
    sx1 = Sem(nc, "sx1")
    srx = Sem(nc, "srx")
    wsp = (ws, ws1)

    def bar():
        nc.all_engine_barrier()

    def dma(q, out, in_, sem=ds, **kw):
        sem.inc(q.dma_start(out=out, in_=in_, **kw), 16)

    def dwait(eng=None):
        (eng or nc.sync).wait_ge(ds.h, ds.n)

    def dump(name, src):
        if name in d_dbg:
            dma(nc.sync, d_dbg[name].ap(), src)
            dwait()
            bar()

    sb = nc.alloc_sbuf_tensor
    s_convw = sb("s_convw", [128, NM_XBC * 4], F32)
    s_convb = sb("s_convb", [128, NM_XBC], F32)
    s_dtbias = sb("s_dtbias", [H, 1], F32)
    s_acol = sb("s_acol", [H, 1], F32)
    s_drep = sb("s_drep", [128, 32], F32)
    s_nw = sb("s_nw", [128, 32], F32)
    s_ibf = sb("s_ibf", [128, 128], BF16)
    s_i64 = sb("s_i64", [64, 64], F32)
    s_i128 = sb("s_i128", [128, 128], F32)
    s_onesb = sb("s_onesb", [128, 1], BF16)
    s_ones1 = sb("s_ones1", [1, 128], F32)
    s_mq = sb("s_mq", [5, 4], F32)
    s_atote = sb("s_atote", [5, H], F32)
    s_mgT = sb("s_mgT", [128, 16 * NG * 8], BF16)
    s_gateT = sb("s_gateT", [128, 16 * E], BF16)
    s_PT = sb("s_PT", [E, E], F32)
    s_tmp64 = sb("s_tmp64", [H, CH], F32)
    s_tmp64b = sb("s_tmp64b", [H, CH], F32)

    q = nc.sync
    dma(q, s_convw[:].rearrange("p (m t) -> p m t", t=4),
        d_convw.ap().rearrange("(m p) t -> p m t", p=128))
    dma(q, s_convb[:], d_convb.ap().rearrange("(m p) -> p m", p=128),
        allow_slow_non_contiguous=True)
    dma(q, s_dtbias[:], d_dtbias.ap().rearrange("(h o) -> h o", o=1))
    dma(q, s_acol[:], d_alog.ap().rearrange("(h o) -> h o", o=1))
    dma(q, s_drep[:], d_drep.ap().rearrange("(m p) -> p m", p=128),
        allow_slow_non_contiguous=True)
    dma(q, s_nw[:], d_nw.ap().rearrange("(m p) -> p m", p=128),
        allow_slow_non_contiguous=True)
    dma(q, s_ibf[:], d_ibf.ap())
    dma(q, s_i64[:], d_i64.ap())
    dma(q, s_i128[:], d_i128.ap())
    dma(q, s_onesb[:], d_onesb.ap())
    dma(q, s_ones1[:], d_ones1.ap())
    dma(q, s_mq[:], d_mq.ap())
    dma(nc.gpsimd, s_mgT[:].rearrange("p (k e) -> p k e", e=NG * 8),
        d_mupT.ap().rearrange("(k p) e -> p k e", p=128), sem=ws)
    dma(nc.gpsimd, s_gateT[:].rearrange("p (k e) -> p k e", e=E),
        d_gatewT.ap().rearrange("(k p) e -> p k e", p=128), sem=ws)
    with MultiCtx(
        nc.sbuf_tensor("s_invi", [E, 1], I32),
        nc.sbuf_tensor("s_invf", [E, 1], F32),
        nc.sbuf_tensor("s_iotac", [E, 1], F32),
        nc.sbuf_tensor("s_row0", [1, E], F32),
        nc.psum_tensor("pp0", [64, 64], F32),
    ) as (s_invi, s_invf, s_iotac, s_row0, pp0):
        dma(q, s_invi[:], d_invmap.ap().rearrange("(e o) -> e o", o=1))
        dwait()
        bar()
        nc.vector.tensor_copy(s_invf[:], s_invi[:])
        nc.gpsimd.iota(s_iotac[:], [[1, 1]], base=0, channel_multiplier=1,
                       allow_small_or_imprecise_dtypes=True)
        bar()
        nc.tensor.matmul(pp0[0:1, 0:64], s_invf[:], s_i64[:], start=True,
                         stop=True)
        bar()
        nc.vector.tensor_copy(s_row0[:], pp0[0:1, 0:64])
        bar()
        nc.tensor.matmul(pp0[0:64, 0:64], s_ones1[0:1, 0:64], s_row0[:],
                         start=True, stop=True)
        bar()
        nc.vector.tensor_scalar(s_PT[:], pp0[0:64, 0:64], s_iotac[:], None,
                                OP.is_equal)
        bar()

    with MultiCtx(
        nc.sbuf_tensor("s_Bbf", [128, TOK], BF16),
        nc.sbuf_tensor("s_Cbf", [128, TOK], BF16),
        nc.sbuf_tensor("s_colt", [H, TOK], F32),
        nc.sbuf_tensor("s_acum", [H, TOK], F32),
        nc.sbuf_tensor("s_asum", [H, NCH], F32),
        nc.sbuf_tensor("s_dcb", [128, NCH * H], F32),
    ) as (s_Bbf, s_Cbf, s_colt, s_acum, s_asum, s_dcb):
        if STAGE >= 1:
            with MultiCtx(
                nc.sbuf_tensor("suT", [128, 16 * TOKH], BF16),
                nc.sbuf_tensor("wb0", [128, 16 * 128], BF16),
                nc.sbuf_tensor("wb1", [128, 16 * 128], BF16),
                nc.sbuf_tensor("xbc0", [128, TOKH], F32),
                nc.sbuf_tensor("cv0", [128, TOK], F32),
                nc.sbuf_tensor("xr0", [128, TOK], BF16),
                nc.sbuf_tensor("xr1", [128, TOK], BF16),
                nc.sbuf_tensor("xd0", [128, TOK], BF16),
                nc.sbuf_tensor("xd1", [128, TOK], BF16),
                nc.sbuf_tensor("zb0", [128, TOK], F32),
                nc.sbuf_tensor("zb1", [128, TOK], F32),
                nc.sbuf_tensor("sdtev", [H, TOK], F32),
                nc.psum_tensor("pA", [128, 2048], F32),
                nc.psum_tensor("pH", [128, 8], F32),
            ) as (suT, wb0, wb1, xbc0, cv0, xr0, xr1, xd0, xd1, zb0, zb1,
                  sdtev, pA, pH):
                for k in range(16):
                    dma(nc.gpsimd, suT[:, k * TOKH:(k + 1) * TOKH],
                        d_uT.ap().rearrange("(k p) t -> k p t", p=128)[k],
                        sem=su)
                wview = d_winT.ap().rearrange("(k p) m -> p k m", p=128)

                def wload(mi, buf):
                    wdt = min(128, DIP - mi * 128)
                    sm = wsp[mi % 2]
                    dma(nc.gpsimd,
                        buf[:].rearrange("p (k m) -> p k m", m=128)
                        [:, :, 0:wdt],
                        wview[:, :, mi * 128:mi * 128 + wdt], sem=sm)
                    return (sm, sm.n)

                wmk = []
                wmk.append(wload(0, wb0))
                wbufs = (wb0, wb1)
                xrs, xds, zbs = (xr0, xr1), (xd0, xd1), (zb0, zb1)
                guard = {}

                for mi in range(min(NM, P1CAP)):
                    par = mi % 2
                    wb = wbufs[par]
                    rows = 64 if mi == M_DT else 128
                    if mi + 1 < NM:
                        nc.gpsimd.wait_ge(ps.h, mi)
                        wmk.append(wload(mi + 1, wbufs[(mi + 1) % 2]))
                    if mi == 0:
                        nc.tensor.wait_ge(su.h, su.n)
                    nc.tensor.wait_ge(wmk[mi][0].h, wmk[mi][1])
                    nc.tensor.wait_ge(es.h, mi)
                    is_xbc = NM_Z <= mi < NM_Z + NM_XBC
                    for t in range(4):
                        for k in range(16):
                            _mm = nc.tensor.matmul(
                                pA[0:rows, t * 512:(t + 1) * 512],
                                wb[:, k * 128:k * 128 + rows],
                                suT[:, k * TOKH + 3 + t * 512:
                                    k * TOKH + 3 + (t + 1) * 512],
                                start=(k == 0), stop=(k == 15))
                    if is_xbc:
                        for k in range(16):
                            _mm = nc.tensor.matmul(
                                pH[:, 0:3], wb[:, k * 128:(k + 1) * 128],
                                suT[:, k * TOKH:k * TOKH + 3],
                                start=(k == 0), stop=(k == 15))
                    ps.inc(_mm)

                    if mi < NM_Z:
                        zb = zbs[par]
                        szp = (sz0, sz1)[par]
                        nc.scalar.wait_ge(ps.h, mi + 1)
                        nc.scalar.wait_ge(szp.h, szp.n)
                        es.inc(nc.scalar.activation(zb[:], pA[:], AF.Silu))
                        nc.sync.wait_ge(es.h, es.n)
                        dma(nc.sync, d_zsc[mi * 128:(mi + 1) * 128, :], zb[:],
                            sem=szp)
                    elif is_xbc:
                        ci = mi - NM_Z
                        nc.vector.wait_ge(ps.h, mi + 1)
                        nc.vector.tensor_copy(xbc0[:, 0:3], pH[:, 0:3])
                        es.inc(nc.vector.tensor_copy(xbc0[:, 3:2051], pA[:]))
                        nc.vector.tensor_scalar(
                            cv0[:], xbc0[:, 0:2048],
                            s_convw[:, ci * 4:ci * 4 + 1],
                            s_convb[:, ci:ci + 1], OP.mult, OP.add)
                        for w in range(1, 4):
                            _cv = nc.vector.scalar_tensor_tensor(
                                cv0[:], xbc0[:, w:w + 2048],
                                s_convw[:, ci * 4 + w:ci * 4 + w + 1], cv0[:],
                                OP.mult, OP.add)
                        cs.inc(_cv)
                        nc.scalar.wait_ge(cs.h, cs.n)
                        if ci < 32:
                            xrb, xdb = xrs[par], xds[par]
                            sxq = (sx0, sx1)[par]
                            nc.scalar.wait_ge(sxq.h, sxq.n)
                            zs.inc(nc.scalar.activation(xrb[:], cv0[:],
                                                         AF.Silu))
                            nc.vector.wait_ge(zs.h, zs.n)
                            zs.inc(nc.vector.tensor_scalar_mul(
                                xdb[:], xrb[:], s_drep[:, ci:ci + 1]))
                            nc.sync.wait_ge(zs.h, zs.n)
                            nc.scalar.wait_ge(zs.h, zs.n)
                            dma(nc.sync,
                                d_xsc[:, ci * 128:(ci + 1) * 128, :]
                                .transpose([1, 0, 2]),
                                xrb[:].rearrange("p (c t) -> p c t", t=128),
                                sem=sxq)
                            dma(nc.scalar,
                                d_xdsc[:, ci * 128:(ci + 1) * 128, :]
                                .transpose([1, 0, 2]),
                                xdb[:].rearrange("p (c t) -> p c t", t=128),
                                sem=sxq)
                        elif ci == 32:
                            zs.inc(nc.scalar.activation(s_Bbf[:], cv0[:],
                                                        AF.Silu))
                        else:
                            zs.inc(nc.scalar.activation(s_Cbf[:], cv0[:],
                                                        AF.Silu))
                    else:
                        nc.vector.wait_ge(ps.h, mi + 1)
                        es.inc(nc.vector.tensor_copy(sdtev[:], pA[0:64, :]))
                for smx in (sz0, sz1, sx0, sx1, ss):
                    nc.sync.wait_ge(smx.h, smx.n)
                bar()
                if P1CAP >= NM:
                    dma(nc.sync, d_dtsc[0], sdtev[:])
                    dwait()
                bar()

        if STAGE >= 2:
            with MultiCtx(
                nc.sbuf_tensor("p2raw", [H, TOK], F32),
                nc.sbuf_tensor("p2dt", [H, TOK], F32),
            ) as (p2raw, p2dt):
                dma(nc.sync, p2raw[:], d_dtsc[0])
                dwait()
                bar()
                nc.scalar.activation(p2dt[:], p2raw[:], AF.Abs,
                                     bias=s_dtbias[:])
                bar()
                nc.scalar.activation(p2dt[:], p2dt[:], AF.Exp, scale=-1.0)
                bar()
                nc.scalar.activation(p2dt[:], p2dt[:], AF.Ln, bias=1.0)
                nc.vector.tensor_scalar(p2raw[:], p2raw[:], s_dtbias[:], 0.0,
                                        OP.add, OP.max)
                bar()
                nc.vector.tensor_add(p2dt[:], p2dt[:], p2raw[:])
                bar()
                nc.scalar.activation(s_colt[:], p2dt[:], AF.Ln)
                nc.scalar.activation(s_acol[:], s_acol[:], AF.Exp)
                bar()
                dma(nc.sync, d_dtsc[1], p2dt[:])
                nc.vector.tensor_scalar_mul(s_acol[:], s_acol[:], -1.0)
                bar()
                nc.vector.tensor_scalar_mul(p2raw[:], p2dt[:], s_acol[:])
                bar()
                for c in range(NCH):
                    nc.vector.tensor_tensor_scan(
                        s_acum[:, c * CH:(c + 1) * CH],
                        p2raw[:, c * CH:(c + 1) * CH],
                        p2raw[:, c * CH:(c + 1) * CH], 0.0, OP.add, OP.bypass)
                bar()
                nc.vector.tensor_sub(s_colt[:], s_colt[:], s_acum[:])
                nc.vector.tensor_copy(
                    s_asum[:],
                    s_acum[:].rearrange("h (c t) -> h c t", t=CH)
                    [:, :, CH - 1])
                bar()
                nc.scalar.activation(s_tmp64[:, 0:NCH], s_asum[:], AF.Exp)
                bar()
                with nc.psum_tensor("pdc", [64, 64], F32) as pdc:
                    nc.tensor.matmul(pdc[0:NCH, 0:64], s_tmp64[:, 0:NCH],
                                     s_i64[:], start=True, stop=True)
                    bar()
                    nc.vector.tensor_copy(s_tmp64b[0:NCH, 0:64],
                                          pdc[0:NCH, 0:64])
                    bar()
                dma(nc.sync, d_dcrow.ap(), s_tmp64b[0:NCH, 0:64])
                for c in range(NCH):
                    dma(nc.sync, d_acrow[c].rearrange("(h l) -> h l", h=H),
                        s_acum[:, c * CH:(c + 1) * CH])
                dwait()
                bar()
                for c in range(NCH):
                    dma(nc.sync, s_dcb[:, c * H:(c + 1) * H],
                        d_dcrow[c:c + 1, :].broadcast_to([128, H]))
                dwait()
                bar()
                dump("dt", p2dt[:])
            dump("acum", s_acum[:])
            dump("z", d_zsc.ap())
            dump("x", d_xsc.ap())

        _s34 = MultiCtx(
            nc.sbuf_tensor("s_R", [128, DIN], F32),
            nc.sbuf_tensor("s_prevbf", [128, DIN], BF16),
            nc.sbuf_tensor("s_yng", [128, 32 * 512], BF16),
        )
        s_R, s_prevbf, s_yng = _s34.__enter__()

        if STAGE >= 3:
            with MultiCtx(
                nc.sbuf_tensor("xtm0", [128, DIN], BF16),
                nc.sbuf_tensor("xtm1", [128, DIN], BF16),
                nc.sbuf_tensor("sxd", [128, DIN], BF16),
                nc.sbuf_tensor("sBt", [128, 128], BF16),
                nc.sbuf_tensor("sdtd", [128, 64], BF16),
                nc.sbuf_tensor("sSc", [128, DIN], F32),
                nc.sbuf_tensor("p3dt", [H, TOK], F32),
                nc.psum_tensor("p3a", [128, 2048], F32),
                nc.psum_tensor("p3s", [128, 128], F32),
                nc.psum_tensor("p3w", [128, 64], F32),
            ) as (xtm0, xtm1, sxd, sBt, sdtd, sSc, p3dt, p3a, p3s, p3w):
                dma(nc.sync, p3dt[:], d_dtsc[1])
                nc.vector.memset(s_R[:], 0.0)
                xtms = (xtm0, xtm1)
                dma(nc.sync, xtm0[:], d_xsc[0], transpose=True)
                dwait()
                bar()
                dump("xtm", xtm0[:])
                for c in range(NCH):
                    if c + 1 < NCH:
                        dma(nc.sync, xtms[(c + 1) % 2][:], d_xsc[c + 1],
                            transpose=True)
                    xtm = xtms[c % 2]
                    nc.tensor.matmul(p3s[:], s_Bbf[:, c * CH:(c + 1) * CH],
                                     s_ibf[:], start=True, stop=True)
                    nc.scalar.activation(
                        s_tmp64[:], s_acum[:, c * CH:(c + 1) * CH], AF.Exp,
                        scale=-1.0,
                        bias=s_acum[:, c * CH + CH - 1:c * CH + CH])
                    bar()
                    nc.vector.tensor_mul(s_tmp64[:], s_tmp64[:],
                                         p3dt[:, c * CH:(c + 1) * CH])
                    nc.scalar.copy(sBt[:], p3s[:])
                    bar()
                    nc.tensor.matmul(p3w[:], s_tmp64[:], s_i64[:], start=True,
                                     stop=True)
                    bar()
                    nc.vector.tensor_copy(sdtd[:], p3w[:])
                    bar()
                    nc.vector.tensor_tensor(
                        sxd[:].rearrange("s (h p) -> s h p", p=PH),
                        xtm[:].rearrange("s (h p) -> s h p", p=PH),
                        sdtd[:].rearrange("s (h o) -> s h o", o=1)
                        .broadcast_to([128, H, PH]), OP.mult)
                    bar()
                    for half in range(2):
                        for j in range(4):
                            nc.tensor.matmul(
                                p3a[:, j * 512:(j + 1) * 512], sBt[:],
                                sxd[:, half * 2048 + j * 512:
                                    half * 2048 + (j + 1) * 512],
                                start=True, stop=True)
                        bar()
                        nc.scalar.copy(sSc[:, half * 2048:(half + 1) * 2048],
                                       p3a[:])
                        bar()
                    dma(nc.sync, d_scsc[c], sSc[:], sem=ss)
                    nc.vector.tensor_tensor(
                        s_R[:].rearrange("n (h p) -> n h p", p=PH),
                        s_R[:].rearrange("n (h p) -> n h p", p=PH),
                        s_dcb[:, c * H:(c + 1) * H]
                        .rearrange("n (h o) -> n h o", o=1)
                        .broadcast_to([128, H, PH]), OP.mult)
                    bar()
                    nc.vector.tensor_add(s_R[:], s_R[:], sSc[:])
                    bar()
                nc.vector.reduce_sum(s_tmp64[:, 0:1], s_asum[:], AX.X)
                nc.sync.wait_ge(ss.h, ss.n)
                bar()
                dump("sc", d_scsc.ap())
                dump("slocal", s_R[:])
                dma(nc.sync,
                    d_agin.ap()[0:128 * DIN].rearrange("(p f) -> p f", p=128),
                    s_R[:])
                dma(nc.sync,
                    d_agin.ap()[128 * DIN:].rearrange("(h o) -> h o", o=1),
                    s_tmp64[:, 0:1])
                dwait()
                bar()
                cc.inc(nc.gpsimd.collective_compute(
                    "AllGather", OP.bypass,
                    replica_groups=[[0, 1, 2, 3], [4, 5, 6, 7]],
                    ins=[d_agin.ap()], outs=[d_agout.ap()]), 1)
                nc.gpsimd.wait_ge(cc.h, cc.n)
                bar()

        if STAGE >= 4:
            with MultiCtx(
                nc.sbuf_tensor("sSg", [128, DIN], F32),
                nc.sbuf_tensor("sWb", [128, 4 * 64], F32),
                nc.psum_tensor("p4w", [64, 64], F32),
            ) as (sSg, sWb, p4w):
                nc.vector.memset(s_atote[:], 1.0)
                bar()
                for p in range(4):
                    dma(nc.sync, s_atote[p:p + 1, :],
                        d_agout[p, 128 * DIN:].rearrange("(o h) -> o h", o=1))
                dwait()
                bar()
                nc.tensor.matmul(p4w[0:64, 0:4], s_atote[:], s_mq.ap(),
                                 start=True, stop=True)
                bar()
                nc.scalar.activation(s_tmp64[:, 0:4], p4w[0:64, 0:4], AF.Exp)
                bar()
                nc.tensor.matmul(p4w[0:4, 0:64], s_tmp64[:, 0:4], s_i64[:],
                                 start=True, stop=True)
                bar()
                nc.vector.tensor_copy(s_tmp64b[0:4, 0:64], p4w[0:4, 0:64])
                bar()
                dma(nc.sync, d_wrow.ap(), s_tmp64b[0:4, 0:64])
                dwait()
                bar()
                for p in range(4):
                    dma(nc.sync, sWb[:, p * 64:(p + 1) * 64],
                        d_wrow[p:p + 1, :].broadcast_to([128, H]))
                dwait()
                bar()
                nc.vector.memset(s_R[:], 0.0)
                bar()
                for p in range(4):
                    dma(nc.sync, sSg[:],
                        d_agout[p, 0:128 * DIN]
                        .rearrange("(n f) -> n f", n=128))
                    dwait()
                    bar()
                    nc.vector.tensor_tensor(
                        sSg[:].rearrange("n (h p) -> n h p", p=PH),
                        sSg[:].rearrange("n (h p) -> n h p", p=PH),
                        sWb[:, p * 64:(p + 1) * 64]
                        .rearrange("n (h o) -> n h o", o=1)
                        .broadcast_to([128, H, PH]), OP.mult)
                    bar()
                    nc.vector.tensor_add(s_R[:], s_R[:], sSg[:])
                    bar()
                dump("carry", s_R[:])

            with MultiCtx(
                nc.sbuf_tensor("xtmA", [128, DIN], BF16),
                nc.sbuf_tensor("xtmB", [128, DIN], BF16),
                nc.sbuf_tensor("xdc", [128, DIN], BF16),
                nc.sbuf_tensor("srBh", [128, 32 * CH], F32),
                nc.sbuf_tensor("sLW", [128, 32 * CH], BF16),
                nc.sbuf_tensor("sGt", [128, 128], F32),
                nc.sbuf_tensor("scolT", [128, 64], F32),
                nc.sbuf_tensor("sevT", [128, 64], F32),
                nc.sbuf_tensor("sYtm", [128, DIN], BF16),
                nc.sbuf_tensor("sYo", [128, DIN], BF16),
                nc.sbuf_tensor("syfm", [128, DIN], F32),
                nc.sbuf_tensor("shalf", [128, 2048], F32),
                nc.sbuf_tensor("srs", [1, 128], F32),
                nc.sbuf_tensor("swog0", [128, 32 * 128], BF16),
                nc.sbuf_tensor("swog1", [128, 32 * 128], BF16),
                nc.sbuf_tensor("sxrg", [128, 512], F32),
                nc.sbuf_tensor("sxrb", [128, 512], BF16),
                nc.sbuf_tensor("sxrb2", [128, 512], BF16),
                nc.sbuf_tensor("srowt", [1, 512], F32),
                nc.psum_tensor("pYd", [128, 1024], F32),
                nc.psum_tensor("pYo", [128, 1024], F32),
                nc.psum_tensor("p4s", [128, 128], F32),
                nc.psum_tensor("p4r1", [1, 512], F32),
                nc.psum_tensor("p4r2", [1, 512], F32),
            ) as (xtmA, xtmB, xdc, srBh, sLW, sGt, scolT, sevT, sYtm, sYo,
                  syfm,
                  shalf, srs, swog0, swog1, sxrg, sxrb, sxrb2, srowt,
                  pYd, pYo, p4s, p4r1, p4r2):
                wov = d_woutT.ap().rearrange("(k p) m -> p k m", p=128)
                smk = []
                for m in range(16):
                    buf = (swog0, swog1)[m % 2]
                    wsm = wsp[m % 2]
                    if m >= 2:
                        nc.gpsimd.wait_ge(ss.h, smk[m - 2])
                    dma(nc.gpsimd, buf[:].rearrange("p (k m) -> p k m", m=128),
                        wov[:, :, m * 128:(m + 1) * 128], sem=wsm)
                    nc.vector.wait_ge(wsm.h, wsm.n)
                    zs.inc(nc.vector.tensor_tensor(
                        buf[:].rearrange("p (k m) -> p k m", m=128),
                        buf[:].rearrange("p (k m) -> p k m", m=128),
                        s_nw[:].rearrange("p (k o) -> p k o", o=1)
                        .broadcast_to([128, 32, 128]), OP.mult))
                    nc.sync.wait_ge(zs.h, zs.n)
                    dma(nc.sync,
                        d_woutbf[:, m * 128:(m + 1) * 128]
                        .rearrange("(k p) m -> p k m", p=128),
                        buf[:].rearrange("p (k m) -> p k m", m=128), sem=ss)
                    smk.append(ss.n)
                    nc.vector.wait_ge(ss.h, ss.n)
                nc.sync.wait_ge(ss.h, ss.n)
                bar()
                dump("woutbf", d_woutbf.ap())

                xtms = (xtmA, xtmB)
                dma(nc.sync, xtmA[:], d_xsc[0], transpose=True)
                dma(nc.scalar, xdc[:].rearrange("p (i t) -> p i t", t=128),
                    d_xdsc[0].rearrange("(i p) t -> p i t", p=128))
                dwait()
                dwait(nc.scalar)
                bar()
                for grp in range(4):
                    for c4 in range(4):
                        c = grp * 4 + c4
                        if c + 1 < NCH:
                            dma(nc.sync, xtms[(c + 1) % 2][:], d_xsc[c + 1],
                                transpose=True)
                        xtm = xtms[c % 2]
                        nc.scalar.copy(s_prevbf[:], s_R[:])
                        bar()
                        if "prev5" in d_dbg and c == 5:
                            dump("prev5", s_prevbf[:])
                        nc.vector.tensor_tensor(
                            s_R[:].rearrange("n (h p) -> n h p", p=PH),
                            s_R[:].rearrange("n (h p) -> n h p", p=PH),
                            s_dcb[:, c * H:(c + 1) * H]
                            .rearrange("n (h o) -> n h o", o=1)
                            .broadcast_to([128, H, PH]), OP.mult)
                        nc.tensor.matmul(p4s[:, 0:64],
                                         s_colt[:, c * CH:(c + 1) * CH],
                                         s_i64[:], start=True, stop=True)
                        bar()
                        for half in range(2):
                            dma(nc.sync, shalf[:],
                                d_scsc[c][:, half * 2048:(half + 1) * 2048])
                            dwait()
                            bar()
                            nc.vector.tensor_add(
                                s_R[:, half * 2048:(half + 1) * 2048],
                                s_R[:, half * 2048:(half + 1) * 2048],
                                shalf[:])
                            bar()
                        nc.scalar.copy(scolT[:, 0:64], p4s[:, 0:64])
                        bar()
                        nc.scalar.activation(
                            s_tmp64[:], s_acum[:, c * CH:(c + 1) * CH],
                            AF.Exp)
                        bar()
                        nc.tensor.matmul(p4s[:, 0:64], s_tmp64[:], s_i64[:],
                                         start=True, stop=True)
                        bar()
                        nc.scalar.copy(sevT[:, 0:64], p4s[:, 0:64])
                        bar()
                        nc.tensor.matmul(p4s[:],
                                         s_Bbf[:, c * CH:(c + 1) * CH],
                                         s_Cbf[:, c * CH:(c + 1) * CH],
                                         start=True, stop=True)
                        bar()
                        nc.vector.tensor_copy(sGt[:], p4s[:])
                        bar()
                        for hf in range(2):
                            dma(nc.sync, srBh[:],
                                d_acrow[c:c + 1, hf * 32 * CH:
                                        (hf + 1) * 32 * CH]
                                .broadcast_to([128, 32 * CH]))
                            dwait()
                            bar()
                            nc.vector.tensor_tensor(
                                srBh[:].rearrange("s (h l) -> s h l", l=CH),
                                srBh[:].rearrange("s (h l) -> s h l", l=CH),
                                scolT[:, hf * 32:(hf + 1) * 32]
                                .rearrange("s (h o) -> s h o", o=1)
                                .broadcast_to([128, 32, CH]), OP.add)
                            bar()
                            nc.gpsimd.affine_select(
                                srBh[:], srBh[:], [[0, 32], [1, CH]],
                                OP.is_ge, NEG, base=0, channel_multiplier=-1)
                            bar()
                            nc.scalar.activation(srBh[:], srBh[:], AF.Exp)
                            bar()
                            nc.vector.tensor_tensor(
                                sLW[:].rearrange("s (h l) -> s h l", l=CH),
                                srBh[:].rearrange("s (h l) -> s h l", l=CH),
                                sGt[:].rearrange("s (o l) -> s o l", o=1)
                                .broadcast_to([128, 32, CH]), OP.mult)
                            bar()
                            if "wt" in d_dbg and c == 0 and hf == 0:
                                dump("wt", sLW[:])
                            for qq in range(2):
                                for hh in range(16):
                                    hq = qq * 16 + hh
                                    h = hf * 32 + hq
                                    nc.tensor.matmul(
                                        pYd[:, hh * 64:(hh + 1) * 64],
                                        sLW[:, hq * CH:(hq + 1) * CH],
                                        xtm[:, h * PH:(h + 1) * PH],
                                        start=True, stop=True)
                                    nc.tensor.matmul(
                                        pYo[:, hh * 64:(hh + 1) * 64],
                                        s_Cbf[:, c * CH:(c + 1) * CH],
                                        s_prevbf[:, h * PH:(h + 1) * PH],
                                        start=True, stop=True)
                                bar()
                                nc.scalar.copy(
                                    sYtm[:, (hf * 32 + qq * 16) * PH:
                                         (hf * 32 + qq * 16 + 16) * PH],
                                    pYd[:])
                                for hh in range(16):
                                    h = hf * 32 + qq * 16 + hh
                                    nc.scalar.activation(
                                        sYo[:, h * PH:(h + 1) * PH],
                                        pYo[:, hh * 64:(hh + 1) * 64],
                                        AF.Copy, scale=sevT[:, h:h + 1])
                                bar()
                        for qr in range(4):
                            for i8 in range(8):
                                i = qr * 8 + i8
                                nc.tensor.matmul(
                                    pYd[:, i8 * 128:(i8 + 1) * 128],
                                    sYtm[:, i * 128:(i + 1) * 128],
                                    s_ibf[:], start=True, stop=False)
                                nc.tensor.matmul(
                                    pYd[:, i8 * 128:(i8 + 1) * 128],
                                    sYo[:, i * 128:(i + 1) * 128],
                                    s_ibf[:], start=False, stop=True)
                            bar()
                            nc.vector.tensor_add(
                                syfm[:, qr * 1024:(qr + 1) * 1024], pYd[:],
                                xdc[:, qr * 1024:(qr + 1) * 1024])
                            bar()
                        if c + 1 < NCH:
                            dma(nc.scalar,
                                xdc[:].rearrange("p (i t) -> p i t", t=128),
                                d_xdsc[c + 1]
                                .rearrange("(i p) t -> p i t", p=128))
                        if "yfm" in d_dbg and c == 0:
                            dump("yfm", syfm[:])
                        if "yfm5" in d_dbg and c == 5:
                            dump("yfm5", syfm[:])
                        for half in range(2):
                            dma(nc.sync,
                                shalf[:].rearrange("p (i t) -> p i t", t=128),
                                d_zsc[half * 2048:(half + 1) * 2048,
                                      c * CH:(c + 1) * CH]
                                .rearrange("(i p) t -> p i t", p=128))
                            dwait()
                            bar()
                            nc.vector.tensor_mul(
                                syfm[:, half * 2048:(half + 1) * 2048],
                                syfm[:, half * 2048:(half + 1) * 2048],
                                shalf[:])
                            bar()
                        nc.scalar.activation(sLW[:], syfm[:], AF.Square)
                        bar()
                        for i in range(32):
                            nc.tensor.matmul(p4r1[0:1, 0:128], s_onesb[:],
                                             sLW[:, i * 128:(i + 1) * 128],
                                             start=(i == 0), stop=(i == 31))
                        bar()
                        nc.vector.tensor_scalar(srs[:], p4r1[0:1, 0:128],
                                                1.0 / DIN, EPS, OP.mult,
                                                OP.add)
                        bar()
                        nc.scalar.activation(srs[:], srs[:], AF.Sqrt)
                        bar()
                        nc.vector.reciprocal(srs[:], srs[:])
                        bar()
                        nc.tensor.matmul(p4s[:], s_ones1[:], srs[:],
                                         start=True, stop=True)
                        bar()
                        nc.vector.tensor_tensor(
                            s_yng[:].rearrange("p (i t) -> p i t", t=512)
                            [:, :, c4 * 128:(c4 + 1) * 128],
                            syfm[:].rearrange("p (i t) -> p i t", t=128),
                            p4s[:].rearrange("p (o t) -> p o t", o=1)
                            .broadcast_to([128, 32, 128]), OP.mult)
                        bar()
                    if "ynorm" in d_dbg and grp == 0:
                        dump("ynorm", s_yng[:])
                    wobv = d_woutbf.ap().rearrange("(k p) m -> p k m", p=128)
                    wmarks = []
                    dma(nc.gpsimd,
                        swog0[:].rearrange("p (k m) -> p k m", m=128),
                        wobv[:, :, 0:128], sem=wsp[0])
                    wmarks.append((wsp[0], wsp[0].n))
                    for m in range(16):
                        buf = (swog0, swog1)[m % 2]
                        if m + 1 < 16:
                            wsm = wsp[(m + 1) % 2]
                            dma(nc.gpsimd,
                                (swog0, swog1)[(m + 1) % 2][:]
                                .rearrange("p (k m) -> p k m", m=128),
                                wobv[:, :, (m + 1) * 128:(m + 2) * 128],
                                sem=wsm)
                            wmarks.append((wsm, wsm.n))
                        nc.tensor.wait_ge(wmarks[m][0].h, wmarks[m][1])
                        for k in range(32):
                            nc.tensor.matmul(
                                pYd[:, 0:512], buf[:, k * 128:(k + 1) * 128],
                                s_yng[:, k * 512:(k + 1) * 512],
                                start=(k == 0), stop=(k == 31))
                        nc.sync.wait_ge(srx.h, srx.n)
                        dma(nc.sync, sxrg[:],
                            d_uT[m * 128:(m + 1) * 128,
                                 3 + grp * 512:3 + (grp + 1) * 512])
                        dwait()
                        bar()
                        nc.vector.tensor_add(sxrg[:], sxrg[:], pYd[:, 0:512])
                        bar()
                        dma(nc.sync,
                            d_xres[m * 128:(m + 1) * 128,
                                   grp * 512:(grp + 1) * 512], sxrg[:],
                            sem=srx)
                        nc.scalar.activation(sxrb[:], sxrg[:], AF.Copy)
                        nc.scalar.activation(sxrb2[:], sxrg[:], AF.Square)
                        bar()
                        nc.tensor.matmul(p4r1[0:1, 0:512], s_onesb[:],
                                         sxrb[:], start=(m == 0),
                                         stop=(m == 15))
                        nc.tensor.matmul(p4r2[0:1, 0:512], s_onesb[:],
                                         sxrb2[:], start=(m == 0),
                                         stop=(m == 15))
                        bar()
                    nc.vector.tensor_copy(srowt[:], p4r1[0:1, 0:512])
                    bar()
                    dma(nc.sync, d_lnrow[0:1, grp * 512:(grp + 1) * 512],
                        srowt[:])
                    dwait()
                    bar()
                    nc.vector.tensor_copy(srowt[:], p4r2[0:1, 0:512])
                    bar()
                    dma(nc.sync, d_lnrow[1:2, grp * 512:(grp + 1) * 512],
                        srowt[:])
                    dwait()
                    bar()
                nc.sync.wait_ge(srx.h, srx.n)
                nc.sync.wait_ge(ss.h, ss.n)
                bar()
                dump("xres", d_xres.ap())

        _s34.__exit__(None, None, None)

    if STAGE >= 6:
        _p67 = MultiCtx(
            nc.sbuf_tensor("s_xp1", [128, 16 * 2048], BF16),
            nc.sbuf_tensor("s_mu1", [1, TOK], F32),
            nc.sbuf_tensor("s_rs1", [1, TOK], F32),
        )
        s_xp1, s_mu1, s_rs1 = _p67.__enter__()
        with MultiCtx(
            nc.sbuf_tensor("s_mu1b", [128, 2048], F32),
            nc.sbuf_tensor("s_rs1b", [128, 2048], F32),
            nc.sbuf_tensor("sxr", [128, 2048], F32),
            nc.sbuf_tensor("srowa", [1, 2048], F32),
            nc.sbuf_tensor("sg1", [64, 2048], F32),
            nc.sbuf_tensor("sltm", [128, 16 * 64], F32),
            nc.sbuf_tensor("gt0", [128, 64], F32),
            nc.sbuf_tensor("gt1", [128, 64], F32),
            nc.sbuf_tensor("gt2", [128, 8], F32),
            nc.sbuf_tensor("gt3", [128, 8], F32),
            nc.sbuf_tensor("gt4", [128, 8], F32),
            nc.sbuf_tensor("gcol", [128, 1], F32),
            nc.sbuf_tensor("gstg", [64, 128], BF16),
            nc.sbuf_tensor("gstgf", [64, 128], F32),
            nc.psum_tensor("p6", [128, 2048], F32),
            nc.psum_tensor("p6b", [64, 512], F32),
        ) as (s_mu1b, s_rs1b, sxr, srowa, sg1, sltm, gt0, gt1, gt2, gt3,
              gt4, gcol, gstg, gstgf, p6, p6b):
            dma(nc.sync, s_mu1[:], d_lnrow[0:1, :])
            dma(nc.sync, s_rs1[:], d_lnrow[1:2, :])
            dwait()
            bar()
            nc.vector.tensor_scalar_mul(s_mu1[:], s_mu1[:], 1.0 / DM)
            bar()
            nc.vector.tensor_scalar_mul(s_rs1[:], s_rs1[:], 1.0 / DM)
            bar()
            nc.vector.tensor_tensor(srowa[:], s_mu1[:], s_mu1[:], OP.mult)
            bar()
            nc.vector.tensor_sub(s_rs1[:], s_rs1[:], srowa[:])
            bar()
            nc.vector.tensor_scalar(s_rs1[:], s_rs1[:], EPS, None, OP.add)
            bar()
            nc.scalar.activation(s_rs1[:], s_rs1[:], AF.Sqrt)
            bar()
            nc.vector.reciprocal(s_rs1[:], s_rs1[:])
            bar()
            dma(nc.sync, d_murs[0:1, :], s_mu1[:])
            dma(nc.sync, d_murs[1:2, :], s_rs1[:])
            dwait()
            bar()
            for t in range(4):
                nc.tensor.matmul(p6[:, t * 512:(t + 1) * 512], s_ones1[:],
                                 s_mu1[0:1, t * 512:(t + 1) * 512],
                                 start=True, stop=True)
            bar()
            nc.scalar.copy(s_mu1b[:], p6[:])
            bar()
            for t in range(4):
                nc.tensor.matmul(p6[:, t * 512:(t + 1) * 512], s_ones1[:],
                                 s_rs1[0:1, t * 512:(t + 1) * 512],
                                 start=True, stop=True)
            bar()
            nc.scalar.copy(s_rs1b[:], p6[:])
            bar()
            for m in range(16):
                dma(nc.sync, sxr[:], d_xres[m * 128:(m + 1) * 128, :])
                dwait()
                bar()
                nc.vector.tensor_sub(sxr[:], sxr[:], s_mu1b[:])
                bar()
                nc.vector.tensor_tensor(s_xp1[:, m * 2048:(m + 1) * 2048],
                                        sxr[:], s_rs1b[:], OP.mult)
                bar()
            dump("xp1", s_xp1[:])
            for t in range(4):
                for k in range(16):
                    nc.tensor.matmul(
                        p6[0:64, t * 512:(t + 1) * 512],
                        s_gateT[:, k * 64:(k + 1) * 64],
                        s_xp1[:, k * 2048 + t * 512:k * 2048 + (t + 1) * 512],
                        start=(k == 0), stop=(k == 15))
            bar()
            nc.vector.tensor_copy(sg1[:], p6[0:64, :])
            bar()
            for t in range(4):
                nc.tensor.matmul(p6[0:64, t * 512:(t + 1) * 512], s_PT[:],
                                 sg1[:, t * 512:(t + 1) * 512],
                                 start=True, stop=True)
            bar()
            nc.vector.tensor_copy(sg1[:], p6[0:64, :])
            bar()
            for i in range(16):
                nc.tensor.matmul(p6[:, i * 64:(i + 1) * 64],
                                 sg1[:, i * 128:(i + 1) * 128], s_i64[:],
                                 start=True, stop=True)
            bar()
            nc.scalar.copy(sltm[:], p6[:, 0:1024])
            bar()
            dump("scoretm", sltm[:])
            for i in range(16):
                lg = sltm[:, i * 64:(i + 1) * 64]
                nc.vector.reduce_max(gcol[:], lg, AX.X, negate=True)
                bar()
                nc.scalar.activation(gt0[:], lg, AF.Exp, bias=gcol[:])
                bar()
                nc.vector.reduce_sum(gcol[:], gt0[:], AX.X)
                bar()
                nc.vector.reciprocal(gcol[:], gcol[:])
                bar()
                nc.vector.tensor_scalar_mul(gt0[:], gt0[:], gcol[:])
                bar()
                nc.vector.max(out=gt2[:], in_=gt0[:])
                bar()
                nc.vector.tensor_scalar(gt1[:], gt0[:], gt2[:, 7:8], None,
                                        OP.is_ge)
                bar()
                nc.vector.tensor_mul(gt1[:], gt1[:], gt0[:])
                bar()
                nc.vector.tensor_reduce(
                    gt3[:], gt1[:].rearrange("p (g i) -> p g i", i=8), AX.X,
                    OP.add)
                bar()
                nc.vector.tensor_scalar(gt1[:], gt1[:], 0.0, None, OP.is_gt)
                bar()
                nc.vector.scalar_tensor_tensor(gt0[:], gt0[:], 100.0, gt1[:],
                                               OP.add, OP.mult)
                bar()
                nc.vector.tensor_scalar(gt0[:], gt0[:], 100.0, None,
                                        OP.subtract)
                bar()
                nc.vector.tensor_reduce(
                    gt4[:], gt0[:].rearrange("p (g i) -> p g i", i=8), AX.X,
                    OP.max, negate=True)
                bar()
                nc.vector.tensor_tensor(
                    gt0[:].rearrange("p (g i) -> p g i", i=8),
                    gt0[:].rearrange("p (g i) -> p g i", i=8),
                    gt4[:].rearrange("p (g o) -> p g o", o=1)
                    .broadcast_to([128, 8, 8]), OP.add)
                bar()
                nc.scalar.activation(gt0[:], gt0[:], AF.Exp)
                bar()
                nc.vector.tensor_reduce(
                    gt4[:], gt0[:].rearrange("p (g i) -> p g i", i=8), AX.X,
                    OP.add)
                bar()
                nc.vector.reciprocal(gt4[:], gt4[:])
                bar()
                nc.vector.tensor_tensor(
                    gt0[:].rearrange("p (g i) -> p g i", i=8),
                    gt0[:].rearrange("p (g i) -> p g i", i=8),
                    gt4[:].rearrange("p (g o) -> p g o", o=1)
                    .broadcast_to([128, 8, 8]), OP.mult)
                bar()
                nc.tensor.matmul(p6b[0:64, 0:128], gt0[:], s_i128[:],
                                 start=True, stop=True)
                bar()
                nc.scalar.copy(gstg[:], p6b[0:64, 0:128])
                bar()
                for g in range(NG):
                    dma(nc.sync, d_soft8[:, g, i * 128:(i + 1) * 128],
                        gstg[g * 8:(g + 1) * 8, :])
                nc.tensor.matmul(p6b[0:8, 0:128], gt3[:], s_i128[:],
                                 start=True, stop=True)
                bar()
                nc.vector.tensor_copy(gstgf[0:8, :], p6b[0:8, 0:128])
                bar()
                for g in range(NG):
                    dma(nc.sync, d_scrow[g:g + 1, i * 128:(i + 1) * 128],
                        gstgf[g:g + 1, :])
                dwait()
                bar()
            dump("soft8", d_soft8.ap())
            dump("scrow", d_scrow.ap())

        if STAGE >= 7:
            with MultiCtx(
                nc.sbuf_tensor("fw0", [128, 16 * 128], BF16),
                nc.sbuf_tensor("fw1", [128, 16 * 128], BF16),
                nc.sbuf_tensor("sgg", [128, 2048], F32),
                nc.sbuf_tensor("sab", [128, 11 * 2048], BF16),
                nc.sbuf_tensor("sSgb", [128, 2048], F32),
                nc.sbuf_tensor("swpt", [8, 11 * 128], BF16),
                nc.sbuf_tensor("sprod", [128, 8], BF16),
                nc.sbuf_tensor("syd", [128, 2048], F32),
                nc.sbuf_tensor("s7s", [8, 2048], BF16),
                nc.sbuf_tensor("s7r", [1, 2048], F32),
                nc.psum_tensor("f7m", [128, 2048], F32),
                nc.psum_tensor("f7p", [128, 8], F32),
                nc.psum_tensor("f7t", [8, 128], F32),
            ) as (fw0, fw1, sgg, sab, sSgb, swpt, sprod, syd, s7s, s7r,
                  f7m, f7p, f7t):
                fws = (fw0, fw1)
                for g in range(NG):
                    dma(nc.sync, s7s[:], d_soft8[:, g, :])
                    dma(nc.sync, s7r[:], d_scrow[g:g + 1, :])
                    dwait()
                    bar()
                    for t in range(4):
                        nc.tensor.matmul(
                            f7m[:, t * 512:(t + 1) * 512], s_ones1[:],
                            s7r[0:1, t * 512:(t + 1) * 512],
                            start=True, stop=True)
                    bar()
                    nc.scalar.copy(sSgb[:], f7m[:])
                    bar()
                    gvw = d_gpT[g].rearrange("(k p) i -> p k i", p=128)
                    uvw = d_upT[g].rearrange("(k p) i -> p k i", p=128)
                    wvs = (gvw, uvw)
                    wmarks = []
                    dma(nc.gpsimd,
                        fw0[:].rearrange("p (k m) -> p k m", m=128),
                        gvw[:, :, 0:128], sem=wsp[0])
                    wmarks.append((wsp[0], wsp[0].n))
                    for step in range(22):
                        m, proj = step // 2, step % 2
                        buf = fws[step % 2]
                        if step + 1 < 22:
                            m2, p2 = (step + 1) // 2, (step + 1) % 2
                            wsm = wsp[(step + 1) % 2]
                            dma(nc.gpsimd,
                                fws[(step + 1) % 2][:]
                                .rearrange("p (k m) -> p k m", m=128),
                                wvs[p2][:, :, m2 * 128:(m2 + 1) * 128],
                                sem=wsm)
                            wmarks.append((wsm, wsm.n))
                        nc.tensor.wait_ge(wmarks[step][0].h, wmarks[step][1])
                        for k in range(16):
                            for t in range(4):
                                nc.tensor.matmul(
                                    f7m[:, t * 512:(t + 1) * 512],
                                    buf[:, k * 128:(k + 1) * 128],
                                    s_xp1[:, k * 2048 + t * 512:
                                          k * 2048 + (t + 1) * 512],
                                    start=(k == 0), stop=False)
                            nc.tensor.matmul(
                                f7p[:], buf[:, k * 128:(k + 1) * 128],
                                s_mgT[:, k * 64 + g * 8:
                                      k * 64 + g * 8 + 8],
                                start=(k == 0), stop=(k == 15))
                        bar()
                        nc.vector.tensor_copy(sprod[:], f7p[:])
                        bar()
                        nc.tensor.matmul(f7t[:], sprod[:], s_ibf[:],
                                         start=True, stop=True)
                        bar()
                        nc.vector.tensor_copy(
                            swpt[:, m * 128:(m + 1) * 128], f7t[:])
                        bar()
                        for t in range(4):
                            nc.tensor.matmul(
                                f7m[:, t * 512:(t + 1) * 512],
                                swpt[0:8, m * 128:(m + 1) * 128],
                                s7s[:, t * 512:(t + 1) * 512],
                                start=False, stop=True)
                        bar()
                        if proj == 0:
                            nc.scalar.activation(sgg[:], f7m[:], AF.Silu)
                            bar()
                            if "gg0" in d_dbg and g == 0 and m == 0:
                                dump("gg0", sgg[:])
                        else:
                            nc.vector.tensor_mul(sgg[:], sgg[:], f7m[:])
                            bar()
                            nc.gpsimd.tensor_tensor(
                                sab[:, m * 2048:(m + 1) * 2048], sgg[:],
                                sSgb[:], OP.mult)
                            bar()
                    if "a0" in d_dbg and g == 0:
                        dump("a0", sab[:])
                    if "wpt0" in d_dbg and g == 0:
                        dump("wpt0", swpt[:])
                    if "sgb0" in d_dbg and g == 0:
                        dump("sgb0", sSgb[:])
                    wv = d_dnT[g].rearrange("(k p) m -> p k m", p=128)
                    wmarks = []
                    dma(nc.gpsimd,
                        fw0[:, 0:11 * 128]
                        .rearrange("p (k m) -> p k m", m=128),
                        wv[:, :, 0:128], sem=wsp[0])
                    wmarks.append((wsp[0], wsp[0].n))
                    for m in range(16):
                        buf = fws[m % 2]
                        if m + 1 < 16:
                            wsm = wsp[(m + 1) % 2]
                            dma(nc.gpsimd,
                                fws[(m + 1) % 2][:, 0:11 * 128]
                                .rearrange("p (k m) -> p k m", m=128),
                                wv[:, :, (m + 1) * 128:(m + 2) * 128],
                                sem=wsm)
                            wmarks.append((wsm, wsm.n))
                        nc.tensor.wait_ge(wmarks[m][0].h, wmarks[m][1])
                        for k in range(11):
                            for t in range(4):
                                nc.tensor.matmul(
                                    f7m[:, t * 512:(t + 1) * 512],
                                    buf[:, k * 128:(k + 1) * 128],
                                    sab[:, k * 2048 + t * 512:
                                        k * 2048 + (t + 1) * 512],
                                    start=(k == 0), stop=(k == 10))
                        bar()
                        zs.inc(nc.scalar.copy(syd[:], f7m[:]))
                        nc.gpsimd.wait_ge(zs.h, zs.n)
                        dma(nc.gpsimd, d_y2[m * 128:(m + 1) * 128, :],
                            syd[:], sem=ya,
                            accum_op=(OP.bypass if g == 0 else OP.add))
                        nc.scalar.wait_ge(ya.h, ya.n)
                        bar()
                nc.sync.wait_ge(ya.h, ya.n)
                bar()
                dump("y2", d_y2.ap())
        _p67.__exit__(None, None, None)

    if STAGE >= 8:
        with MultiCtx(
            nc.sbuf_tensor("sxr8", [128, 2048], F32),
            nc.sbuf_tensor("sy28", [128, 2048], F32),
            nc.sbuf_tensor("sxb8", [128, 2048], BF16),
            nc.sbuf_tensor("sxb8b", [128, 2048], BF16),
            nc.sbuf_tensor("s_m1b", [128, 2048], F32),
            nc.sbuf_tensor("s_r1b", [128, 2048], F32),
            nc.sbuf_tensor("s_m2b", [128, 2048], F32),
            nc.sbuf_tensor("s_r2b", [128, 2048], F32),
            nc.sbuf_tensor("srow8", [1, 2 * 2048], F32),
        ) as (sxr8, sy28, sxb8, sxb8b, s_m1b, s_r1b, s_m2b, s_r2b, srow8):
            dma(nc.sync, srow8[0:1, 0:2048], d_murs[0:1, :])
            dma(nc.sync, srow8[0:1, 2048:4096], d_murs[1:2, :])
            dwait()
            bar()
            with nc.psum_tensor("p8c", [128, 2048], F32) as p8c:
                for t in range(4):
                    nc.tensor.matmul(p8c[:, t * 512:(t + 1) * 512],
                                     s_ones1[:],
                                     srow8[0:1, t * 512:(t + 1) * 512],
                                     start=True, stop=True)
                bar()
                nc.scalar.copy(s_m1b[:], p8c[:])
                bar()
                for t in range(4):
                    nc.tensor.matmul(p8c[:, t * 512:(t + 1) * 512],
                                     s_ones1[:],
                                     srow8[0:1, 2048 + t * 512:
                                           2048 + (t + 1) * 512],
                                     start=True, stop=True)
                bar()
                nc.scalar.copy(s_r1b[:], p8c[:])
                bar()

            def t2_chunk(m):
                dma(nc.sync, sxr8[:], d_xres[m * 128:(m + 1) * 128, :])
                dma(nc.scalar, sy28[:], d_y2[m * 128:(m + 1) * 128, :])
                dwait()
                dwait(nc.scalar)
                bar()
                nc.vector.tensor_sub(sxr8[:], sxr8[:], s_m1b[:])
                bar()
                nc.vector.tensor_mul(sxr8[:], sxr8[:], s_r1b[:])
                bar()
                nc.vector.tensor_add(sxr8[:], sxr8[:], sy28[:])
                bar()

            with MultiCtx(
                nc.psum_tensor("p8r1", [1, 2048], F32),
                nc.psum_tensor("p8r2", [1, 2048], F32),
            ) as (p8r1, p8r2):
                for m in range(16):
                    t2_chunk(m)
                    nc.scalar.activation(sxb8[:], sxr8[:], AF.Copy)
                    nc.scalar.activation(sxb8b[:], sxr8[:], AF.Square)
                    bar()
                    for t in range(4):
                        nc.tensor.matmul(
                            p8r1[0:1, t * 512:(t + 1) * 512], s_onesb[:],
                            sxb8[:, t * 512:(t + 1) * 512],
                            start=(m == 0), stop=(m == 15))
                        nc.tensor.matmul(
                            p8r2[0:1, t * 512:(t + 1) * 512], s_onesb[:],
                            sxb8b[:, t * 512:(t + 1) * 512],
                            start=(m == 0), stop=(m == 15))
                    bar()
                nc.vector.tensor_copy(srow8[0:1, 0:2048], p8r1[0:1, :])
                bar()
                nc.vector.tensor_copy(srow8[0:1, 2048:4096], p8r2[0:1, :])
                bar()
            nc.vector.tensor_scalar_mul(srow8[0:1, 0:2048],
                                        srow8[0:1, 0:2048], 1.0 / DM)
            bar()
            nc.vector.tensor_scalar_mul(srow8[0:1, 2048:4096],
                                        srow8[0:1, 2048:4096], 1.0 / DM)
            bar()
            nc.vector.tensor_tensor(sy28[0:1, 0:2048], srow8[0:1, 0:2048],
                                    srow8[0:1, 0:2048], OP.mult)
            bar()
            nc.vector.tensor_sub(srow8[0:1, 2048:4096],
                                 srow8[0:1, 2048:4096], sy28[0:1, 0:2048])
            bar()
            nc.vector.tensor_scalar(srow8[0:1, 2048:4096],
                                    srow8[0:1, 2048:4096], EPS, None, OP.add)
            bar()
            nc.scalar.activation(srow8[0:1, 2048:4096],
                                 srow8[0:1, 2048:4096], AF.Sqrt)
            bar()
            nc.vector.reciprocal(srow8[0:1, 2048:4096],
                                 srow8[0:1, 2048:4096])
            bar()
            with nc.psum_tensor("p8b", [128, 2048], F32) as p8b:
                for t in range(4):
                    nc.tensor.matmul(p8b[:, t * 512:(t + 1) * 512],
                                     s_ones1[:],
                                     srow8[0:1, t * 512:(t + 1) * 512],
                                     start=True, stop=True)
                bar()
                nc.scalar.copy(s_m2b[:], p8b[:])
                bar()
                for t in range(4):
                    nc.tensor.matmul(p8b[:, t * 512:(t + 1) * 512],
                                     s_ones1[:],
                                     srow8[0:1, 2048 + t * 512:
                                           2048 + (t + 1) * 512],
                                     start=True, stop=True)
                bar()
                nc.scalar.copy(s_r2b[:], p8b[:])
                bar()
            for m in range(16):
                t2_chunk(m)
                nc.vector.tensor_sub(sxr8[:], sxr8[:], s_m2b[:])
                bar()
                nc.vector.tensor_mul(sxr8[:], sxr8[:], s_r2b[:])
                bar()
                dma(nc.sync, d_out[m * 128:(m + 1) * 128, :], sxr8[:],
                    sem=ss)
                nc.sync.wait_ge(ss.h, ss.n)
                bar()

    if STAGE < 8:
        with nc.sbuf_tensor("sfin", [128, TOK], F32) as sfin:
            nc.vector.memset(sfin[:], 0.0)
            bar()
            for m in range(16):
                dma(nc.sync, d_out[m * 128:(m + 1) * 128, :], sfin[:],
                    sem=ss)
            nc.sync.wait_ge(ss.h, ss.n)
            bar()
    nc.sync.wait_ge(ds.h, ds.n)
    nc.sync.wait_ge(ss.h, ss.n)
    bar()
    return nc



def _prep(inputs):
    ident = np.eye(128, dtype=ml_dtypes.bfloat16)
    i64 = np.eye(64, dtype=np.float32)
    i128 = np.eye(128, dtype=np.float32)
    onesb = np.ones((128, 1), dtype=ml_dtypes.bfloat16)
    ones1 = np.ones((1, 128), dtype=np.float32)
    W_inT = np.ascontiguousarray(np.asarray(inputs["L0_attn_W_in"]).T)
    woutT = np.ascontiguousarray(np.asarray(inputs["L0_attn_W_out"]).T)
    gatewT = np.ascontiguousarray(np.asarray(inputs["L0_ffn_gate_w"]).T)
    mupT = np.ascontiguousarray(
        np.asarray(inputs["L0_ffn_mask_up_w"]).transpose(1, 0, 2)
        .reshape(DM, NG * 8))
    gpT = np.ascontiguousarray(np.asarray(inputs["L0_ffn_gate_proj_w"])
                               .transpose(0, 2, 1))
    upT = np.ascontiguousarray(np.asarray(inputs["L0_ffn_up_proj_w"])
                               .transpose(0, 2, 1))
    dnT = np.ascontiguousarray(np.asarray(inputs["L0_ffn_down_proj_w"])
                               .transpose(0, 2, 1))
    drep = np.ascontiguousarray(np.repeat(np.asarray(inputs["L0_attn_D"]),
                                          PH))
    invmap = np.ascontiguousarray(
        np.asarray(inputs["L0_ffn_inv_map"]).reshape(-1).astype(np.int32))
    u = np.asarray(inputs["L0_attn_u"])
    shared = dict(
        winT=W_inT, convw=np.ascontiguousarray(inputs["L0_attn_conv_w"]),
        convb=np.ascontiguousarray(inputs["L0_attn_conv_b"]),
        dtbias=np.ascontiguousarray(inputs["L0_attn_dt_bias"]),
        alog=np.ascontiguousarray(inputs["L0_attn_A_log"]),
        drep=drep, nw=np.ascontiguousarray(inputs["L0_attn_norm_w"]),
        woutT=woutT, gatewT=gatewT, invmap=invmap, mupT=mupT,
        gpT=gpT, upT=upT, dnT=dnT, ibf=ident, i64=i64, i128=i128,
        onesb=onesb, ones1=ones1)
    in_maps = []
    for core in range(8):
        b, qq = core // 4, core % 4
        t0 = qq * TOK
        uslice = np.zeros((TOKH, DM), dtype=np.float32)
        lo = t0 - 3
        src = u[b, max(lo, 0):t0 + TOK]
        uslice[TOKH - src.shape[0]:] = src
        uT = np.ascontiguousarray(uslice.T)
        mq = np.zeros((5, 4), dtype=np.float32)
        mq[4, :] = NEG
        for pq in range(4):
            if pq < qq:
                mq[4, pq] = 0.0
                for rq in range(4):
                    if pq < rq < qq:
                        mq[rq, pq] = 1.0
        m = dict(shared)
        m["uT"] = uT
        m["mq"] = mq
        in_maps.append(m)
    return in_maps


def kernel(**inputs):
    in_maps = _prep(inputs)
    nc = build_nc()
    res = run_bass_kernel_spmd(nc, in_maps, core_ids=list(range(8)))
    out = np.zeros((2, 8192, DM), dtype=np.float32)
    for core in range(8):
        b, qq = core // 4, core % 4
        out[b, qq * TOK:(qq + 1) * TOK] = res.results[core]["out"].T
    return out


# version 02
# speedup vs baseline: 1.1769x; 1.1769x over previous
import numpy as np
import ml_dtypes
import concourse.bass as bass
import concourse.mybir as mybir
from concourse.bass_utils import run_bass_kernel_spmd

F32 = mybir.dt.float32
BF16 = mybir.dt.bfloat16
I32 = mybir.dt.int32
OP = mybir.AluOpType
AF = mybir.ActivationFunctionType
AX = mybir.AxisListType

DM = 2048
DIN = 4096
DST = 128
H = 64
PH = 64
CONVD = 4352
DIP = 8512
TOK = 2048
TOKH = 2051
CH = 128
NCH = 16
IF = 1408
NG = 8
E = 64
NEG = -1.0e30
EPS = 1e-5
NM_Z = 32
NM_XBC = 34
M_DT = 66
NM = 67

STAGE = 99
P1CAP = 999


class MultiCtx:

    def __init__(self, *cms):
        self.cms = cms

    def __enter__(self):
        from contextlib import ExitStack
        self.st = ExitStack()
        return tuple(self.st.enter_context(c) for c in self.cms)

    def __exit__(self, *a):
        return self.st.__exit__(*a)


class Sem:
    def __init__(self, nc, name):
        self.h = nc.alloc_semaphore(name)
        self.n = 0

    def inc(self, instr, k=1):
        instr.then_inc(self.h, k)
        self.n += k
        return instr


def build_nc(dbg=()):
    nc = bass.Bass("TRN2", target_bir_lowering=False,
                   detect_race_conditions=False)

    d_uT = nc.dram_tensor("uT", [DM, TOKH], F32, kind="ExternalInput")
    d_winT = nc.dram_tensor("winT", [DM, DIP], F32, kind="ExternalInput")
    d_convw = nc.dram_tensor("convw", [CONVD, 4], F32, kind="ExternalInput")
    d_convb = nc.dram_tensor("convb", [CONVD], F32, kind="ExternalInput")
    d_dtbias = nc.dram_tensor("dtbias", [H], F32, kind="ExternalInput")
    d_alog = nc.dram_tensor("alog", [H], F32, kind="ExternalInput")
    d_drep = nc.dram_tensor("drep", [DIN], F32, kind="ExternalInput")
    d_nw = nc.dram_tensor("nw", [DIN], F32, kind="ExternalInput")
    d_woutT = nc.dram_tensor("woutT", [DIN, DM], F32, kind="ExternalInput")
    d_gatewT = nc.dram_tensor("gatewT", [DM, E], F32, kind="ExternalInput")
    d_invmap = nc.dram_tensor("invmap", [E], I32, kind="ExternalInput")
    d_mupT = nc.dram_tensor("mupT", [DM, NG * 8], F32, kind="ExternalInput")
    d_gpT = nc.dram_tensor("gpT", [NG, DM, IF], F32, kind="ExternalInput")
    d_upT = nc.dram_tensor("upT", [NG, DM, IF], F32, kind="ExternalInput")
    d_dnT = nc.dram_tensor("dnT", [NG, IF, DM], F32, kind="ExternalInput")
    d_mq = nc.dram_tensor("mq", [5, 4], F32, kind="ExternalInput")
    d_ibf = nc.dram_tensor("ibf", [128, 128], BF16, kind="ExternalInput")
    d_i64 = nc.dram_tensor("i64", [64, 64], F32, kind="ExternalInput")
    d_i128 = nc.dram_tensor("i128", [128, 128], F32, kind="ExternalInput")
    d_onesb = nc.dram_tensor("onesb", [128, 1], BF16, kind="ExternalInput")
    d_ones1 = nc.dram_tensor("ones1", [1, 128], F32, kind="ExternalInput")

    d_out = nc.dram_tensor("out", [DM, TOK], F32, kind="ExternalOutput")

    d_zsc = nc.dram_tensor("zsc", [DIN, TOK], F32, kind="Internal")
    d_xsc = nc.dram_tensor("xsc", [NCH, DIN, CH], BF16, kind="Internal")
    d_xdsc = nc.dram_tensor("xdsc", [NCH, DIN, CH], BF16, kind="Internal")
    d_scsc = nc.dram_tensor("scsc", [NCH, 128, DIN], F32, kind="Internal")
    d_acrow = nc.dram_tensor("acrow", [NCH, H * CH], F32, kind="Internal")
    d_dcrow = nc.dram_tensor("dcrow", [NCH, H], F32, kind="Internal")
    d_wrow = nc.dram_tensor("wrow", [4, H], F32, kind="Internal")
    d_xres = nc.dram_tensor("xres", [DM, TOK], F32, kind="Internal")
    d_y2 = nc.dram_tensor("y2", [DM, TOK], F32, kind="Internal")
    d_woutbf = nc.dram_tensor("woutbf", [DIN, DM], BF16, kind="Internal")
    d_agin = nc.dram_tensor("agin", [128 * DIN + 64], F32, kind="Internal")
    d_agout = nc.dram_tensor("agout", [4, 128 * DIN + 64], F32,
                             kind="Internal")
    d_dtsc = nc.dram_tensor("dtsc", [2, H, TOK], F32, kind="Internal")
    d_lnrow = nc.dram_tensor("lnrow", [2, TOK], F32, kind="Internal")
    d_murs = nc.dram_tensor("murs", [2, TOK], F32, kind="Internal")
    d_soft8 = nc.dram_tensor("soft8", [8, NG, TOK], BF16, kind="Internal")
    d_scrow = nc.dram_tensor("scrow", [NG, TOK], F32, kind="Internal")

    dbg_spec = dict(
        z=([DIN, TOK], F32), x=([NCH, DIN, CH], BF16), dt=([H, TOK], F32),
        acum=([H, TOK], F32), sc=([NCH, 128, DIN], F32),
        slocal=([128, DIN], F32), carry=([128, DIN], F32),
        yfm=([128, DIN], F32), ynorm=([128, 32 * 512], BF16),
        xres=([DM, TOK], F32), xp1=([128, 16 * 2048], BF16),
        scoretm=([128, 16 * 64], F32), soft8=([8, NG, TOK], BF16),
        scrow=([NG, TOK], F32), y2=([DM, TOK], F32),
        wt=([128, 32 * CH], BF16), xtm=([128, DIN], BF16),
        yfm5=([128, DIN], F32), woutbf=([DIN, DM], BF16),
        prev5=([128, DIN], BF16), a0=([128, 11 * 2048], BF16),
        gg0=([128, 2048], F32), wpt0=([8, 11 * 128], BF16),
        sgb0=([128, 2048], F32),
    )
    d_dbg = {}
    for name in dbg:
        shape, dt_ = dbg_spec[name]
        d_dbg[name] = nc.dram_tensor("dbg_" + name, shape, dt_,
                                     kind="ExternalOutput")

    ds = Sem(nc, "ds")
    ws = Sem(nc, "ws")
    ps = Sem(nc, "ps")
    es = Sem(nc, "es")
    cs = Sem(nc, "cs")
    zs = Sem(nc, "zs")
    ss = Sem(nc, "ss")
    cc = Sem(nc, "cc")
    ya = Sem(nc, "ya")
    su = Sem(nc, "su")
    ws1 = Sem(nc, "ws1")
    sz0 = Sem(nc, "sz0")
    sz1 = Sem(nc, "sz1")
    sx0 = Sem(nc, "sx0")
    sx1 = Sem(nc, "sx1")
    srx = Sem(nc, "srx")
    wsp = (ws, ws1)

    def bar():
        nc.all_engine_barrier()

    def dma(q, out, in_, sem=ds, **kw):
        sem.inc(q.dma_start(out=out, in_=in_, **kw), 16)

    def dwait(eng=None):
        (eng or nc.sync).wait_ge(ds.h, ds.n)

    def dump(name, src):
        if name in d_dbg:
            dma(nc.sync, d_dbg[name].ap(), src)
            dwait()
            bar()

    sb = nc.alloc_sbuf_tensor
    s_convw = sb("s_convw", [128, NM_XBC * 4], F32)
    s_convb = sb("s_convb", [128, NM_XBC], F32)
    s_dtbias = sb("s_dtbias", [H, 1], F32)
    s_acol = sb("s_acol", [H, 1], F32)
    s_drep = sb("s_drep", [128, 32], F32)
    s_nw = sb("s_nw", [128, 32], F32)
    s_ibf = sb("s_ibf", [128, 128], BF16)
    s_i64 = sb("s_i64", [64, 64], F32)
    s_i128 = sb("s_i128", [128, 128], F32)
    s_onesb = sb("s_onesb", [128, 1], BF16)
    s_ones1 = sb("s_ones1", [1, 128], F32)
    s_mq = sb("s_mq", [5, 4], F32)
    s_atote = sb("s_atote", [5, H], F32)
    s_mgT = sb("s_mgT", [128, 16 * NG * 8], BF16)
    s_gateT = sb("s_gateT", [128, 16 * E], BF16)
    s_PT = sb("s_PT", [E, E], F32)
    s_tmp64 = sb("s_tmp64", [H, CH], F32)
    s_tmp64b = sb("s_tmp64b", [H, CH], F32)

    q = nc.sync
    dma(q, s_convw[:].rearrange("p (m t) -> p m t", t=4),
        d_convw.ap().rearrange("(m p) t -> p m t", p=128))
    dma(q, s_convb[:], d_convb.ap().rearrange("(m p) -> p m", p=128),
        allow_slow_non_contiguous=True)
    dma(q, s_dtbias[:], d_dtbias.ap().rearrange("(h o) -> h o", o=1))
    dma(q, s_acol[:], d_alog.ap().rearrange("(h o) -> h o", o=1))
    dma(q, s_drep[:], d_drep.ap().rearrange("(m p) -> p m", p=128),
        allow_slow_non_contiguous=True)
    dma(q, s_nw[:], d_nw.ap().rearrange("(m p) -> p m", p=128),
        allow_slow_non_contiguous=True)
    dma(q, s_ibf[:], d_ibf.ap())
    dma(q, s_i64[:], d_i64.ap())
    dma(q, s_i128[:], d_i128.ap())
    dma(q, s_onesb[:], d_onesb.ap())
    dma(q, s_ones1[:], d_ones1.ap())
    dma(q, s_mq[:], d_mq.ap())
    dma(nc.gpsimd, s_mgT[:].rearrange("p (k e) -> p k e", e=NG * 8),
        d_mupT.ap().rearrange("(k p) e -> p k e", p=128), sem=ws)
    dma(nc.gpsimd, s_gateT[:].rearrange("p (k e) -> p k e", e=E),
        d_gatewT.ap().rearrange("(k p) e -> p k e", p=128), sem=ws)
    with MultiCtx(
        nc.sbuf_tensor("s_invi", [E, 1], I32),
        nc.sbuf_tensor("s_invf", [E, 1], F32),
        nc.sbuf_tensor("s_iotac", [E, 1], F32),
        nc.sbuf_tensor("s_row0", [1, E], F32),
        nc.psum_tensor("pp0", [64, 64], F32),
    ) as (s_invi, s_invf, s_iotac, s_row0, pp0):
        dma(q, s_invi[:], d_invmap.ap().rearrange("(e o) -> e o", o=1))
        dwait()
        bar()
        nc.vector.tensor_copy(s_invf[:], s_invi[:])
        nc.gpsimd.iota(s_iotac[:], [[1, 1]], base=0, channel_multiplier=1,
                       allow_small_or_imprecise_dtypes=True)
        bar()
        nc.tensor.matmul(pp0[0:1, 0:64], s_invf[:], s_i64[:], start=True,
                         stop=True)
        bar()
        nc.vector.tensor_copy(s_row0[:], pp0[0:1, 0:64])
        bar()
        nc.tensor.matmul(pp0[0:64, 0:64], s_ones1[0:1, 0:64], s_row0[:],
                         start=True, stop=True)
        bar()
        nc.vector.tensor_scalar(s_PT[:], pp0[0:64, 0:64], s_iotac[:], None,
                                OP.is_equal)
        bar()

    with MultiCtx(
        nc.sbuf_tensor("s_Bbf", [128, TOK], BF16),
        nc.sbuf_tensor("s_Cbf", [128, TOK], BF16),
        nc.sbuf_tensor("s_colt", [H, TOK], F32),
        nc.sbuf_tensor("s_acum", [H, TOK], F32),
        nc.sbuf_tensor("s_asum", [H, NCH], F32),
        nc.sbuf_tensor("s_dcb", [128, NCH * H], F32),
    ) as (s_Bbf, s_Cbf, s_colt, s_acum, s_asum, s_dcb):
        if STAGE >= 1:
            _sc = nc.named_scope("P1"); _sc.__enter__()
            with MultiCtx(
                nc.sbuf_tensor("suT", [128, 16 * TOKH], BF16),
                nc.sbuf_tensor("wb0", [128, 16 * 128], BF16),
                nc.sbuf_tensor("wb1", [128, 16 * 128], BF16),
                nc.sbuf_tensor("xbc0", [128, TOKH], F32),
                nc.sbuf_tensor("cv0", [128, TOK], F32),
                nc.sbuf_tensor("xr0", [128, TOK], BF16),
                nc.sbuf_tensor("xr1", [128, TOK], BF16),
                nc.sbuf_tensor("xd0", [128, TOK], BF16),
                nc.sbuf_tensor("xd1", [128, TOK], BF16),
                nc.sbuf_tensor("zb0", [128, TOK], F32),
                nc.sbuf_tensor("zb1", [128, TOK], F32),
                nc.sbuf_tensor("sdtev", [H, TOK], F32),
                nc.psum_tensor("pA", [128, 2048], F32),
                nc.psum_tensor("pH", [128, 8], F32),
            ) as (suT, wb0, wb1, xbc0, cv0, xr0, xr1, xd0, xd1, zb0, zb1,
                  sdtev, pA, pH):
                for k in range(16):
                    dma(nc.gpsimd, suT[:, k * TOKH:(k + 1) * TOKH],
                        d_uT.ap().rearrange("(k p) t -> k p t", p=128)[k],
                        sem=su)
                wview = d_winT.ap().rearrange("(k p) m -> p k m", p=128)

                def wload(mi, buf):
                    wdt = min(128, DIP - mi * 128)
                    sm = wsp[mi % 2]
                    dma(nc.gpsimd,
                        buf[:].rearrange("p (k m) -> p k m", m=128)
                        [:, :, 0:wdt],
                        wview[:, :, mi * 128:mi * 128 + wdt], sem=sm)
                    return (sm, sm.n)

                wmk = []
                wmk.append(wload(0, wb0))
                wbufs = (wb0, wb1)
                xrs, xds, zbs = (xr0, xr1), (xd0, xd1), (zb0, zb1)
                guard = {}

                for mi in range(min(NM, P1CAP)):
                    par = mi % 2
                    wb = wbufs[par]
                    rows = 64 if mi == M_DT else 128
                    if mi + 1 < NM:
                        nc.gpsimd.wait_ge(ps.h, mi)
                        wmk.append(wload(mi + 1, wbufs[(mi + 1) % 2]))
                    if mi == 0:
                        nc.tensor.wait_ge(su.h, su.n)
                    nc.tensor.wait_ge(wmk[mi][0].h, wmk[mi][1])
                    nc.tensor.wait_ge(es.h, mi)
                    is_xbc = NM_Z <= mi < NM_Z + NM_XBC
                    for t in range(4):
                        for k in range(16):
                            _mm = nc.tensor.matmul(
                                pA[0:rows, t * 512:(t + 1) * 512],
                                wb[:, k * 128:k * 128 + rows],
                                suT[:, k * TOKH + 3 + t * 512:
                                    k * TOKH + 3 + (t + 1) * 512],
                                start=(k == 0), stop=(k == 15))
                    if is_xbc:
                        for k in range(16):
                            _mm = nc.tensor.matmul(
                                pH[:, 0:3], wb[:, k * 128:(k + 1) * 128],
                                suT[:, k * TOKH:k * TOKH + 3],
                                start=(k == 0), stop=(k == 15))
                    ps.inc(_mm)

                    if mi < NM_Z:
                        zb = zbs[par]
                        szp = (sz0, sz1)[par]
                        nc.scalar.wait_ge(ps.h, mi + 1)
                        nc.scalar.wait_ge(szp.h, szp.n)
                        es.inc(nc.scalar.activation(zb[:], pA[:], AF.Silu))
                        nc.sync.wait_ge(es.h, es.n)
                        dma(nc.sync, d_zsc[mi * 128:(mi + 1) * 128, :], zb[:],
                            sem=szp)
                    elif is_xbc:
                        ci = mi - NM_Z
                        nc.vector.wait_ge(ps.h, mi + 1)
                        nc.vector.tensor_copy(xbc0[:, 0:3], pH[:, 0:3])
                        es.inc(nc.vector.tensor_copy(xbc0[:, 3:2051], pA[:]))
                        nc.vector.tensor_scalar(
                            cv0[:], xbc0[:, 0:2048],
                            s_convw[:, ci * 4:ci * 4 + 1],
                            s_convb[:, ci:ci + 1], OP.mult, OP.add)
                        for w in range(1, 4):
                            _cv = nc.vector.scalar_tensor_tensor(
                                cv0[:], xbc0[:, w:w + 2048],
                                s_convw[:, ci * 4 + w:ci * 4 + w + 1], cv0[:],
                                OP.mult, OP.add)
                        cs.inc(_cv)
                        nc.scalar.wait_ge(cs.h, cs.n)
                        if ci < 32:
                            xrb, xdb = xrs[par], xds[par]
                            sxq = (sx0, sx1)[par]
                            nc.scalar.wait_ge(sxq.h, sxq.n)
                            zs.inc(nc.scalar.activation(xrb[:], cv0[:],
                                                         AF.Silu))
                            nc.vector.wait_ge(zs.h, zs.n)
                            zs.inc(nc.vector.tensor_scalar_mul(
                                xdb[:], xrb[:], s_drep[:, ci:ci + 1]))
                            nc.sync.wait_ge(zs.h, zs.n)
                            nc.scalar.wait_ge(zs.h, zs.n)
                            dma(nc.sync,
                                d_xsc[:, ci * 128:(ci + 1) * 128, :]
                                .transpose([1, 0, 2]),
                                xrb[:].rearrange("p (c t) -> p c t", t=128),
                                sem=sxq)
                            dma(nc.scalar,
                                d_xdsc[:, ci * 128:(ci + 1) * 128, :]
                                .transpose([1, 0, 2]),
                                xdb[:].rearrange("p (c t) -> p c t", t=128),
                                sem=sxq)
                        elif ci == 32:
                            zs.inc(nc.scalar.activation(s_Bbf[:], cv0[:],
                                                        AF.Silu))
                        else:
                            zs.inc(nc.scalar.activation(s_Cbf[:], cv0[:],
                                                        AF.Silu))
                    else:
                        nc.vector.wait_ge(ps.h, mi + 1)
                        es.inc(nc.vector.tensor_copy(sdtev[:], pA[0:64, :]))
                for smx in (sz0, sz1, sx0, sx1, ss):
                    nc.sync.wait_ge(smx.h, smx.n)
                bar()
                if P1CAP >= NM:
                    dma(nc.sync, d_dtsc[0], sdtev[:])
                    dwait()
                bar()
            _sc.__exit__(None, None, None)

        if STAGE >= 2:
            _sc = nc.named_scope("P2"); _sc.__enter__()
            with MultiCtx(
                nc.sbuf_tensor("p2raw", [H, TOK], F32),
                nc.sbuf_tensor("p2dt", [H, TOK], F32),
            ) as (p2raw, p2dt):
                dma(nc.sync, p2raw[:], d_dtsc[0])
                dwait()
                bar()
                nc.scalar.activation(p2dt[:], p2raw[:], AF.Abs,
                                     bias=s_dtbias[:])
                bar()
                nc.scalar.activation(p2dt[:], p2dt[:], AF.Exp, scale=-1.0)
                bar()
                nc.scalar.activation(p2dt[:], p2dt[:], AF.Ln, bias=1.0)
                nc.vector.tensor_scalar(p2raw[:], p2raw[:], s_dtbias[:], 0.0,
                                        OP.add, OP.max)
                bar()
                nc.vector.tensor_add(p2dt[:], p2dt[:], p2raw[:])
                bar()
                nc.scalar.activation(s_colt[:], p2dt[:], AF.Ln)
                nc.scalar.activation(s_acol[:], s_acol[:], AF.Exp)
                bar()
                dma(nc.sync, d_dtsc[1], p2dt[:])
                nc.vector.tensor_scalar_mul(s_acol[:], s_acol[:], -1.0)
                bar()
                nc.vector.tensor_scalar_mul(p2raw[:], p2dt[:], s_acol[:])
                bar()
                for c in range(NCH):
                    nc.vector.tensor_tensor_scan(
                        s_acum[:, c * CH:(c + 1) * CH],
                        p2raw[:, c * CH:(c + 1) * CH],
                        p2raw[:, c * CH:(c + 1) * CH], 0.0, OP.add, OP.bypass)
                bar()
                nc.vector.tensor_sub(s_colt[:], s_colt[:], s_acum[:])
                nc.vector.tensor_copy(
                    s_asum[:],
                    s_acum[:].rearrange("h (c t) -> h c t", t=CH)
                    [:, :, CH - 1])
                bar()
                nc.scalar.activation(s_tmp64[:, 0:NCH], s_asum[:], AF.Exp)
                bar()
                with nc.psum_tensor("pdc", [64, 64], F32) as pdc:
                    nc.tensor.matmul(pdc[0:NCH, 0:64], s_tmp64[:, 0:NCH],
                                     s_i64[:], start=True, stop=True)
                    bar()
                    nc.vector.tensor_copy(s_tmp64b[0:NCH, 0:64],
                                          pdc[0:NCH, 0:64])
                    bar()
                dma(nc.sync, d_dcrow.ap(), s_tmp64b[0:NCH, 0:64])
                for c in range(NCH):
                    dma(nc.sync, d_acrow[c].rearrange("(h l) -> h l", h=H),
                        s_acum[:, c * CH:(c + 1) * CH])
                dwait()
                bar()
                for c in range(NCH):
                    dma(nc.sync, s_dcb[:, c * H:(c + 1) * H],
                        d_dcrow[c:c + 1, :].broadcast_to([128, H]))
                dwait()
                bar()
                dump("dt", p2dt[:])
            dump("acum", s_acum[:])
            dump("z", d_zsc.ap())
            dump("x", d_xsc.ap())
            _sc.__exit__(None, None, None)

        _s34 = MultiCtx(
            nc.sbuf_tensor("s_R", [128, DIN], F32),
            nc.sbuf_tensor("s_prevbf", [128, DIN], BF16),
            nc.sbuf_tensor("s_yng", [128, 32 * 512], BF16),
        )
        s_R, s_prevbf, s_yng = _s34.__enter__()

        if STAGE >= 3:
            _sc = nc.named_scope("P3"); _sc.__enter__()
            with MultiCtx(
                nc.sbuf_tensor("xtm0", [128, DIN], BF16),
                nc.sbuf_tensor("xtm1", [128, DIN], BF16),
                nc.sbuf_tensor("sxd", [128, DIN], BF16),
                nc.sbuf_tensor("sBt", [128, 128], BF16),
                nc.sbuf_tensor("sdtd", [128, 64], BF16),
                nc.sbuf_tensor("sSc", [128, DIN], F32),
                nc.sbuf_tensor("p3dt", [H, TOK], F32),
                nc.psum_tensor("p3a", [128, 2048], F32),
                nc.psum_tensor("p3s", [128, 128], F32),
                nc.psum_tensor("p3w", [128, 64], F32),
            ) as (xtm0, xtm1, sxd, sBt, sdtd, sSc, p3dt, p3a, p3s, p3w):
                dma(nc.sync, p3dt[:], d_dtsc[1])
                nc.vector.memset(s_R[:], 0.0)
                xtms = (xtm0, xtm1)
                dma(nc.sync, xtm0[:], d_xsc[0], transpose=True)
                dwait()
                bar()
                dump("xtm", xtm0[:])
                for c in range(NCH):
                    if c + 1 < NCH:
                        dma(nc.sync, xtms[(c + 1) % 2][:], d_xsc[c + 1],
                            transpose=True)
                    xtm = xtms[c % 2]
                    nc.tensor.matmul(p3s[:], s_Bbf[:, c * CH:(c + 1) * CH],
                                     s_ibf[:], start=True, stop=True)
                    nc.scalar.activation(
                        s_tmp64[:], s_acum[:, c * CH:(c + 1) * CH], AF.Exp,
                        scale=-1.0,
                        bias=s_acum[:, c * CH + CH - 1:c * CH + CH])
                    bar()
                    nc.vector.tensor_mul(s_tmp64[:], s_tmp64[:],
                                         p3dt[:, c * CH:(c + 1) * CH])
                    nc.scalar.copy(sBt[:], p3s[:])
                    bar()
                    nc.tensor.matmul(p3w[:], s_tmp64[:], s_i64[:], start=True,
                                     stop=True)
                    bar()
                    nc.vector.tensor_copy(sdtd[:], p3w[:])
                    bar()
                    nc.vector.tensor_tensor(
                        sxd[:].rearrange("s (h p) -> s h p", p=PH),
                        xtm[:].rearrange("s (h p) -> s h p", p=PH),
                        sdtd[:].rearrange("s (h o) -> s h o", o=1)
                        .broadcast_to([128, H, PH]), OP.mult)
                    bar()
                    for half in range(2):
                        for j in range(4):
                            nc.tensor.matmul(
                                p3a[:, j * 512:(j + 1) * 512], sBt[:],
                                sxd[:, half * 2048 + j * 512:
                                    half * 2048 + (j + 1) * 512],
                                start=True, stop=True)
                        bar()
                        nc.scalar.copy(sSc[:, half * 2048:(half + 1) * 2048],
                                       p3a[:])
                        bar()
                    dma(nc.sync, d_scsc[c], sSc[:], sem=ss)
                    nc.vector.tensor_tensor(
                        s_R[:].rearrange("n (h p) -> n h p", p=PH),
                        s_R[:].rearrange("n (h p) -> n h p", p=PH),
                        s_dcb[:, c * H:(c + 1) * H]
                        .rearrange("n (h o) -> n h o", o=1)
                        .broadcast_to([128, H, PH]), OP.mult)
                    bar()
                    nc.vector.tensor_add(s_R[:], s_R[:], sSc[:])
                    bar()
                nc.vector.reduce_sum(s_tmp64[:, 0:1], s_asum[:], AX.X)
                nc.sync.wait_ge(ss.h, ss.n)
                bar()
                dump("sc", d_scsc.ap())
                dump("slocal", s_R[:])
                dma(nc.sync,
                    d_agin.ap()[0:128 * DIN].rearrange("(p f) -> p f", p=128),
                    s_R[:])
                dma(nc.sync,
                    d_agin.ap()[128 * DIN:].rearrange("(h o) -> h o", o=1),
                    s_tmp64[:, 0:1])
                dwait()
                bar()
                _sc.__exit__(None, None, None)
                _sc = nc.named_scope("AG"); _sc.__enter__()
                cc.inc(nc.gpsimd.collective_compute(
                    "AllGather", OP.bypass,
                    replica_groups=[[0, 1, 2, 3], [4, 5, 6, 7]],
                    ins=[d_agin.ap()], outs=[d_agout.ap()]), 1)
                nc.gpsimd.wait_ge(cc.h, cc.n)
                bar()
                _sc.__exit__(None, None, None)

        if STAGE >= 4:
            _sc = nc.named_scope("P4carry"); _sc.__enter__()
            with MultiCtx(
                nc.sbuf_tensor("sSg", [128, DIN], F32),
                nc.sbuf_tensor("sWb", [128, 4 * 64], F32),
                nc.psum_tensor("p4w", [64, 64], F32),
            ) as (sSg, sWb, p4w):
                nc.vector.memset(s_atote[:], 1.0)
                bar()
                for p in range(4):
                    dma(nc.sync, s_atote[p:p + 1, :],
                        d_agout[p, 128 * DIN:].rearrange("(o h) -> o h", o=1))
                dwait()
                bar()
                nc.tensor.matmul(p4w[0:64, 0:4], s_atote[:], s_mq.ap(),
                                 start=True, stop=True)
                bar()
                nc.scalar.activation(s_tmp64[:, 0:4], p4w[0:64, 0:4], AF.Exp)
                bar()
                nc.tensor.matmul(p4w[0:4, 0:64], s_tmp64[:, 0:4], s_i64[:],
                                 start=True, stop=True)
                bar()
                nc.vector.tensor_copy(s_tmp64b[0:4, 0:64], p4w[0:4, 0:64])
                bar()
                dma(nc.sync, d_wrow.ap(), s_tmp64b[0:4, 0:64])
                dwait()
                bar()
                for p in range(4):
                    dma(nc.sync, sWb[:, p * 64:(p + 1) * 64],
                        d_wrow[p:p + 1, :].broadcast_to([128, H]))
                dwait()
                bar()
                nc.vector.memset(s_R[:], 0.0)
                bar()
                for p in range(4):
                    dma(nc.sync, sSg[:],
                        d_agout[p, 0:128 * DIN]
                        .rearrange("(n f) -> n f", n=128))
                    dwait()
                    bar()
                    nc.vector.tensor_tensor(
                        sSg[:].rearrange("n (h p) -> n h p", p=PH),
                        sSg[:].rearrange("n (h p) -> n h p", p=PH),
                        sWb[:, p * 64:(p + 1) * 64]
                        .rearrange("n (h o) -> n h o", o=1)
                        .broadcast_to([128, H, PH]), OP.mult)
                    bar()
                    nc.vector.tensor_add(s_R[:], s_R[:], sSg[:])
                    bar()
                dump("carry", s_R[:])
            _sc.__exit__(None, None, None)
            _sc = nc.named_scope("P4main"); _sc.__enter__()
            with MultiCtx(
                nc.sbuf_tensor("xtmA", [128, DIN], BF16),
                nc.sbuf_tensor("xtmB", [128, DIN], BF16),
                nc.sbuf_tensor("xdc", [128, DIN], BF16),
                nc.sbuf_tensor("srBh", [128, 32 * CH], F32),
                nc.sbuf_tensor("sLW", [128, 32 * CH], BF16),
                nc.sbuf_tensor("sGt", [128, 128], F32),
                nc.sbuf_tensor("scolT", [128, 64], F32),
                nc.sbuf_tensor("sevT", [128, 64], F32),
                nc.sbuf_tensor("sYtm", [128, DIN], BF16),
                nc.sbuf_tensor("sYo", [128, DIN], BF16),
                nc.sbuf_tensor("syfm", [128, DIN], F32),
                nc.sbuf_tensor("shalf", [128, 2048], F32),
                nc.sbuf_tensor("srs", [1, 128], F32),
                nc.sbuf_tensor("swog0", [128, 32 * 128], BF16),
                nc.sbuf_tensor("swog1", [128, 32 * 128], BF16),
                nc.sbuf_tensor("sxrg", [128, 512], F32),
                nc.sbuf_tensor("sxrb", [128, 512], BF16),
                nc.sbuf_tensor("sxrb2", [128, 512], BF16),
                nc.sbuf_tensor("srowt", [1, 512], F32),
                nc.psum_tensor("pYd", [128, 1024], F32),
                nc.psum_tensor("pYo", [128, 1024], F32),
                nc.psum_tensor("p4s", [128, 128], F32),
                nc.psum_tensor("p4r1", [1, 512], F32),
                nc.psum_tensor("p4r2", [1, 512], F32),
            ) as (xtmA, xtmB, xdc, srBh, sLW, sGt, scolT, sevT, sYtm, sYo,
                  syfm,
                  shalf, srs, swog0, swog1, sxrg, sxrb, sxrb2, srowt,
                  pYd, pYo, p4s, p4r1, p4r2):
                wov = d_woutT.ap().rearrange("(k p) m -> p k m", p=128)
                smk = []
                for m in range(16):
                    buf = (swog0, swog1)[m % 2]
                    wsm = wsp[m % 2]
                    if m >= 2:
                        nc.gpsimd.wait_ge(ss.h, smk[m - 2])
                    dma(nc.gpsimd, buf[:].rearrange("p (k m) -> p k m", m=128),
                        wov[:, :, m * 128:(m + 1) * 128], sem=wsm)
                    nc.vector.wait_ge(wsm.h, wsm.n)
                    zs.inc(nc.vector.tensor_tensor(
                        buf[:].rearrange("p (k m) -> p k m", m=128),
                        buf[:].rearrange("p (k m) -> p k m", m=128),
                        s_nw[:].rearrange("p (k o) -> p k o", o=1)
                        .broadcast_to([128, 32, 128]), OP.mult))
                    nc.sync.wait_ge(zs.h, zs.n)
                    dma(nc.sync,
                        d_woutbf[:, m * 128:(m + 1) * 128]
                        .rearrange("(k p) m -> p k m", p=128),
                        buf[:].rearrange("p (k m) -> p k m", m=128), sem=ss)
                    smk.append(ss.n)
                    nc.vector.wait_ge(ss.h, ss.n)
                nc.sync.wait_ge(ss.h, ss.n)
                bar()
                dump("woutbf", d_woutbf.ap())

                xtms = (xtmA, xtmB)
                dma(nc.sync, xtmA[:], d_xsc[0], transpose=True)
                dma(nc.scalar, xdc[:].rearrange("p (i t) -> p i t", t=128),
                    d_xdsc[0].rearrange("(i p) t -> p i t", p=128))
                dwait()
                dwait(nc.scalar)
                bar()
                for grp in range(4):
                    for c4 in range(4):
                        c = grp * 4 + c4
                        if c + 1 < NCH:
                            dma(nc.sync, xtms[(c + 1) % 2][:], d_xsc[c + 1],
                                transpose=True)
                        xtm = xtms[c % 2]
                        nc.scalar.copy(s_prevbf[:], s_R[:])
                        bar()
                        if "prev5" in d_dbg and c == 5:
                            dump("prev5", s_prevbf[:])
                        nc.vector.tensor_tensor(
                            s_R[:].rearrange("n (h p) -> n h p", p=PH),
                            s_R[:].rearrange("n (h p) -> n h p", p=PH),
                            s_dcb[:, c * H:(c + 1) * H]
                            .rearrange("n (h o) -> n h o", o=1)
                            .broadcast_to([128, H, PH]), OP.mult)
                        nc.tensor.matmul(p4s[:, 0:64],
                                         s_colt[:, c * CH:(c + 1) * CH],
                                         s_i64[:], start=True, stop=True)
                        bar()
                        for half in range(2):
                            dma(nc.sync, shalf[:],
                                d_scsc[c][:, half * 2048:(half + 1) * 2048])
                            dwait()
                            bar()
                            nc.vector.tensor_add(
                                s_R[:, half * 2048:(half + 1) * 2048],
                                s_R[:, half * 2048:(half + 1) * 2048],
                                shalf[:])
                            bar()
                        nc.scalar.copy(scolT[:, 0:64], p4s[:, 0:64])
                        bar()
                        nc.scalar.activation(
                            s_tmp64[:], s_acum[:, c * CH:(c + 1) * CH],
                            AF.Exp)
                        bar()
                        nc.tensor.matmul(p4s[:, 0:64], s_tmp64[:], s_i64[:],
                                         start=True, stop=True)
                        bar()
                        nc.scalar.copy(sevT[:, 0:64], p4s[:, 0:64])
                        bar()
                        nc.tensor.matmul(p4s[:],
                                         s_Bbf[:, c * CH:(c + 1) * CH],
                                         s_Cbf[:, c * CH:(c + 1) * CH],
                                         start=True, stop=True)
                        bar()
                        nc.vector.tensor_copy(sGt[:], p4s[:])
                        bar()
                        for hf in range(2):
                            dma(nc.sync, srBh[:],
                                d_acrow[c:c + 1, hf * 32 * CH:
                                        (hf + 1) * 32 * CH]
                                .broadcast_to([128, 32 * CH]))
                            dwait()
                            bar()
                            nc.vector.tensor_tensor(
                                srBh[:].rearrange("s (h l) -> s h l", l=CH),
                                srBh[:].rearrange("s (h l) -> s h l", l=CH),
                                scolT[:, hf * 32:(hf + 1) * 32]
                                .rearrange("s (h o) -> s h o", o=1)
                                .broadcast_to([128, 32, CH]), OP.add)
                            bar()
                            nc.gpsimd.affine_select(
                                srBh[:], srBh[:], [[0, 32], [1, CH]],
                                OP.is_ge, NEG, base=0, channel_multiplier=-1)
                            bar()
                            nc.scalar.activation(srBh[:], srBh[:], AF.Exp)
                            bar()
                            nc.vector.tensor_tensor(
                                sLW[:].rearrange("s (h l) -> s h l", l=CH),
                                srBh[:].rearrange("s (h l) -> s h l", l=CH),
                                sGt[:].rearrange("s (o l) -> s o l", o=1)
                                .broadcast_to([128, 32, CH]), OP.mult)
                            bar()
                            if "wt" in d_dbg and c == 0 and hf == 0:
                                dump("wt", sLW[:])
                            for qq in range(2):
                                for hh in range(16):
                                    hq = qq * 16 + hh
                                    h = hf * 32 + hq
                                    nc.tensor.matmul(
                                        pYd[:, hh * 64:(hh + 1) * 64],
                                        sLW[:, hq * CH:(hq + 1) * CH],
                                        xtm[:, h * PH:(h + 1) * PH],
                                        start=True, stop=True)
                                    nc.tensor.matmul(
                                        pYo[:, hh * 64:(hh + 1) * 64],
                                        s_Cbf[:, c * CH:(c + 1) * CH],
                                        s_prevbf[:, h * PH:(h + 1) * PH],
                                        start=True, stop=True)
                                bar()
                                nc.scalar.copy(
                                    sYtm[:, (hf * 32 + qq * 16) * PH:
                                         (hf * 32 + qq * 16 + 16) * PH],
                                    pYd[:])
                                for hh in range(16):
                                    h = hf * 32 + qq * 16 + hh
                                    nc.scalar.activation(
                                        sYo[:, h * PH:(h + 1) * PH],
                                        pYo[:, hh * 64:(hh + 1) * 64],
                                        AF.Copy, scale=sevT[:, h:h + 1])
                                bar()
                        for qr in range(4):
                            for i8 in range(8):
                                i = qr * 8 + i8
                                nc.tensor.matmul(
                                    pYd[:, i8 * 128:(i8 + 1) * 128],
                                    sYtm[:, i * 128:(i + 1) * 128],
                                    s_ibf[:], start=True, stop=False)
                                nc.tensor.matmul(
                                    pYd[:, i8 * 128:(i8 + 1) * 128],
                                    sYo[:, i * 128:(i + 1) * 128],
                                    s_ibf[:], start=False, stop=True)
                            bar()
                            nc.vector.tensor_add(
                                syfm[:, qr * 1024:(qr + 1) * 1024], pYd[:],
                                xdc[:, qr * 1024:(qr + 1) * 1024])
                            bar()
                        if c + 1 < NCH:
                            dma(nc.scalar,
                                xdc[:].rearrange("p (i t) -> p i t", t=128),
                                d_xdsc[c + 1]
                                .rearrange("(i p) t -> p i t", p=128))
                        if "yfm" in d_dbg and c == 0:
                            dump("yfm", syfm[:])
                        if "yfm5" in d_dbg and c == 5:
                            dump("yfm5", syfm[:])
                        for half in range(2):
                            dma(nc.sync,
                                shalf[:].rearrange("p (i t) -> p i t", t=128),
                                d_zsc[half * 2048:(half + 1) * 2048,
                                      c * CH:(c + 1) * CH]
                                .rearrange("(i p) t -> p i t", p=128))
                            dwait()
                            bar()
                            nc.vector.tensor_mul(
                                syfm[:, half * 2048:(half + 1) * 2048],
                                syfm[:, half * 2048:(half + 1) * 2048],
                                shalf[:])
                            bar()
                        nc.scalar.activation(sLW[:], syfm[:], AF.Square)
                        bar()
                        for i in range(32):
                            nc.tensor.matmul(p4r1[0:1, 0:128], s_onesb[:],
                                             sLW[:, i * 128:(i + 1) * 128],
                                             start=(i == 0), stop=(i == 31))
                        bar()
                        nc.vector.tensor_scalar(srs[:], p4r1[0:1, 0:128],
                                                1.0 / DIN, EPS, OP.mult,
                                                OP.add)
                        bar()
                        nc.scalar.activation(srs[:], srs[:], AF.Sqrt)
                        bar()
                        nc.vector.reciprocal(srs[:], srs[:])
                        bar()
                        nc.tensor.matmul(p4s[:], s_ones1[:], srs[:],
                                         start=True, stop=True)
                        bar()
                        nc.vector.tensor_tensor(
                            s_yng[:].rearrange("p (i t) -> p i t", t=512)
                            [:, :, c4 * 128:(c4 + 1) * 128],
                            syfm[:].rearrange("p (i t) -> p i t", t=128),
                            p4s[:].rearrange("p (o t) -> p o t", o=1)
                            .broadcast_to([128, 32, 128]), OP.mult)
                        bar()
                    if "ynorm" in d_dbg and grp == 0:
                        dump("ynorm", s_yng[:])
                    wobv = d_woutbf.ap().rearrange("(k p) m -> p k m", p=128)
                    wmarks = []
                    dma(nc.gpsimd,
                        swog0[:].rearrange("p (k m) -> p k m", m=128),
                        wobv[:, :, 0:128], sem=wsp[0])
                    wmarks.append((wsp[0], wsp[0].n))
                    for m in range(16):
                        buf = (swog0, swog1)[m % 2]
                        if m + 1 < 16:
                            wsm = wsp[(m + 1) % 2]
                            dma(nc.gpsimd,
                                (swog0, swog1)[(m + 1) % 2][:]
                                .rearrange("p (k m) -> p k m", m=128),
                                wobv[:, :, (m + 1) * 128:(m + 2) * 128],
                                sem=wsm)
                            wmarks.append((wsm, wsm.n))
                        nc.tensor.wait_ge(wmarks[m][0].h, wmarks[m][1])
                        for k in range(32):
                            nc.tensor.matmul(
                                pYd[:, 0:512], buf[:, k * 128:(k + 1) * 128],
                                s_yng[:, k * 512:(k + 1) * 512],
                                start=(k == 0), stop=(k == 31))
                        nc.sync.wait_ge(srx.h, srx.n)
                        dma(nc.sync, sxrg[:],
                            d_uT[m * 128:(m + 1) * 128,
                                 3 + grp * 512:3 + (grp + 1) * 512])
                        dwait()
                        bar()
                        nc.vector.tensor_add(sxrg[:], sxrg[:], pYd[:, 0:512])
                        bar()
                        dma(nc.sync,
                            d_xres[m * 128:(m + 1) * 128,
                                   grp * 512:(grp + 1) * 512], sxrg[:],
                            sem=srx)
                        nc.scalar.activation(sxrb[:], sxrg[:], AF.Copy)
                        nc.scalar.activation(sxrb2[:], sxrg[:], AF.Square)
                        bar()
                        nc.tensor.matmul(p4r1[0:1, 0:512], s_onesb[:],
                                         sxrb[:], start=(m == 0),
                                         stop=(m == 15))
                        nc.tensor.matmul(p4r2[0:1, 0:512], s_onesb[:],
                                         sxrb2[:], start=(m == 0),
                                         stop=(m == 15))
                        bar()
                    nc.vector.tensor_copy(srowt[:], p4r1[0:1, 0:512])
                    bar()
                    dma(nc.sync, d_lnrow[0:1, grp * 512:(grp + 1) * 512],
                        srowt[:])
                    dwait()
                    bar()
                    nc.vector.tensor_copy(srowt[:], p4r2[0:1, 0:512])
                    bar()
                    dma(nc.sync, d_lnrow[1:2, grp * 512:(grp + 1) * 512],
                        srowt[:])
                    dwait()
                    bar()
                nc.sync.wait_ge(srx.h, srx.n)
                nc.sync.wait_ge(ss.h, ss.n)
                bar()
                dump("xres", d_xres.ap())
            _sc.__exit__(None, None, None)

        _s34.__exit__(None, None, None)

    if STAGE >= 6:
        _sc = nc.named_scope("P6"); _sc.__enter__()
        _p67 = MultiCtx(
            nc.sbuf_tensor("s_xp1", [128, 16 * 2048], BF16),
            nc.sbuf_tensor("s_mu1", [1, TOK], F32),
            nc.sbuf_tensor("s_rs1", [1, TOK], F32),
        )
        s_xp1, s_mu1, s_rs1 = _p67.__enter__()
        with MultiCtx(
            nc.sbuf_tensor("s_mu1b", [128, 2048], F32),
            nc.sbuf_tensor("s_rs1b", [128, 2048], F32),
            nc.sbuf_tensor("sxr", [128, 2048], F32),
            nc.sbuf_tensor("srowa", [1, 2048], F32),
            nc.sbuf_tensor("sg1", [64, 2048], F32),
            nc.sbuf_tensor("sltm", [128, 16 * 64], F32),
            nc.sbuf_tensor("gt0", [128, 64], F32),
            nc.sbuf_tensor("gt1", [128, 64], F32),
            nc.sbuf_tensor("gt2", [128, 8], F32),
            nc.sbuf_tensor("gt3", [128, 8], F32),
            nc.sbuf_tensor("gt4", [128, 8], F32),
            nc.sbuf_tensor("gcol", [128, 1], F32),
            nc.sbuf_tensor("gstg", [64, 128], BF16),
            nc.sbuf_tensor("gstgf", [64, 128], F32),
            nc.psum_tensor("p6", [128, 2048], F32),
            nc.psum_tensor("p6b", [64, 512], F32),
        ) as (s_mu1b, s_rs1b, sxr, srowa, sg1, sltm, gt0, gt1, gt2, gt3,
              gt4, gcol, gstg, gstgf, p6, p6b):
            dma(nc.sync, s_mu1[:], d_lnrow[0:1, :])
            dma(nc.sync, s_rs1[:], d_lnrow[1:2, :])
            dwait()
            bar()
            nc.vector.tensor_scalar_mul(s_mu1[:], s_mu1[:], 1.0 / DM)
            bar()
            nc.vector.tensor_scalar_mul(s_rs1[:], s_rs1[:], 1.0 / DM)
            bar()
            nc.vector.tensor_tensor(srowa[:], s_mu1[:], s_mu1[:], OP.mult)
            bar()
            nc.vector.tensor_sub(s_rs1[:], s_rs1[:], srowa[:])
            bar()
            nc.vector.tensor_scalar(s_rs1[:], s_rs1[:], EPS, None, OP.add)
            bar()
            nc.scalar.activation(s_rs1[:], s_rs1[:], AF.Sqrt)
            bar()
            nc.vector.reciprocal(s_rs1[:], s_rs1[:])
            bar()
            dma(nc.sync, d_murs[0:1, :], s_mu1[:])
            dma(nc.sync, d_murs[1:2, :], s_rs1[:])
            dwait()
            bar()
            for t in range(4):
                nc.tensor.matmul(p6[:, t * 512:(t + 1) * 512], s_ones1[:],
                                 s_mu1[0:1, t * 512:(t + 1) * 512],
                                 start=True, stop=True)
            bar()
            nc.scalar.copy(s_mu1b[:], p6[:])
            bar()
            for t in range(4):
                nc.tensor.matmul(p6[:, t * 512:(t + 1) * 512], s_ones1[:],
                                 s_rs1[0:1, t * 512:(t + 1) * 512],
                                 start=True, stop=True)
            bar()
            nc.scalar.copy(s_rs1b[:], p6[:])
            bar()
            for m in range(16):
                dma(nc.sync, sxr[:], d_xres[m * 128:(m + 1) * 128, :])
                dwait()
                bar()
                nc.vector.tensor_sub(sxr[:], sxr[:], s_mu1b[:])
                bar()
                nc.vector.tensor_tensor(s_xp1[:, m * 2048:(m + 1) * 2048],
                                        sxr[:], s_rs1b[:], OP.mult)
                bar()
            dump("xp1", s_xp1[:])
            for t in range(4):
                for k in range(16):
                    nc.tensor.matmul(
                        p6[0:64, t * 512:(t + 1) * 512],
                        s_gateT[:, k * 64:(k + 1) * 64],
                        s_xp1[:, k * 2048 + t * 512:k * 2048 + (t + 1) * 512],
                        start=(k == 0), stop=(k == 15))
            bar()
            nc.vector.tensor_copy(sg1[:], p6[0:64, :])
            bar()
            for t in range(4):
                nc.tensor.matmul(p6[0:64, t * 512:(t + 1) * 512], s_PT[:],
                                 sg1[:, t * 512:(t + 1) * 512],
                                 start=True, stop=True)
            bar()
            nc.vector.tensor_copy(sg1[:], p6[0:64, :])
            bar()
            for i in range(16):
                nc.tensor.matmul(p6[:, i * 64:(i + 1) * 64],
                                 sg1[:, i * 128:(i + 1) * 128], s_i64[:],
                                 start=True, stop=True)
            bar()
            nc.scalar.copy(sltm[:], p6[:, 0:1024])
            bar()
            dump("scoretm", sltm[:])
            for i in range(16):
                lg = sltm[:, i * 64:(i + 1) * 64]
                nc.vector.reduce_max(gcol[:], lg, AX.X, negate=True)
                bar()
                nc.scalar.activation(gt0[:], lg, AF.Exp, bias=gcol[:])
                bar()
                nc.vector.reduce_sum(gcol[:], gt0[:], AX.X)
                bar()
                nc.vector.reciprocal(gcol[:], gcol[:])
                bar()
                nc.vector.tensor_scalar_mul(gt0[:], gt0[:], gcol[:])
                bar()
                nc.vector.max(out=gt2[:], in_=gt0[:])
                bar()
                nc.vector.tensor_scalar(gt1[:], gt0[:], gt2[:, 7:8], None,
                                        OP.is_ge)
                bar()
                nc.vector.tensor_mul(gt1[:], gt1[:], gt0[:])
                bar()
                nc.vector.tensor_reduce(
                    gt3[:], gt1[:].rearrange("p (g i) -> p g i", i=8), AX.X,
                    OP.add)
                bar()
                nc.vector.tensor_scalar(gt1[:], gt1[:], 0.0, None, OP.is_gt)
                bar()
                nc.vector.scalar_tensor_tensor(gt0[:], gt0[:], 100.0, gt1[:],
                                               OP.add, OP.mult)
                bar()
                nc.vector.tensor_scalar(gt0[:], gt0[:], 100.0, None,
                                        OP.subtract)
                bar()
                nc.vector.tensor_reduce(
                    gt4[:], gt0[:].rearrange("p (g i) -> p g i", i=8), AX.X,
                    OP.max, negate=True)
                bar()
                nc.vector.tensor_tensor(
                    gt0[:].rearrange("p (g i) -> p g i", i=8),
                    gt0[:].rearrange("p (g i) -> p g i", i=8),
                    gt4[:].rearrange("p (g o) -> p g o", o=1)
                    .broadcast_to([128, 8, 8]), OP.add)
                bar()
                nc.scalar.activation(gt0[:], gt0[:], AF.Exp)
                bar()
                nc.vector.tensor_reduce(
                    gt4[:], gt0[:].rearrange("p (g i) -> p g i", i=8), AX.X,
                    OP.add)
                bar()
                nc.vector.reciprocal(gt4[:], gt4[:])
                bar()
                nc.vector.tensor_tensor(
                    gt0[:].rearrange("p (g i) -> p g i", i=8),
                    gt0[:].rearrange("p (g i) -> p g i", i=8),
                    gt4[:].rearrange("p (g o) -> p g o", o=1)
                    .broadcast_to([128, 8, 8]), OP.mult)
                bar()
                nc.tensor.matmul(p6b[0:64, 0:128], gt0[:], s_i128[:],
                                 start=True, stop=True)
                bar()
                nc.scalar.copy(gstg[:], p6b[0:64, 0:128])
                bar()
                for g in range(NG):
                    dma(nc.sync, d_soft8[:, g, i * 128:(i + 1) * 128],
                        gstg[g * 8:(g + 1) * 8, :])
                nc.tensor.matmul(p6b[0:8, 0:128], gt3[:], s_i128[:],
                                 start=True, stop=True)
                bar()
                nc.vector.tensor_copy(gstgf[0:8, :], p6b[0:8, 0:128])
                bar()
                for g in range(NG):
                    dma(nc.sync, d_scrow[g:g + 1, i * 128:(i + 1) * 128],
                        gstgf[g:g + 1, :])
                dwait()
                bar()
            dump("soft8", d_soft8.ap())
            dump("scrow", d_scrow.ap())
        _sc.__exit__(None, None, None)

        if STAGE >= 7:
            _sc = nc.named_scope("P7"); _sc.__enter__()
            with MultiCtx(
                nc.sbuf_tensor("fw0", [128, 16 * 128], BF16),
                nc.sbuf_tensor("fw1", [128, 16 * 128], BF16),
                nc.sbuf_tensor("sgg", [128, 2048], F32),
                nc.sbuf_tensor("sab", [128, 11 * 2048], BF16),
                nc.sbuf_tensor("sSgb", [128, 2048], F32),
                nc.sbuf_tensor("swpt", [8, 128], BF16),
                nc.sbuf_tensor("syd0", [128, 2048], F32),
                nc.sbuf_tensor("syd1", [128, 2048], F32),
                nc.sbuf_tensor("s7s", [8, 2048], BF16),
                nc.sbuf_tensor("s7r", [1, 2048], F32),
                nc.psum_tensor("f7a", [128, 1024], F32),
                nc.psum_tensor("f7b", [128, 1024], F32),
                nc.psum_tensor("f7t", [8, 128], F32),
            ) as (fw0, fw1, sgg, sab, sSgb, swpt, syd0, syd1, s7s, s7r,
                  f7a, f7b, f7t):
                fws = (fw0, fw1)
                f7ms = (f7a, f7b)
                ydg = {}
                for g in range(NG):
                    dma(nc.sync, s7s[:], d_soft8[:, g, :])
                    dma(nc.sync, s7r[:], d_scrow[g:g + 1, :])
                    dwait()
                    bar()
                    for t in range(4):
                        nc.tensor.matmul(
                            f7ms[t // 2][:, (t % 2) * 512:(t % 2 + 1) * 512],
                            s_ones1[:], s7r[0:1, t * 512:(t + 1) * 512],
                            start=True, stop=True)
                    bar()
                    nc.scalar.copy(sSgb[:, 0:1024], f7a[:])
                    nc.scalar.copy(sSgb[:, 1024:2048], f7b[:])
                    bar()
                    emk = {0: es.n, 1: es.n}
                    cmk = {0: cs.n, 1: cs.n}
                    zmk = zs.n
                    gvw = d_gpT[g].rearrange("(k p) i -> p k i", p=128)
                    uvw = d_upT[g].rearrange("(k p) i -> p k i", p=128)
                    wvs = (gvw, uvw)
                    wmarks = []
                    dma(nc.gpsimd, fw0[:].rearrange("p (k m) -> p k m", m=128),
                        gvw[:, :, 0:128], sem=wsp[0])
                    wmarks.append((wsp[0], wsp[0].n))
                    for step in range(22):
                        m, proj = step // 2, step % 2
                        buf = fws[step % 2]
                        if step + 1 < 22:
                            m2, p2 = (step + 1) // 2, (step + 1) % 2
                            wsm = wsp[(step + 1) % 2]
                            nc.gpsimd.wait_ge(ps.h, ps.n)
                            dma(nc.gpsimd,
                                fws[(step + 1) % 2][:]
                                .rearrange("p (k m) -> p k m", m=128),
                                wvs[p2][:, :, m2 * 128:(m2 + 1) * 128],
                                sem=wsm)
                            wmarks.append((wsm, wsm.n))
                        nc.tensor.wait_ge(wmarks[step][0].h, wmarks[step][1])
                        nc.tensor.wait_ge(zs.h, zmk)
                        for k in range(16):
                            _mm = nc.tensor.matmul(
                                f7t[:], s_mgT[:, k * 64 + g * 8:
                                              k * 64 + g * 8 + 8],
                                buf[:, k * 128:(k + 1) * 128],
                                start=(k == 0), stop=(k == 15))
                        ps.inc(_mm)
                        nc.vector.wait_ge(ps.h, ps.n)
                        zs.inc(nc.vector.tensor_copy(swpt[:], f7t[:]))
                        zmk = zs.n
                        for half in range(2):
                            pm = f7ms[half]
                            nc.tensor.wait_ge(es.h, emk[half])
                            for k in range(16):
                                for t2 in range(2):
                                    nc.tensor.matmul(
                                        pm[:, t2 * 512:(t2 + 1) * 512],
                                        buf[:, k * 128:(k + 1) * 128],
                                        s_xp1[:, k * 2048 + (half * 2 + t2)
                                              * 512:k * 2048 +
                                              (half * 2 + t2 + 1) * 512],
                                        start=(k == 0), stop=False)
                            if half == 0:
                                nc.tensor.wait_ge(zs.h, zs.n)
                            for t2 in range(2):
                                _mm = nc.tensor.matmul(
                                    pm[:, t2 * 512:(t2 + 1) * 512],
                                    swpt[:],
                                    s7s[:, (half * 2 + t2) * 512:
                                        (half * 2 + t2 + 1) * 512],
                                    start=False, stop=True)
                            ps.inc(_mm)
                            if proj == 0:
                                nc.scalar.wait_ge(ps.h, ps.n)
                                nc.scalar.wait_ge(cs.h, cmk[half])
                                es.inc(nc.scalar.activation(
                                    sgg[:, half * 1024:(half + 1) * 1024],
                                    pm[:], AF.Silu))
                                emk[half] = es.n
                            else:
                                nc.vector.wait_ge(ps.h, ps.n)
                                es.inc(nc.vector.tensor_mul(
                                    sgg[:, half * 1024:(half + 1) * 1024],
                                    sgg[:, half * 1024:(half + 1) * 1024],
                                    pm[:]))
                                emk[half] = es.n
                                nc.gpsimd.wait_ge(es.h, es.n)
                                cs.inc(nc.gpsimd.tensor_tensor(
                                    sab[:, m * 2048 + half * 1024:
                                        m * 2048 + (half + 1) * 1024],
                                    sgg[:, half * 1024:(half + 1) * 1024],
                                    sSgb[:, half * 1024:(half + 1) * 1024],
                                    OP.mult))
                                cmk[half] = cs.n
                    nc.tensor.wait_ge(cs.h, cs.n)
                    wv = d_dnT[g].rearrange("(k p) m -> p k m", p=128)
                    wmarks = []
                    dma(nc.gpsimd,
                        fw0[:, 0:11 * 128].rearrange("p (k m) -> p k m", m=128),
                        wv[:, :, 0:128], sem=wsp[0])
                    wmarks.append((wsp[0], wsp[0].n))
                    for m in range(16):
                        buf = fws[m % 2]
                        syd = (syd0, syd1)[m % 2]
                        if m + 1 < 16:
                            wsm = wsp[(m + 1) % 2]
                            nc.gpsimd.wait_ge(ps.h, ps.n)
                            dma(nc.gpsimd,
                                fws[(m + 1) % 2][:, 0:11 * 128]
                                .rearrange("p (k m) -> p k m", m=128),
                                wv[:, :, (m + 1) * 128:(m + 2) * 128],
                                sem=wsm)
                            wmarks.append((wsm, wsm.n))
                        nc.tensor.wait_ge(wmarks[m][0].h, wmarks[m][1])
                        for half in range(2):
                            pm = f7ms[half]
                            nc.tensor.wait_ge(es.h, emk[half])
                            for k in range(11):
                                for t2 in range(2):
                                    _mm = nc.tensor.matmul(
                                        pm[:, t2 * 512:(t2 + 1) * 512],
                                        buf[:, k * 128:(k + 1) * 128],
                                        sab[:, k * 2048 + (half * 2 + t2)
                                            * 512:k * 2048 +
                                            (half * 2 + t2 + 1) * 512],
                                        start=(k == 0), stop=(k == 10))
                            ps.inc(_mm)
                            nc.scalar.wait_ge(ps.h, ps.n)
                            if m % 2 in ydg:
                                nc.scalar.wait_ge(ya.h, ydg[m % 2])
                            es.inc(nc.scalar.copy(
                                syd[:, half * 1024:(half + 1) * 1024],
                                pm[:]))
                            emk[half] = es.n
                        nc.gpsimd.wait_ge(es.h, es.n)
                        dma(nc.gpsimd, d_y2[m * 128:(m + 1) * 128, :],
                            syd[:], sem=ya,
                            accum_op=(OP.bypass if g == 0 else OP.add))
                        ydg[m % 2] = ya.n
                nc.sync.wait_ge(ya.h, ya.n)
                bar()
                dump("y2", d_y2.ap())
            _sc.__exit__(None, None, None)
        _p67.__exit__(None, None, None)

    if STAGE >= 8:
        _sc = nc.named_scope("P8"); _sc.__enter__()
        with MultiCtx(
            nc.sbuf_tensor("sxr8", [128, 2048], F32),
            nc.sbuf_tensor("sy28", [128, 2048], F32),
            nc.sbuf_tensor("sxb8", [128, 2048], BF16),
            nc.sbuf_tensor("sxb8b", [128, 2048], BF16),
            nc.sbuf_tensor("s_m1b", [128, 2048], F32),
            nc.sbuf_tensor("s_r1b", [128, 2048], F32),
            nc.sbuf_tensor("s_m2b", [128, 2048], F32),
            nc.sbuf_tensor("s_r2b", [128, 2048], F32),
            nc.sbuf_tensor("srow8", [1, 2 * 2048], F32),
        ) as (sxr8, sy28, sxb8, sxb8b, s_m1b, s_r1b, s_m2b, s_r2b, srow8):
            dma(nc.sync, srow8[0:1, 0:2048], d_murs[0:1, :])
            dma(nc.sync, srow8[0:1, 2048:4096], d_murs[1:2, :])
            dwait()
            bar()
            with nc.psum_tensor("p8c", [128, 2048], F32) as p8c:
                for t in range(4):
                    nc.tensor.matmul(p8c[:, t * 512:(t + 1) * 512],
                                     s_ones1[:],
                                     srow8[0:1, t * 512:(t + 1) * 512],
                                     start=True, stop=True)
                bar()
                nc.scalar.copy(s_m1b[:], p8c[:])
                bar()
                for t in range(4):
                    nc.tensor.matmul(p8c[:, t * 512:(t + 1) * 512],
                                     s_ones1[:],
                                     srow8[0:1, 2048 + t * 512:
                                           2048 + (t + 1) * 512],
                                     start=True, stop=True)
                bar()
                nc.scalar.copy(s_r1b[:], p8c[:])
                bar()

            def t2_chunk(m):
                dma(nc.sync, sxr8[:], d_xres[m * 128:(m + 1) * 128, :])
                dma(nc.scalar, sy28[:], d_y2[m * 128:(m + 1) * 128, :])
                dwait()
                dwait(nc.scalar)
                bar()
                nc.vector.tensor_sub(sxr8[:], sxr8[:], s_m1b[:])
                bar()
                nc.vector.tensor_mul(sxr8[:], sxr8[:], s_r1b[:])
                bar()
                nc.vector.tensor_add(sxr8[:], sxr8[:], sy28[:])
                bar()

            with MultiCtx(
                nc.psum_tensor("p8r1", [1, 2048], F32),
                nc.psum_tensor("p8r2", [1, 2048], F32),
            ) as (p8r1, p8r2):
                for m in range(16):
                    t2_chunk(m)
                    nc.scalar.activation(sxb8[:], sxr8[:], AF.Copy)
                    nc.scalar.activation(sxb8b[:], sxr8[:], AF.Square)
                    bar()
                    for t in range(4):
                        nc.tensor.matmul(
                            p8r1[0:1, t * 512:(t + 1) * 512], s_onesb[:],
                            sxb8[:, t * 512:(t + 1) * 512],
                            start=(m == 0), stop=(m == 15))
                        nc.tensor.matmul(
                            p8r2[0:1, t * 512:(t + 1) * 512], s_onesb[:],
                            sxb8b[:, t * 512:(t + 1) * 512],
                            start=(m == 0), stop=(m == 15))
                    bar()
                nc.vector.tensor_copy(srow8[0:1, 0:2048], p8r1[0:1, :])
                bar()
                nc.vector.tensor_copy(srow8[0:1, 2048:4096], p8r2[0:1, :])
                bar()
            nc.vector.tensor_scalar_mul(srow8[0:1, 0:2048],
                                        srow8[0:1, 0:2048], 1.0 / DM)
            bar()
            nc.vector.tensor_scalar_mul(srow8[0:1, 2048:4096],
                                        srow8[0:1, 2048:4096], 1.0 / DM)
            bar()
            nc.vector.tensor_tensor(sy28[0:1, 0:2048], srow8[0:1, 0:2048],
                                    srow8[0:1, 0:2048], OP.mult)
            bar()
            nc.vector.tensor_sub(srow8[0:1, 2048:4096],
                                 srow8[0:1, 2048:4096], sy28[0:1, 0:2048])
            bar()
            nc.vector.tensor_scalar(srow8[0:1, 2048:4096],
                                    srow8[0:1, 2048:4096], EPS, None, OP.add)
            bar()
            nc.scalar.activation(srow8[0:1, 2048:4096],
                                 srow8[0:1, 2048:4096], AF.Sqrt)
            bar()
            nc.vector.reciprocal(srow8[0:1, 2048:4096],
                                 srow8[0:1, 2048:4096])
            bar()
            with nc.psum_tensor("p8b", [128, 2048], F32) as p8b:
                for t in range(4):
                    nc.tensor.matmul(p8b[:, t * 512:(t + 1) * 512],
                                     s_ones1[:],
                                     srow8[0:1, t * 512:(t + 1) * 512],
                                     start=True, stop=True)
                bar()
                nc.scalar.copy(s_m2b[:], p8b[:])
                bar()
                for t in range(4):
                    nc.tensor.matmul(p8b[:, t * 512:(t + 1) * 512],
                                     s_ones1[:],
                                     srow8[0:1, 2048 + t * 512:
                                           2048 + (t + 1) * 512],
                                     start=True, stop=True)
                bar()
                nc.scalar.copy(s_r2b[:], p8b[:])
                bar()
            for m in range(16):
                t2_chunk(m)
                nc.vector.tensor_sub(sxr8[:], sxr8[:], s_m2b[:])
                bar()
                nc.vector.tensor_mul(sxr8[:], sxr8[:], s_r2b[:])
                bar()
                dma(nc.sync, d_out[m * 128:(m + 1) * 128, :], sxr8[:],
                    sem=ss)
                nc.sync.wait_ge(ss.h, ss.n)
                bar()

        _sc.__exit__(None, None, None)
    if STAGE < 8:
        with nc.sbuf_tensor("sfin", [128, TOK], F32) as sfin:
            nc.vector.memset(sfin[:], 0.0)
            bar()
            for m in range(16):
                dma(nc.sync, d_out[m * 128:(m + 1) * 128, :], sfin[:],
                    sem=ss)
            nc.sync.wait_ge(ss.h, ss.n)
            bar()
    nc.sync.wait_ge(ds.h, ds.n)
    nc.sync.wait_ge(ss.h, ss.n)
    bar()
    return nc



def _prep(inputs):
    ident = np.eye(128, dtype=ml_dtypes.bfloat16)
    i64 = np.eye(64, dtype=np.float32)
    i128 = np.eye(128, dtype=np.float32)
    onesb = np.ones((128, 1), dtype=ml_dtypes.bfloat16)
    ones1 = np.ones((1, 128), dtype=np.float32)
    W_inT = np.ascontiguousarray(np.asarray(inputs["L0_attn_W_in"]).T)
    woutT = np.ascontiguousarray(np.asarray(inputs["L0_attn_W_out"]).T)
    gatewT = np.ascontiguousarray(np.asarray(inputs["L0_ffn_gate_w"]).T)
    mupT = np.ascontiguousarray(
        np.asarray(inputs["L0_ffn_mask_up_w"]).transpose(1, 0, 2)
        .reshape(DM, NG * 8))
    gpT = np.ascontiguousarray(np.asarray(inputs["L0_ffn_gate_proj_w"])
                               .transpose(0, 2, 1))
    upT = np.ascontiguousarray(np.asarray(inputs["L0_ffn_up_proj_w"])
                               .transpose(0, 2, 1))
    dnT = np.ascontiguousarray(np.asarray(inputs["L0_ffn_down_proj_w"])
                               .transpose(0, 2, 1))
    drep = np.ascontiguousarray(np.repeat(np.asarray(inputs["L0_attn_D"]),
                                          PH))
    invmap = np.ascontiguousarray(
        np.asarray(inputs["L0_ffn_inv_map"]).reshape(-1).astype(np.int32))
    u = np.asarray(inputs["L0_attn_u"])
    shared = dict(
        winT=W_inT, convw=np.ascontiguousarray(inputs["L0_attn_conv_w"]),
        convb=np.ascontiguousarray(inputs["L0_attn_conv_b"]),
        dtbias=np.ascontiguousarray(inputs["L0_attn_dt_bias"]),
        alog=np.ascontiguousarray(inputs["L0_attn_A_log"]),
        drep=drep, nw=np.ascontiguousarray(inputs["L0_attn_norm_w"]),
        woutT=woutT, gatewT=gatewT, invmap=invmap, mupT=mupT,
        gpT=gpT, upT=upT, dnT=dnT, ibf=ident, i64=i64, i128=i128,
        onesb=onesb, ones1=ones1)
    in_maps = []
    for core in range(8):
        b, qq = core // 4, core % 4
        t0 = qq * TOK
        uslice = np.zeros((TOKH, DM), dtype=np.float32)
        lo = t0 - 3
        src = u[b, max(lo, 0):t0 + TOK]
        uslice[TOKH - src.shape[0]:] = src
        uT = np.ascontiguousarray(uslice.T)
        mq = np.zeros((5, 4), dtype=np.float32)
        mq[4, :] = NEG
        for pq in range(4):
            if pq < qq:
                mq[4, pq] = 0.0
                for rq in range(4):
                    if pq < rq < qq:
                        mq[rq, pq] = 1.0
        m = dict(shared)
        m["uT"] = uT
        m["mq"] = mq
        in_maps.append(m)
    return in_maps


def kernel(**inputs):
    in_maps = _prep(inputs)
    nc = build_nc()
    res = run_bass_kernel_spmd(nc, in_maps, core_ids=list(range(8)))
    out = np.zeros((2, 8192, DM), dtype=np.float32)
    for core in range(8):
        b, qq = core // 4, core % 4
        out[b, qq * TOK:(qq + 1) * TOK] = res.results[core]["out"].T
    return out


# version 03
# speedup vs baseline: 1.3228x; 1.1240x over previous
import numpy as np
import ml_dtypes
import concourse.bass as bass
import concourse.mybir as mybir
from concourse.bass_utils import run_bass_kernel_spmd

F32 = mybir.dt.float32
BF16 = mybir.dt.bfloat16
I32 = mybir.dt.int32
OP = mybir.AluOpType
AF = mybir.ActivationFunctionType
AX = mybir.AxisListType

DM = 2048
DIN = 4096
DST = 128
H = 64
PH = 64
CONVD = 4352
DIP = 8512
TOK = 2048
TOKH = 2051
CH = 128
NCH = 16
IF = 1408
NG = 8
E = 64
NEG = -1.0e30
EPS = 1e-5
NM_Z = 32
NM_XBC = 34
M_DT = 66
NM = 67

STAGE = 99
P1CAP = 999


class MultiCtx:

    def __init__(self, *cms):
        self.cms = cms

    def __enter__(self):
        from contextlib import ExitStack
        self.st = ExitStack()
        return tuple(self.st.enter_context(c) for c in self.cms)

    def __exit__(self, *a):
        return self.st.__exit__(*a)


class Sem:
    def __init__(self, nc, name):
        self.h = nc.alloc_semaphore(name)
        self.n = 0

    def inc(self, instr, k=1):
        instr.then_inc(self.h, k)
        self.n += k
        return instr


def build_nc(dbg=()):
    nc = bass.Bass("TRN2", target_bir_lowering=False,
                   detect_race_conditions=False)

    d_uT = nc.dram_tensor("uT", [DM, TOKH], F32, kind="ExternalInput")
    d_winT = nc.dram_tensor("winT", [DM, DIP], F32, kind="ExternalInput")
    d_convw = nc.dram_tensor("convw", [CONVD, 4], F32, kind="ExternalInput")
    d_convb = nc.dram_tensor("convb", [CONVD], F32, kind="ExternalInput")
    d_dtbias = nc.dram_tensor("dtbias", [H], F32, kind="ExternalInput")
    d_alog = nc.dram_tensor("alog", [H], F32, kind="ExternalInput")
    d_drep = nc.dram_tensor("drep", [DIN], F32, kind="ExternalInput")
    d_nw = nc.dram_tensor("nw", [DIN], F32, kind="ExternalInput")
    d_woutT = nc.dram_tensor("woutT", [DIN, DM], F32, kind="ExternalInput")
    d_gatewT = nc.dram_tensor("gatewT", [DM, E], F32, kind="ExternalInput")
    d_invmap = nc.dram_tensor("invmap", [E], I32, kind="ExternalInput")
    d_mupT = nc.dram_tensor("mupT", [DM, NG * 8], F32, kind="ExternalInput")
    d_gpT = nc.dram_tensor("gpT", [NG, DM, IF], F32, kind="ExternalInput")
    d_upT = nc.dram_tensor("upT", [NG, DM, IF], F32, kind="ExternalInput")
    d_dnT = nc.dram_tensor("dnT", [NG, IF, DM], F32, kind="ExternalInput")
    d_mq = nc.dram_tensor("mq", [5, 4], F32, kind="ExternalInput")
    d_ibf = nc.dram_tensor("ibf", [128, 128], BF16, kind="ExternalInput")
    d_i64 = nc.dram_tensor("i64", [64, 64], F32, kind="ExternalInput")
    d_i128 = nc.dram_tensor("i128", [128, 128], F32, kind="ExternalInput")
    d_onesb = nc.dram_tensor("onesb", [128, 1], BF16, kind="ExternalInput")
    d_ones1 = nc.dram_tensor("ones1", [1, 128], F32, kind="ExternalInput")

    d_out = nc.dram_tensor("out", [DM, TOK], F32, kind="ExternalOutput")

    d_zsc = nc.dram_tensor("zsc", [DIN, TOK], F32, kind="Internal")
    d_xsc = nc.dram_tensor("xsc", [NCH, DIN, CH], BF16, kind="Internal")
    d_xdsc = nc.dram_tensor("xdsc", [NCH, DIN, CH], BF16, kind="Internal")
    d_scsc = nc.dram_tensor("scsc", [NCH, 128, DIN], F32, kind="Internal")
    d_acrow = nc.dram_tensor("acrow", [NCH, H * CH], F32, kind="Internal")
    d_dcrow = nc.dram_tensor("dcrow", [NCH, H], F32, kind="Internal")
    d_wrow = nc.dram_tensor("wrow", [4, H], F32, kind="Internal")
    d_xres = nc.dram_tensor("xres", [DM, TOK], F32, kind="Internal")
    d_y2 = nc.dram_tensor("y2", [DM, TOK], F32, kind="Internal")
    d_woutbf = nc.dram_tensor("woutbf", [DIN, DM], BF16, kind="Internal")
    d_agin = nc.dram_tensor("agin", [128 * DIN + 64], F32, kind="Internal")
    d_agout = nc.dram_tensor("agout", [4, 128 * DIN + 64], F32,
                             kind="Internal")
    d_dtsc = nc.dram_tensor("dtsc", [2, H, TOK], F32, kind="Internal")
    d_lnrow = nc.dram_tensor("lnrow", [2, TOK], F32, kind="Internal")
    d_murs = nc.dram_tensor("murs", [2, TOK], F32, kind="Internal")
    d_soft8 = nc.dram_tensor("soft8", [8, NG, TOK], BF16, kind="Internal")
    d_scrow = nc.dram_tensor("scrow", [NG, TOK], F32, kind="Internal")

    dbg_spec = dict(
        z=([DIN, TOK], F32), x=([NCH, DIN, CH], BF16), dt=([H, TOK], F32),
        acum=([H, TOK], F32), sc=([NCH, 128, DIN], F32),
        slocal=([128, DIN], F32), carry=([128, DIN], F32),
        yfm=([128, DIN], F32), ynorm=([128, 32 * 512], BF16),
        xres=([DM, TOK], F32), xp1=([128, 16 * 2048], BF16),
        scoretm=([128, 16 * 64], F32), soft8=([8, NG, TOK], BF16),
        scrow=([NG, TOK], F32), y2=([DM, TOK], F32),
        wt=([128, 16 * CH], BF16), xtm=([128, DIN], BF16),
        yfm5=([128, DIN], F32), woutbf=([DIN, DM], BF16),
        prev5=([128, DIN], BF16), a0=([128, 11 * 2048], BF16),
        gg0=([128, 2048], F32), wpt0=([8, 11 * 128], BF16),
        sgb0=([128, 2048], F32),
    )
    d_dbg = {}
    for name in dbg:
        shape, dt_ = dbg_spec[name]
        d_dbg[name] = nc.dram_tensor("dbg_" + name, shape, dt_,
                                     kind="ExternalOutput")

    ds = Sem(nc, "ds")
    ws = Sem(nc, "ws")
    ps = Sem(nc, "ps")
    es = Sem(nc, "es")
    cs = Sem(nc, "cs")
    zs = Sem(nc, "zs")
    ss = Sem(nc, "ss")
    cc = Sem(nc, "cc")
    ya = Sem(nc, "ya")
    su = Sem(nc, "su")
    ws1 = Sem(nc, "ws1")
    sz0 = Sem(nc, "sz0")
    sz1 = Sem(nc, "sz1")
    sx0 = Sem(nc, "sx0")
    sx1 = Sem(nc, "sx1")
    srx = Sem(nc, "srx")
    sh0 = Sem(nc, "sh0")
    sh1 = Sem(nc, "sh1")
    uq0 = Sem(nc, "uq0")
    uq1 = Sem(nc, "uq1")
    gsm = Sem(nc, "gsm")
    sxc = Sem(nc, "sxc")
    wsp = (ws, ws1)

    def bar():
        nc.all_engine_barrier()

    def dma(q, out, in_, sem=ds, **kw):
        sem.inc(q.dma_start(out=out, in_=in_, **kw), 16)

    def dwait(eng=None):
        (eng or nc.sync).wait_ge(ds.h, ds.n)

    def dump(name, src):
        if name in d_dbg:
            dma(nc.sync, d_dbg[name].ap(), src)
            dwait()
            bar()

    sb = nc.alloc_sbuf_tensor
    s_convw = sb("s_convw", [128, NM_XBC * 4], F32)
    s_convb = sb("s_convb", [128, NM_XBC], F32)
    s_dtbias = sb("s_dtbias", [H, 1], F32)
    s_acol = sb("s_acol", [H, 1], F32)
    s_drep = sb("s_drep", [128, 32], F32)
    s_nw = sb("s_nw", [128, 32], F32)
    s_ibf = sb("s_ibf", [128, 128], BF16)
    s_i64 = sb("s_i64", [64, 64], F32)
    s_i128 = sb("s_i128", [128, 128], F32)
    s_onesb = sb("s_onesb", [128, 1], BF16)
    s_ones1 = sb("s_ones1", [1, 128], F32)
    s_mq = sb("s_mq", [5, 4], F32)
    s_atote = sb("s_atote", [5, H], F32)
    s_mgT = sb("s_mgT", [128, 16 * NG * 8], BF16)
    s_gateT = sb("s_gateT", [128, 16 * E], BF16)
    s_PT = sb("s_PT", [E, E], F32)
    s_tmp64 = sb("s_tmp64", [H, CH], F32)
    s_tmp64b = sb("s_tmp64b", [H, CH], F32)

    q = nc.sync
    dma(q, s_convw[:].rearrange("p (m t) -> p m t", t=4),
        d_convw.ap().rearrange("(m p) t -> p m t", p=128))
    dma(q, s_convb[:], d_convb.ap().rearrange("(m p) -> p m", p=128),
        allow_slow_non_contiguous=True)
    dma(q, s_dtbias[:], d_dtbias.ap().rearrange("(h o) -> h o", o=1))
    dma(q, s_acol[:], d_alog.ap().rearrange("(h o) -> h o", o=1))
    dma(q, s_drep[:], d_drep.ap().rearrange("(m p) -> p m", p=128),
        allow_slow_non_contiguous=True)
    dma(q, s_nw[:], d_nw.ap().rearrange("(m p) -> p m", p=128),
        allow_slow_non_contiguous=True)
    dma(q, s_ibf[:], d_ibf.ap())
    dma(q, s_i64[:], d_i64.ap())
    dma(q, s_i128[:], d_i128.ap())
    dma(q, s_onesb[:], d_onesb.ap())
    dma(q, s_ones1[:], d_ones1.ap())
    dma(q, s_mq[:], d_mq.ap())
    dma(nc.gpsimd, s_mgT[:].rearrange("p (k e) -> p k e", e=NG * 8),
        d_mupT.ap().rearrange("(k p) e -> p k e", p=128), sem=ws)
    dma(nc.gpsimd, s_gateT[:].rearrange("p (k e) -> p k e", e=E),
        d_gatewT.ap().rearrange("(k p) e -> p k e", p=128), sem=ws)
    with MultiCtx(
        nc.sbuf_tensor("s_invi", [E, 1], I32),
        nc.sbuf_tensor("s_invf", [E, 1], F32),
        nc.sbuf_tensor("s_iotac", [E, 1], F32),
        nc.sbuf_tensor("s_row0", [1, E], F32),
        nc.psum_tensor("pp0", [64, 64], F32),
    ) as (s_invi, s_invf, s_iotac, s_row0, pp0):
        dma(q, s_invi[:], d_invmap.ap().rearrange("(e o) -> e o", o=1))
        dwait()
        bar()
        nc.vector.tensor_copy(s_invf[:], s_invi[:])
        nc.gpsimd.iota(s_iotac[:], [[1, 1]], base=0, channel_multiplier=1,
                       allow_small_or_imprecise_dtypes=True)
        bar()
        nc.tensor.matmul(pp0[0:1, 0:64], s_invf[:], s_i64[:], start=True,
                         stop=True)
        bar()
        nc.vector.tensor_copy(s_row0[:], pp0[0:1, 0:64])
        bar()
        nc.tensor.matmul(pp0[0:64, 0:64], s_ones1[0:1, 0:64], s_row0[:],
                         start=True, stop=True)
        bar()
        nc.vector.tensor_scalar(s_PT[:], pp0[0:64, 0:64], s_iotac[:], None,
                                OP.is_equal)
        bar()

    with MultiCtx(
        nc.sbuf_tensor("s_Bbf", [128, TOK], BF16),
        nc.sbuf_tensor("s_Cbf", [128, TOK], BF16),
        nc.sbuf_tensor("s_dcb", [128, NCH * H], F32),
        nc.sbuf_tensor("s_colTa", [128, NCH * H], F32),
        nc.sbuf_tensor("s_evTa", [128, NCH * H], F32),
    ) as (s_Bbf, s_Cbf, s_dcb, s_colTa, s_evTa):
        _sR = MultiCtx(nc.sbuf_tensor("s_R", [128, DIN], F32))
        (s_R,) = _sR.__enter__()
        _colt = MultiCtx(
            nc.sbuf_tensor("s_colt", [H, TOK], F32),
            nc.sbuf_tensor("s_acum", [H, TOK], F32),
            nc.sbuf_tensor("s_asum", [H, NCH], F32),
        )
        s_colt, s_acum, s_asum = _colt.__enter__()
        if STAGE >= 1:
            _sc = nc.named_scope("P1"); _sc.__enter__()
            with MultiCtx(
                nc.sbuf_tensor("suT", [128, 16 * TOKH], BF16),
                nc.sbuf_tensor("wb0", [128, 16 * 128], BF16),
                nc.sbuf_tensor("wb1", [128, 16 * 128], BF16),
                nc.sbuf_tensor("xbc0", [128, TOKH], F32),
                nc.sbuf_tensor("cv0", [128, TOK], F32),
                nc.sbuf_tensor("xr0", [128, TOK], BF16),
                nc.sbuf_tensor("xr1", [128, TOK], BF16),
                nc.sbuf_tensor("xd0", [128, TOK], BF16),
                nc.sbuf_tensor("xd1", [128, TOK], BF16),
                nc.sbuf_tensor("zb0", [128, TOK], F32),
                nc.sbuf_tensor("zb1", [128, TOK], F32),
                nc.sbuf_tensor("sdtev", [H, TOK], F32),
                nc.psum_tensor("pA", [128, 2048], F32),
                nc.psum_tensor("pH", [128, 8], F32),
            ) as (suT, wb0, wb1, xbc0, cv0, xr0, xr1, xd0, xd1, zb0, zb1,
                  sdtev, pA, pH):
                for k in range(16):
                    dma(nc.gpsimd, suT[:, k * TOKH:(k + 1) * TOKH],
                        d_uT.ap().rearrange("(k p) t -> k p t", p=128)[k],
                        sem=su)
                wview = d_winT.ap().rearrange("(k p) m -> p k m", p=128)

                def wload(mi, buf):
                    wdt = min(128, DIP - mi * 128)
                    sm = wsp[mi % 2]
                    dma(nc.gpsimd,
                        buf[:].rearrange("p (k m) -> p k m", m=128)
                        [:, :, 0:wdt],
                        wview[:, :, mi * 128:mi * 128 + wdt], sem=sm)
                    return (sm, sm.n)

                wmk = []
                wmk.append(wload(0, wb0))
                wbufs = (wb0, wb1)
                xrs, xds, zbs = (xr0, xr1), (xd0, xd1), (zb0, zb1)
                guard = {}

                for mi in range(min(NM, P1CAP)):
                    par = mi % 2
                    wb = wbufs[par]
                    rows = 64 if mi == M_DT else 128
                    if mi + 1 < NM:
                        nc.gpsimd.wait_ge(ps.h, mi)
                        wmk.append(wload(mi + 1, wbufs[(mi + 1) % 2]))
                    if mi == 0:
                        nc.tensor.wait_ge(su.h, su.n)
                    nc.tensor.wait_ge(wmk[mi][0].h, wmk[mi][1])
                    nc.tensor.wait_ge(es.h, mi)
                    is_xbc = NM_Z <= mi < NM_Z + NM_XBC
                    for t in range(4):
                        for k in range(16):
                            _mm = nc.tensor.matmul(
                                pA[0:rows, t * 512:(t + 1) * 512],
                                wb[:, k * 128:k * 128 + rows],
                                suT[:, k * TOKH + 3 + t * 512:
                                    k * TOKH + 3 + (t + 1) * 512],
                                start=(k == 0), stop=(k == 15))
                    if is_xbc:
                        for k in range(16):
                            _mm = nc.tensor.matmul(
                                pH[:, 0:3], wb[:, k * 128:(k + 1) * 128],
                                suT[:, k * TOKH:k * TOKH + 3],
                                start=(k == 0), stop=(k == 15))
                    ps.inc(_mm)

                    if mi < NM_Z:
                        zb = zbs[par]
                        szp = (sz0, sz1)[par]
                        nc.scalar.wait_ge(ps.h, mi + 1)
                        nc.scalar.wait_ge(szp.h, szp.n)
                        es.inc(nc.scalar.activation(zb[:], pA[:], AF.Silu))
                        nc.sync.wait_ge(es.h, es.n)
                        dma(nc.sync, d_zsc[mi * 128:(mi + 1) * 128, :], zb[:],
                            sem=szp)
                    elif is_xbc:
                        ci = mi - NM_Z
                        nc.vector.wait_ge(ps.h, mi + 1)
                        nc.vector.tensor_copy(xbc0[:, 0:3], pH[:, 0:3])
                        es.inc(nc.vector.tensor_copy(xbc0[:, 3:2051], pA[:]))
                        nc.vector.tensor_scalar(
                            cv0[:], xbc0[:, 0:2048],
                            s_convw[:, ci * 4:ci * 4 + 1],
                            s_convb[:, ci:ci + 1], OP.mult, OP.add)
                        for w in range(1, 4):
                            _cv = nc.vector.scalar_tensor_tensor(
                                cv0[:], xbc0[:, w:w + 2048],
                                s_convw[:, ci * 4 + w:ci * 4 + w + 1], cv0[:],
                                OP.mult, OP.add)
                        cs.inc(_cv)
                        nc.scalar.wait_ge(cs.h, cs.n)
                        if ci < 32:
                            xrb, xdb = xrs[par], xds[par]
                            sxq = (sx0, sx1)[par]
                            nc.scalar.wait_ge(sxq.h, sxq.n)
                            zs.inc(nc.scalar.activation(xrb[:], cv0[:],
                                                         AF.Silu))
                            nc.vector.wait_ge(zs.h, zs.n)
                            zs.inc(nc.vector.tensor_scalar_mul(
                                xdb[:], xrb[:], s_drep[:, ci:ci + 1]))
                            nc.sync.wait_ge(zs.h, zs.n)
                            nc.scalar.wait_ge(zs.h, zs.n)
                            dma(nc.sync,
                                d_xsc[:, ci * 128:(ci + 1) * 128, :]
                                .transpose([1, 0, 2]),
                                xrb[:].rearrange("p (c t) -> p c t", t=128),
                                sem=sxq)
                            dma(nc.scalar,
                                d_xdsc[:, ci * 128:(ci + 1) * 128, :]
                                .transpose([1, 0, 2]),
                                xdb[:].rearrange("p (c t) -> p c t", t=128),
                                sem=sxq)
                        elif ci == 32:
                            zs.inc(nc.scalar.activation(s_Bbf[:], cv0[:],
                                                        AF.Silu))
                        else:
                            zs.inc(nc.scalar.activation(s_Cbf[:], cv0[:],
                                                        AF.Silu))
                    else:
                        nc.vector.wait_ge(ps.h, mi + 1)
                        es.inc(nc.vector.tensor_copy(sdtev[:], pA[0:64, :]))
                for smx in (sz0, sz1, sx0, sx1, ss):
                    nc.sync.wait_ge(smx.h, smx.n)
                bar()
                if P1CAP >= NM:
                    dma(nc.sync, d_dtsc[0], sdtev[:])
                    dwait()
                bar()
            _sc.__exit__(None, None, None)

        if STAGE >= 2:
            _sc = nc.named_scope("P2"); _sc.__enter__()
            with MultiCtx(
                nc.sbuf_tensor("p2raw", [H, TOK], F32),
                nc.sbuf_tensor("p2dt", [H, TOK], F32),
            ) as (p2raw, p2dt):
                dma(nc.sync, p2raw[:], d_dtsc[0])
                dwait()
                bar()
                nc.scalar.activation(p2dt[:], p2raw[:], AF.Abs,
                                     bias=s_dtbias[:])
                bar()
                nc.scalar.activation(p2dt[:], p2dt[:], AF.Exp, scale=-1.0)
                bar()
                nc.scalar.activation(p2dt[:], p2dt[:], AF.Ln, bias=1.0)
                nc.vector.tensor_scalar(p2raw[:], p2raw[:], s_dtbias[:], 0.0,
                                        OP.add, OP.max)
                bar()
                nc.vector.tensor_add(p2dt[:], p2dt[:], p2raw[:])
                bar()
                nc.scalar.activation(s_colt[:], p2dt[:], AF.Ln)
                nc.scalar.activation(s_acol[:], s_acol[:], AF.Exp)
                bar()
                dma(nc.sync, d_dtsc[1], p2dt[:])
                nc.vector.tensor_scalar_mul(s_acol[:], s_acol[:], -1.0)
                bar()
                nc.vector.tensor_scalar_mul(p2raw[:], p2dt[:], s_acol[:])
                bar()
                for c in range(NCH):
                    nc.vector.tensor_tensor_scan(
                        s_acum[:, c * CH:(c + 1) * CH],
                        p2raw[:, c * CH:(c + 1) * CH],
                        p2raw[:, c * CH:(c + 1) * CH], 0.0, OP.add, OP.bypass)
                bar()
                nc.vector.tensor_sub(s_colt[:], s_colt[:], s_acum[:])
                nc.vector.tensor_copy(
                    s_asum[:],
                    s_acum[:].rearrange("h (c t) -> h c t", t=CH)
                    [:, :, CH - 1])
                bar()
                nc.scalar.activation(s_tmp64[:, 0:NCH], s_asum[:], AF.Exp)
                bar()
                with nc.psum_tensor("pdc", [64, 64], F32) as pdc:
                    nc.tensor.matmul(pdc[0:NCH, 0:64], s_tmp64[:, 0:NCH],
                                     s_i64[:], start=True, stop=True)
                    bar()
                    nc.vector.tensor_copy(s_tmp64b[0:NCH, 0:64],
                                          pdc[0:NCH, 0:64])
                    bar()
                dma(nc.sync, d_dcrow.ap(), s_tmp64b[0:NCH, 0:64])
                for c in range(NCH):
                    dma(nc.sync, d_acrow[c].rearrange("(h l) -> h l", h=H),
                        s_acum[:, c * CH:(c + 1) * CH])
                dwait()
                bar()
                for c in range(NCH):
                    dma(nc.sync, s_dcb[:, c * H:(c + 1) * H],
                        d_dcrow[c:c + 1, :].broadcast_to([128, H]))
                dwait()
                bar()
                with nc.psum_tensor("p2t", [128, 128], F32) as p2t:
                    for c in range(NCH):
                        nc.tensor.matmul(p2t[:, 0:64],
                                         s_colt[:, c * CH:(c + 1) * CH],
                                         s_i64[:], start=True, stop=True)
                        bar()
                        nc.scalar.copy(s_colTa[:, c * H:(c + 1) * H],
                                       p2t[:, 0:64])
                        bar()
                        nc.scalar.activation(s_tmp64[:],
                                             s_acum[:, c * CH:(c + 1) * CH],
                                             AF.Exp)
                        bar()
                        nc.tensor.matmul(p2t[:, 0:64], s_tmp64[:], s_i64[:],
                                         start=True, stop=True)
                        bar()
                        nc.scalar.copy(s_evTa[:, c * H:(c + 1) * H],
                                       p2t[:, 0:64])
                        bar()
                dump("dt", p2dt[:])
            dump("acum", s_acum[:])
            dump("z", d_zsc.ap())
            dump("x", d_xsc.ap())
            _sc.__exit__(None, None, None)

        if STAGE >= 3:
            _sc = nc.named_scope("P3"); _sc.__enter__()
            with MultiCtx(
                nc.sbuf_tensor("xtm0", [128, DIN], BF16),
                nc.sbuf_tensor("xtm1", [128, DIN], BF16),
                nc.sbuf_tensor("sxd", [128, DIN], BF16),
                nc.sbuf_tensor("sBt", [128, 128], BF16),
                nc.sbuf_tensor("sdtd", [128, 64], BF16),
                nc.sbuf_tensor("sSc", [128, DIN], F32),
                nc.sbuf_tensor("p3dt", [H, TOK], F32),
                nc.psum_tensor("p3a", [128, 2048], F32),
                nc.psum_tensor("p3s", [128, 128], F32),
                nc.psum_tensor("p3w", [128, 64], F32),
            ) as (xtm0, xtm1, sxd, sBt, sdtd, sSc, p3dt, p3a, p3s, p3w):
                dma(nc.sync, p3dt[:], d_dtsc[1])
                nc.vector.memset(s_R[:], 0.0)
                xtms = (xtm0, xtm1)
                dma(nc.sync, xtm0[:], d_xsc[0], transpose=True)
                dwait()
                bar()
                dump("xtm", xtm0[:])
                for c in range(NCH):
                    if c + 1 < NCH:
                        dma(nc.sync, xtms[(c + 1) % 2][:], d_xsc[c + 1],
                            transpose=True)
                    xtm = xtms[c % 2]
                    nc.tensor.matmul(p3s[:], s_Bbf[:, c * CH:(c + 1) * CH],
                                     s_ibf[:], start=True, stop=True)
                    nc.scalar.activation(
                        s_tmp64[:], s_acum[:, c * CH:(c + 1) * CH], AF.Exp,
                        scale=-1.0,
                        bias=s_acum[:, c * CH + CH - 1:c * CH + CH])
                    bar()
                    nc.vector.tensor_mul(s_tmp64[:], s_tmp64[:],
                                         p3dt[:, c * CH:(c + 1) * CH])
                    nc.scalar.copy(sBt[:], p3s[:])
                    bar()
                    nc.tensor.matmul(p3w[:], s_tmp64[:], s_i64[:], start=True,
                                     stop=True)
                    bar()
                    nc.vector.tensor_copy(sdtd[:], p3w[:])
                    bar()
                    nc.vector.tensor_tensor(
                        sxd[:].rearrange("s (h p) -> s h p", p=PH),
                        xtm[:].rearrange("s (h p) -> s h p", p=PH),
                        sdtd[:].rearrange("s (h o) -> s h o", o=1)
                        .broadcast_to([128, H, PH]), OP.mult)
                    bar()
                    for half in range(2):
                        for j in range(4):
                            nc.tensor.matmul(
                                p3a[:, j * 512:(j + 1) * 512], sBt[:],
                                sxd[:, half * 2048 + j * 512:
                                    half * 2048 + (j + 1) * 512],
                                start=True, stop=True)
                        bar()
                        nc.scalar.copy(sSc[:, half * 2048:(half + 1) * 2048],
                                       p3a[:])
                        bar()
                    dma(nc.sync, d_scsc[c], sSc[:], sem=ss)
                    nc.vector.tensor_tensor(
                        s_R[:].rearrange("n (h p) -> n h p", p=PH),
                        s_R[:].rearrange("n (h p) -> n h p", p=PH),
                        s_dcb[:, c * H:(c + 1) * H]
                        .rearrange("n (h o) -> n h o", o=1)
                        .broadcast_to([128, H, PH]), OP.mult)
                    bar()
                    nc.vector.tensor_add(s_R[:], s_R[:], sSc[:])
                    bar()
                nc.vector.reduce_sum(s_tmp64[:, 0:1], s_asum[:], AX.X)
                nc.sync.wait_ge(ss.h, ss.n)
                bar()
                dump("sc", d_scsc.ap())
                dump("slocal", s_R[:])
                dma(nc.sync,
                    d_agin.ap()[0:128 * DIN].rearrange("(p f) -> p f", p=128),
                    s_R[:])
                dma(nc.sync,
                    d_agin.ap()[128 * DIN:].rearrange("(h o) -> h o", o=1),
                    s_tmp64[:, 0:1])
                dwait()
                bar()
                _sc.__exit__(None, None, None)
                _sc = nc.named_scope("AG"); _sc.__enter__()
                cc.inc(nc.gpsimd.collective_compute(
                    "AllGather", OP.bypass,
                    replica_groups=[[0, 1, 2, 3], [4, 5, 6, 7]],
                    ins=[d_agin.ap()], outs=[d_agout.ap()]), 1)
                nc.gpsimd.wait_ge(cc.h, cc.n)
                bar()
                _sc.__exit__(None, None, None)

        _colt.__exit__(None, None, None)

        if STAGE >= 4:
            _sc = nc.named_scope("P4carry"); _sc.__enter__()
            with MultiCtx(
                nc.sbuf_tensor("sSg", [128, DIN], F32),
                nc.sbuf_tensor("sWb", [128, 4 * 64], F32),
                nc.psum_tensor("p4w", [64, 64], F32),
            ) as (sSg, sWb, p4w):
                nc.vector.memset(s_atote[:], 1.0)
                bar()
                for p in range(4):
                    dma(nc.sync, s_atote[p:p + 1, :],
                        d_agout[p, 128 * DIN:].rearrange("(o h) -> o h", o=1))
                dwait()
                bar()
                nc.tensor.matmul(p4w[0:64, 0:4], s_atote[:], s_mq.ap(),
                                 start=True, stop=True)
                bar()
                nc.scalar.activation(s_tmp64[:, 0:4], p4w[0:64, 0:4], AF.Exp)
                bar()
                nc.tensor.matmul(p4w[0:4, 0:64], s_tmp64[:, 0:4], s_i64[:],
                                 start=True, stop=True)
                bar()
                nc.vector.tensor_copy(s_tmp64b[0:4, 0:64], p4w[0:4, 0:64])
                bar()
                dma(nc.sync, d_wrow.ap(), s_tmp64b[0:4, 0:64])
                dwait()
                bar()
                for p in range(4):
                    dma(nc.sync, sWb[:, p * 64:(p + 1) * 64],
                        d_wrow[p:p + 1, :].broadcast_to([128, H]))
                dwait()
                bar()
                nc.vector.memset(s_R[:], 0.0)
                bar()
                for p in range(4):
                    dma(nc.sync, sSg[:],
                        d_agout[p, 0:128 * DIN]
                        .rearrange("(n f) -> n f", n=128))
                    dwait()
                    bar()
                    nc.vector.tensor_tensor(
                        sSg[:].rearrange("n (h p) -> n h p", p=PH),
                        sSg[:].rearrange("n (h p) -> n h p", p=PH),
                        sWb[:, p * 64:(p + 1) * 64]
                        .rearrange("n (h o) -> n h o", o=1)
                        .broadcast_to([128, H, PH]), OP.mult)
                    bar()
                    nc.vector.tensor_add(s_R[:], s_R[:], sSg[:])
                    bar()
                dump("carry", s_R[:])
            _sc.__exit__(None, None, None)
            _sc = nc.named_scope("P4main"); _sc.__enter__()
            with MultiCtx(
                nc.sbuf_tensor("s_prevbf", [128, DIN], BF16),
                nc.sbuf_tensor("s_yng", [128, 32 * 512], BF16),
                nc.sbuf_tensor("xtmA", [128, DIN], BF16),
                nc.sbuf_tensor("xtmB", [128, DIN], BF16),
                nc.sbuf_tensor("xdc", [128, DIN], BF16),
                nc.sbuf_tensor("srB0", [128, 16 * CH], F32),
                nc.sbuf_tensor("srB1", [128, 16 * CH], F32),
                nc.sbuf_tensor("sLW0", [128, 16 * CH], BF16),
                nc.sbuf_tensor("sLW1", [128, 16 * CH], BF16),
                nc.sbuf_tensor("sGt", [128, 128], F32),
                nc.sbuf_tensor("sYtm", [128, 2048], BF16),
                nc.sbuf_tensor("sYo", [128, 2048], BF16),
                nc.sbuf_tensor("syfm", [128, DIN], F32),
                nc.sbuf_tensor("shA", [128, 2048], F32),
                nc.sbuf_tensor("shB", [128, 2048], F32),
                nc.sbuf_tensor("srs", [1, 128], F32),
                nc.sbuf_tensor("swog0", [128, 32 * 128], BF16),
                nc.sbuf_tensor("swog1", [128, 32 * 128], BF16),
                nc.sbuf_tensor("sxrg", [128, 1024], F32),
                nc.sbuf_tensor("sxrb", [128, 1024], BF16),
                nc.sbuf_tensor("sxrb2", [128, 1024], BF16),
                nc.sbuf_tensor("srowt", [1, 1024], F32),
                nc.sbuf_tensor("s_mask", [128, 128], F32),
                nc.psum_tensor("pYd", [128, 1024], F32),
                nc.psum_tensor("pY2", [128, 1024], F32),
                nc.psum_tensor("pYo", [128, 1024], F32),
                nc.psum_tensor("p4s", [128, 128], F32),
            ) as (s_prevbf, s_yng, xtmA, xtmB, xdc, srB0, srB1, sLW0, sLW1,
                  sGt, sYtm, sYo, syfm, shA, shB, srs, swog0, swog1,
                  sxrg, sxrb, sxrb2, srowt, s_mask, pYd, pY2, pYo, p4s):
                nc.vector.memset(s_mask[:], 0.0)
                bar()
                nc.gpsimd.affine_select(s_mask[:], s_mask[:], [[1, 128]],
                                        OP.is_ge, NEG, base=0,
                                        channel_multiplier=-1)
                bar()
                wov = d_woutT.ap().rearrange("(k p) m -> p k m", p=128)
                smk = []
                for m in range(16):
                    buf = (swog0, swog1)[m % 2]
                    wsm = wsp[m % 2]
                    if m >= 2:
                        nc.gpsimd.wait_ge(ss.h, smk[m - 2])
                    dma(nc.gpsimd, buf[:].rearrange("p (k m) -> p k m", m=128),
                        wov[:, :, m * 128:(m + 1) * 128], sem=wsm)
                    nc.vector.wait_ge(wsm.h, wsm.n)
                    zs.inc(nc.vector.tensor_tensor(
                        buf[:].rearrange("p (k m) -> p k m", m=128),
                        buf[:].rearrange("p (k m) -> p k m", m=128),
                        s_nw[:].rearrange("p (k o) -> p k o", o=1)
                        .broadcast_to([128, 32, 128]), OP.mult))
                    nc.sync.wait_ge(zs.h, zs.n)
                    dma(nc.sync,
                        d_woutbf[:, m * 128:(m + 1) * 128]
                        .rearrange("(k p) m -> p k m", p=128),
                        buf[:].rearrange("p (k m) -> p k m", m=128), sem=ss)
                    smk.append(ss.n)
                    nc.vector.wait_ge(ss.h, ss.n)
                nc.sync.wait_ge(ss.h, ss.n)
                bar()
                dump("woutbf", d_woutbf.ap())

                m_wml, m_dad, m_aff, m_exp = {}, {}, {}, {}
                m_mm1, m_mm2, m_yev, m_yov = {}, {}, {}, {}
                m_tb, m_tba, m_gmm, m_sgt = {}, {}, {}, {}
                m_prev, m_radd, m_zml = {}, {}, {}
                m_sq, m_rms, m_ts, m_sqr = {}, {}, {}, {}
                m_rcp, m_bc, m_ynm = {}, {}, {}
                rb_ld, sc_ld, z_ld, xtm_ld, xdc_ld, u_ld = \
                    {}, {}, {}, {}, {}, {}
                m_main, m_sum, m_tta, m_xrb = {}, {}, {}, {}
                m_st, m_rowt, m_lnst = {}, {}, {}
                srBs = (srB0, srB1)
                sLWs = (sLW0, sLW1)
                szq = (sz0, sz1)
                shq = (sh0, sh1)
                shb = (shA, shB)
                uqq = (uq0, uq1)
                sxq = (sx0, sx1)
                xtms = (xtmA, xtmB)
                pYp = (pYd, pY2)

                def W(eng, sem, mark):
                    if mark is not None:
                        eng.wait_ge(sem.h, mark)

                def ld_srb(c, q):
                    W(nc.sync, cs, m_wml.get((c, q - 2)) if q >= 2
                      else m_wml.get((c - 1, q + 2)))
                    dma(nc.sync, srBs[q % 2][:],
                        d_acrow[c:c + 1, q * 16 * CH:(q + 1) * 16 * CH]
                        .broadcast_to([128, 16 * CH]), sem=szq[q % 2])
                    rb_ld[(c, q)] = szq[q % 2].n

                def ld_sc(c, h):
                    W(nc.sync, gsm, m_zml.get((c - 1, h)))
                    dma(nc.sync, shb[h][:],
                        d_scsc[c][:, h * 2048:(h + 1) * 2048], sem=shq[h])
                    sc_ld[(c, h)] = shq[h].n

                def ld_z(c, h):
                    W(nc.sync, cs, m_radd[(c, h)])
                    dma(nc.sync,
                        shb[h][:].rearrange("p (i t) -> p i t", t=128),
                        d_zsc[h * 2048:(h + 1) * 2048, c * CH:(c + 1) * CH]
                        .rearrange("(i p) t -> p i t", p=128), sem=shq[h])
                    z_ld[(c, h)] = shq[h].n

                def ld_xtm(c):
                    W(nc.scalar, ps, m_mm1.get((c - 2, 3)))
                    dma(nc.scalar, xtms[c % 2][:], d_xsc[c], transpose=True,
                        sem=sxq[c % 2])
                    xtm_ld[c] = sxq[c % 2].n

                def ld_xdc(c):
                    W(nc.scalar, cs, m_tba.get((c - 1, 1, 1)))
                    dma(nc.scalar,
                        xdc[:].rearrange("p (i t) -> p i t", t=128),
                        d_xdsc[c].rearrange("(i p) t -> p i t", p=128),
                        sem=sxc)
                    xdc_ld[c] = sxc.n

                def g_mm(c):
                    W(nc.tensor, es, m_sgt.get(c - 1))
                    ps.inc(nc.tensor.matmul(
                        p4s[:], s_Bbf[:, c * CH:(c + 1) * CH],
                        s_Cbf[:, c * CH:(c + 1) * CH], start=True, stop=True))
                    m_gmm[c] = ps.n
                    W(nc.scalar, ps, m_gmm[c])
                    W(nc.scalar, cs, m_wml.get((c - 1, 3)))
                    es.inc(nc.scalar.copy(sGt[:], p4s[:]))
                    m_sgt[c] = es.n

                def wchain(c, q):
                    srq, slw = srBs[q % 2], sLWs[q % 2]
                    W(nc.vector, szq[q % 2], rb_ld[(c, q)])
                    cs.inc(nc.vector.tensor_tensor(
                        srq[:].rearrange("s (h l) -> s h l", l=CH),
                        srq[:].rearrange("s (h l) -> s h l", l=CH),
                        s_colTa[:, c * H + q * 16:c * H + q * 16 + 16]
                        .rearrange("s (h o) -> s h o", o=1)
                        .broadcast_to([128, 16, CH]), OP.add))
                    m_dad[(c, q)] = cs.n
                    W(nc.gpsimd, cs, m_dad[(c, q)])
                    gsm.inc(nc.gpsimd.tensor_tensor(
                        srq[:].rearrange("s (h l) -> s h l", l=CH),
                        srq[:].rearrange("s (h l) -> s h l", l=CH),
                        s_mask[:].rearrange("s (o l) -> s o l", o=1)
                        .broadcast_to([128, 16, CH]), OP.add))
                    m_aff[(c, q)] = gsm.n
                    W(nc.scalar, gsm, m_aff[(c, q)])
                    es.inc(nc.scalar.activation(srq[:], srq[:], AF.Exp))
                    m_exp[(c, q)] = es.n
                    W(nc.vector, es, m_exp[(c, q)])
                    if q == 0:
                        W(nc.vector, es, m_sgt[c])
                    W(nc.vector, ps, m_mm1.get((c, q - 2)) if q >= 2
                      else m_mm1.get((c - 1, q + 2)))
                    cs.inc(nc.vector.tensor_tensor(
                        slw[:].rearrange("s (h l) -> s h l", l=CH),
                        srq[:].rearrange("s (h l) -> s h l", l=CH),
                        sGt[:].rearrange("s (o l) -> s o l", o=1)
                        .broadcast_to([128, 16, CH]), OP.mult))
                    m_wml[(c, q)] = cs.n
                    if "wt" in d_dbg and c == 0 and q == 0:
                        bar()
                        dump("wt", slw[:])
                    nq_c, nq_q = (c, q + 2) if q < 2 else (c + 1, q - 2)
                    if nq_c < NCH:
                        ld_srb(nq_c, nq_q)

                def mm_q(c, q):
                    grp, c4 = c // 4, c % 4
                    qq = q % 2
                    slw = sLWs[q % 2]
                    xtm = xtms[c % 2]
                    pYb = pYp[q % 2]
                    W(nc.tensor, cs, m_wml[(c, q)])
                    if q < 2:
                        W(nc.tensor, cs, m_tba.get((c - 1, 1, q)))
                    else:
                        W(nc.tensor, cs, m_tba.get((c, 0, q - 2)))
                    if q == 0:
                        W(nc.tensor, sxq[c % 2], xtm_ld[c])
                        if c4 == 0:
                            W(nc.tensor, cs, m_tta.get((grp - 1, 15)))
                        if q == 1 and c4 == 0:
                            pass
                    for hh in range(16):
                        h = q * 16 + hh
                        _mm = nc.tensor.matmul(
                            pYb[:, hh * 64:(hh + 1) * 64],
                            slw[:, hh * CH:(hh + 1) * CH],
                            xtm[:, h * PH:(h + 1) * PH],
                            start=True, stop=True)
                    ps.inc(_mm)
                    m_mm1[(c, q)] = ps.n
                    W(nc.tensor, cs, m_yov.get((c, q - 1)) if q > 0
                      else m_yov.get((c - 1, 3)))
                    if q == 0:
                        W(nc.tensor, es, m_prev[c])
                        W(nc.tensor, cs, m_ynm.get(c - 1))
                        if c4 == 0:
                            W(nc.tensor, cs, m_rowt.get(grp - 1))
                    for hh in range(16):
                        h = q * 16 + hh
                        _mm = nc.tensor.matmul(
                            pYo[:, hh * 64:(hh + 1) * 64],
                            s_Cbf[:, c * CH:(c + 1) * CH],
                            s_prevbf[:, h * PH:(h + 1) * PH],
                            start=True, stop=True)
                    ps.inc(_mm)
                    m_mm2[(c, q)] = ps.n
                    W(nc.scalar, ps, m_mm1[(c, q)])
                    W(nc.scalar, ps, m_tb[(c, 0, q - 2)] if q >= 2
                      else m_tb.get((c - 1, 1, q)))
                    es.inc(nc.scalar.copy(
                        sYtm[:, qq * 1024:(qq + 1) * 1024], pYb[:]))
                    m_yev[(c, q)] = es.n
                    W(nc.vector, ps, m_mm2[(c, q)])
                    W(nc.vector, ps, m_tb[(c, 0, q - 2)] if q >= 2
                      else m_tb.get((c - 1, 1, q)))
                    cs.inc(nc.vector.tensor_tensor(
                        sYo[:, qq * 1024:(qq + 1) * 1024]
                        .rearrange("s (hh p) -> s hh p", p=PH),
                        pYo[:].rearrange("s (hh p) -> s hh p", p=PH),
                        s_evTa[:, c * H + q * 16:c * H + q * 16 + 16]
                        .rearrange("s (h o) -> s h o", o=1)
                        .broadcast_to([128, 16, PH]), OP.mult))
                    m_yov[(c, q)] = cs.n

                def tb_hf(c, hf):
                    for b in range(2):
                        q = hf * 2 + b
                        W(nc.tensor, es, m_yev[(c, q)])
                        W(nc.tensor, cs, m_yov[(c, q)])
                        for i8 in range(8):
                            i = b * 8 + i8
                            nc.tensor.matmul(
                                pYp[b][:, i8 * 128:(i8 + 1) * 128],
                                sYtm[:, i * 128:(i + 1) * 128],
                                s_ibf[:], start=True, stop=False)
                            _mm = nc.tensor.matmul(
                                pYp[b][:, i8 * 128:(i8 + 1) * 128],
                                sYo[:, i * 128:(i + 1) * 128],
                                s_ibf[:], start=False, stop=True)
                        ps.inc(_mm)
                        m_tb[(c, hf, b)] = ps.n
                        W(nc.vector, ps, m_tb[(c, hf, b)])
                        if b == 0 and hf == 0:
                            W(nc.vector, sxc, xdc_ld[c])
                        cs.inc(nc.vector.tensor_add(
                            syfm[:, (hf * 16 + b * 8) * 128:
                                 (hf * 16 + b * 8 + 8) * 128],
                            pYp[b][:],
                            xdc[:, (hf * 16 + b * 8) * 128:
                                (hf * 16 + b * 8 + 8) * 128]))
                        m_tba[(c, hf, b)] = cs.n

                def zmul(c, h):
                    W(nc.gpsimd, shq[h], z_ld[(c, h)])
                    W(nc.gpsimd, cs, m_tba[(c, h, 1)])
                    gsm.inc(nc.gpsimd.tensor_tensor(
                        syfm[:, h * 2048:(h + 1) * 2048],
                        syfm[:, h * 2048:(h + 1) * 2048],
                        shb[h][:], OP.mult))
                    m_zml[(c, h)] = gsm.n

                def prev_copy(c):
                    if c > 0:
                        W(nc.scalar, ps, m_mm2[(c - 1, 3)])
                        W(nc.scalar, cs, m_radd[(c - 1, 1)])
                    es.inc(nc.scalar.copy(s_prevbf[:], s_R[:]))
                    m_prev[c] = es.n

                def r_update(c):
                    W(nc.vector, es, m_prev[c])
                    cs.inc(nc.vector.tensor_tensor(
                        s_R[:].rearrange("n (h p) -> n h p", p=PH),
                        s_R[:].rearrange("n (h p) -> n h p", p=PH),
                        s_dcb[:, c * H:(c + 1) * H]
                        .rearrange("n (h o) -> n h o", o=1)
                        .broadcast_to([128, H, PH]), OP.mult))
                    for h in range(2):
                        W(nc.vector, shq[h], sc_ld[(c, h)])
                        cs.inc(nc.vector.tensor_add(
                            s_R[:, h * 2048:(h + 1) * 2048],
                            s_R[:, h * 2048:(h + 1) * 2048], shb[h][:]))
                        m_radd[(c, h)] = cs.n
                        ld_z(c, h)

                def out_proj(g, c_tail):
                    wobv = d_woutbf.ap().rearrange("(k p) m -> p k m", p=128)
                    if "ynorm" in d_dbg and g == 0:
                        bar()
                        dump("ynorm", s_yng[:])
                    for mm in range(2):
                        W(nc.scalar, es, m_xrb.get((g - 1, 14 + mm)))
                        W(nc.scalar, srx, m_st.get((g - 1, 14 + mm)))
                        dma(nc.scalar, sxrg[:, mm * 512:(mm + 1) * 512],
                            d_uT[mm * 128:(mm + 1) * 128,
                                 3 + g * 512:3 + (g + 1) * 512],
                            sem=uqq[mm])
                        u_ld[(g, mm)] = uqq[mm].n
                    W(nc.gpsimd, ps, m_rms[c_tail])
                    dma(nc.gpsimd,
                        swog0[:].rearrange("p (k m) -> p k m", m=128),
                        wobv[:, :, 0:128], sem=wsp[0])
                    wmk = {0: wsp[0].n}
                    for m in range(16):
                        hb = m % 2
                        buf = (swog0, swog1)[hb]
                        if m + 1 < 16:
                            W(nc.gpsimd, ps, m_main.get((g, m - 1)))
                            dma(nc.gpsimd,
                                (swog0, swog1)[(m + 1) % 2][:]
                                .rearrange("p (k m) -> p k m", m=128),
                                wobv[:, :, (m + 1) * 128:(m + 2) * 128],
                                sem=wsp[(m + 1) % 2])
                            wmk[m + 1] = wsp[(m + 1) % 2].n
                        W(nc.tensor, wsp[hb], wmk[m])
                        W(nc.tensor, cs, m_tta.get((g, m - 2)))
                        if m <= 1:
                            W(nc.tensor, cs, m_ynm[c_tail])
                            W(nc.tensor, cs, m_tba[(c_tail, 1, 0)])
                        for k in range(32):
                            _mm = nc.tensor.matmul(
                                pYd[:, hb * 512:(hb + 1) * 512],
                                buf[:, k * 128:(k + 1) * 128],
                                s_yng[:, k * 512:(k + 1) * 512],
                                start=(k == 0), stop=(k == 31))
                        ps.inc(_mm)
                        m_main[(g, m)] = ps.n
                        if m >= 1:
                            pb = (m - 1) % 2
                            W(nc.tensor, es, m_xrb[(g, m - 1)])
                            nc.tensor.matmul(
                                pYo[0:1, 0:512], s_onesb[:],
                                sxrb[:, pb * 512:(pb + 1) * 512],
                                start=(m - 1 == 0), stop=False)
                            ps.inc(nc.tensor.matmul(
                                pYo[0:1, 512:1024], s_onesb[:],
                                sxrb2[:, pb * 512:(pb + 1) * 512],
                                start=(m - 1 == 0), stop=False))
                            m_sum[(g, m - 1)] = ps.n
                        W(nc.vector, uqq[hb], u_ld[(g, m)])
                        W(nc.vector, ps, m_main[(g, m)])
                        cs.inc(nc.vector.tensor_add(
                            sxrg[:, hb * 512:(hb + 1) * 512],
                            sxrg[:, hb * 512:(hb + 1) * 512],
                            pYd[:, hb * 512:(hb + 1) * 512]))
                        m_tta[(g, m)] = cs.n
                        W(nc.sync, cs, m_tta[(g, m)])
                        dma(nc.sync,
                            d_xres[m * 128:(m + 1) * 128,
                                   g * 512:(g + 1) * 512],
                            sxrg[:, hb * 512:(hb + 1) * 512], sem=srx)
                        m_st[(g, m)] = srx.n
                        W(nc.scalar, cs, m_tta[(g, m)])
                        nc.scalar.activation(
                            sxrb[:, hb * 512:(hb + 1) * 512],
                            sxrg[:, hb * 512:(hb + 1) * 512], AF.Copy)
                        es.inc(nc.scalar.activation(
                            sxrb2[:, hb * 512:(hb + 1) * 512],
                            sxrg[:, hb * 512:(hb + 1) * 512], AF.Square))
                        m_xrb[(g, m)] = es.n
                        if m + 2 < 16:
                            W(nc.scalar, es, m_xrb[(g, m)])
                            W(nc.scalar, srx, m_st[(g, m)])
                            dma(nc.scalar,
                                sxrg[:, hb * 512:(hb + 1) * 512],
                                d_uT[(m + 2) * 128:(m + 3) * 128,
                                     3 + g * 512:3 + (g + 1) * 512],
                                sem=uqq[hb])
                            u_ld[(g, m + 2)] = uqq[hb].n
                    W(nc.tensor, es, m_xrb[(g, 15)])
                    nc.tensor.matmul(pYo[0:1, 0:512], s_onesb[:],
                                     sxrb[:, 512:1024], start=False,
                                     stop=True)
                    ps.inc(nc.tensor.matmul(pYo[0:1, 512:1024], s_onesb[:],
                                            sxrb2[:, 512:1024], start=False,
                                            stop=True))
                    m_sum[(g, 15)] = ps.n
                    W(nc.vector, ps, m_sum[(g, 15)])
                    W(nc.vector, ss, m_lnst.get(g - 1))
                    nc.vector.tensor_copy(srowt[0:1, 0:512],
                                          pYo[0:1, 0:512])
                    cs.inc(nc.vector.tensor_copy(srowt[0:1, 512:1024],
                                                 pYo[0:1, 512:1024]))
                    m_rowt[g] = cs.n
                    W(nc.sync, cs, m_rowt[g])
                    dma(nc.sync, d_lnrow[0:1, g * 512:(g + 1) * 512],
                        srowt[0:1, 0:512], sem=ss)
                    dma(nc.sync, d_lnrow[1:2, g * 512:(g + 1) * 512],
                        srowt[0:1, 512:1024], sem=ss)
                    m_lnst[g] = ss.n

                ld_xtm(0)
                ld_xtm(1)
                ld_xdc(0)
                ld_sc(0, 0)
                ld_sc(0, 1)
                ld_srb(0, 0)
                ld_srb(0, 1)
                prev_copy(0)
                g_mm(0)
                wchain(0, 0)
                wchain(0, 1)
                r_update(0)

                for c in range(NCH):
                    grp, c4 = c // 4, c % 4
                    dbgy = ("yfm" in d_dbg and c == 0) or \
                        ("yfm5" in d_dbg and c == 5)
                    mm_q(c, 0)
                    wchain(c, 2)
                    mm_q(c, 1)
                    tb_hf(c, 0)
                    wchain(c, 3)
                    mm_q(c, 2)
                    if not dbgy:
                        zmul(c, 0)
                    if c + 1 < NCH:
                        ld_sc(c + 1, 0)
                    mm_q(c, 3)
                    if c + 1 < NCH:
                        prev_copy(c + 1)
                    tb_hf(c, 1)
                    if dbgy:
                        bar()
                        dump("yfm" if c == 0 else "yfm5", syfm[:])
                        zmul(c, 0)
                    if "prev5" in d_dbg and c == 4:
                        bar()
                        dump("prev5", s_prevbf[:])
                    zmul(c, 1)
                    if c + 1 < NCH:
                        ld_sc(c + 1, 1)
                    W(nc.scalar, gsm, m_zml[(c, 1)])
                    W(nc.scalar, gsm, m_zml[(c, 0)])
                    if c4 == 0:
                        W(nc.scalar, ps, m_main.get((grp - 1, 15)))
                    else:
                        W(nc.scalar, ps, m_rms.get(c - 1))
                    es.inc(nc.scalar.activation(swog0[:], syfm[:],
                                                AF.Square))
                    m_sq[c] = es.n
                    W(nc.tensor, es, m_sq[c])
                    W(nc.tensor, cs, m_yov[(c, 3)])
                    if c4 == 0:
                        W(nc.tensor, cs, m_rowt.get(grp - 1))
                    for i in range(32):
                        _mm = nc.tensor.matmul(
                            pYo[0:1, 128:256], s_onesb[:],
                            swog0[:, i * 128:(i + 1) * 128],
                            start=(i == 0), stop=(i == 31))
                    ps.inc(_mm)
                    m_rms[c] = ps.n
                    W(nc.vector, ps, m_rms[c])
                    W(nc.vector, ps, m_bc.get(c - 1))
                    cs.inc(nc.vector.tensor_scalar(
                        srs[:], pYo[0:1, 128:256], 1.0 / DIN, EPS,
                        OP.mult, OP.add))
                    m_ts[c] = cs.n
                    W(nc.scalar, cs, m_ts[c])
                    es.inc(nc.scalar.activation(srs[:], srs[:], AF.Sqrt))
                    m_sqr[c] = es.n
                    W(nc.vector, es, m_sqr[c])
                    cs.inc(nc.vector.reciprocal(srs[:], srs[:]))
                    m_rcp[c] = cs.n
                    W(nc.tensor, cs, m_rcp[c])
                    ps.inc(nc.tensor.matmul(pYo[:, 0:128], s_ones1[:],
                                            srs[:], start=True, stop=True))
                    m_bc[c] = ps.n
                    W(nc.vector, ps, m_bc[c])
                    if c4 == 0:
                        W(nc.vector, ps, m_main.get((grp - 1, 15)))
                    cs.inc(nc.vector.tensor_tensor(
                        s_yng[:].rearrange("p (i t) -> p i t", t=512)
                        [:, :, c4 * 128:(c4 + 1) * 128],
                        syfm[:].rearrange("p (i t) -> p i t", t=128),
                        pYo[:, 0:128].rearrange("p (o t) -> p o t", o=1)
                        .broadcast_to([128, 32, 128]), OP.mult))
                    m_ynm[c] = cs.n
                    if c + 1 < NCH:
                        g_mm(c + 1)
                        wchain(c + 1, 0)
                        wchain(c + 1, 1)
                        if c + 2 < NCH:
                            ld_xtm(c + 2)
                        ld_xdc(c + 1)
                        r_update(c + 1)
                    if c4 == 3:
                        out_proj(grp, c)
                nc.sync.wait_ge(srx.h, srx.n)
                nc.sync.wait_ge(ss.h, ss.n)
                bar()
                dump("xres", d_xres.ap())
            _sc.__exit__(None, None, None)

        _sR.__exit__(None, None, None)

    if STAGE >= 6:
        _sc = nc.named_scope("P6"); _sc.__enter__()
        _p67 = MultiCtx(
            nc.sbuf_tensor("s_xp1", [128, 16 * 2048], BF16),
            nc.sbuf_tensor("s_mu1", [1, TOK], F32),
            nc.sbuf_tensor("s_rs1", [1, TOK], F32),
        )
        s_xp1, s_mu1, s_rs1 = _p67.__enter__()
        with MultiCtx(
            nc.sbuf_tensor("s_mu1b", [128, 2048], F32),
            nc.sbuf_tensor("s_rs1b", [128, 2048], F32),
            nc.sbuf_tensor("sxr", [128, 2048], F32),
            nc.sbuf_tensor("srowa", [1, 2048], F32),
            nc.sbuf_tensor("sg1", [64, 2048], F32),
            nc.sbuf_tensor("sltm", [128, 16 * 64], F32),
            nc.sbuf_tensor("gt0", [128, 64], F32),
            nc.sbuf_tensor("gt1", [128, 64], F32),
            nc.sbuf_tensor("gt2", [128, 8], F32),
            nc.sbuf_tensor("gt3", [128, 8], F32),
            nc.sbuf_tensor("gt4", [128, 8], F32),
            nc.sbuf_tensor("gcol", [128, 1], F32),
            nc.sbuf_tensor("gstg", [64, 128], BF16),
            nc.sbuf_tensor("gstgf", [64, 128], F32),
            nc.psum_tensor("p6", [128, 2048], F32),
            nc.psum_tensor("p6b", [64, 512], F32),
        ) as (s_mu1b, s_rs1b, sxr, srowa, sg1, sltm, gt0, gt1, gt2, gt3,
              gt4, gcol, gstg, gstgf, p6, p6b):
            dma(nc.sync, s_mu1[:], d_lnrow[0:1, :])
            dma(nc.sync, s_rs1[:], d_lnrow[1:2, :])
            dwait()
            bar()
            nc.vector.tensor_scalar_mul(s_mu1[:], s_mu1[:], 1.0 / DM)
            bar()
            nc.vector.tensor_scalar_mul(s_rs1[:], s_rs1[:], 1.0 / DM)
            bar()
            nc.vector.tensor_tensor(srowa[:], s_mu1[:], s_mu1[:], OP.mult)
            bar()
            nc.vector.tensor_sub(s_rs1[:], s_rs1[:], srowa[:])
            bar()
            nc.vector.tensor_scalar(s_rs1[:], s_rs1[:], EPS, None, OP.add)
            bar()
            nc.scalar.activation(s_rs1[:], s_rs1[:], AF.Sqrt)
            bar()
            nc.vector.reciprocal(s_rs1[:], s_rs1[:])
            bar()
            dma(nc.sync, d_murs[0:1, :], s_mu1[:])
            dma(nc.sync, d_murs[1:2, :], s_rs1[:])
            dwait()
            bar()
            for t in range(4):
                nc.tensor.matmul(p6[:, t * 512:(t + 1) * 512], s_ones1[:],
                                 s_mu1[0:1, t * 512:(t + 1) * 512],
                                 start=True, stop=True)
            bar()
            nc.scalar.copy(s_mu1b[:], p6[:])
            bar()
            for t in range(4):
                nc.tensor.matmul(p6[:, t * 512:(t + 1) * 512], s_ones1[:],
                                 s_rs1[0:1, t * 512:(t + 1) * 512],
                                 start=True, stop=True)
            bar()
            nc.scalar.copy(s_rs1b[:], p6[:])
            bar()
            for m in range(16):
                dma(nc.sync, sxr[:], d_xres[m * 128:(m + 1) * 128, :])
                dwait()
                bar()
                nc.vector.tensor_sub(sxr[:], sxr[:], s_mu1b[:])
                bar()
                nc.vector.tensor_tensor(s_xp1[:, m * 2048:(m + 1) * 2048],
                                        sxr[:], s_rs1b[:], OP.mult)
                bar()
            dump("xp1", s_xp1[:])
            for t in range(4):
                for k in range(16):
                    nc.tensor.matmul(
                        p6[0:64, t * 512:(t + 1) * 512],
                        s_gateT[:, k * 64:(k + 1) * 64],
                        s_xp1[:, k * 2048 + t * 512:k * 2048 + (t + 1) * 512],
                        start=(k == 0), stop=(k == 15))
            bar()
            nc.vector.tensor_copy(sg1[:], p6[0:64, :])
            bar()
            for t in range(4):
                nc.tensor.matmul(p6[0:64, t * 512:(t + 1) * 512], s_PT[:],
                                 sg1[:, t * 512:(t + 1) * 512],
                                 start=True, stop=True)
            bar()
            nc.vector.tensor_copy(sg1[:], p6[0:64, :])
            bar()
            for i in range(16):
                nc.tensor.matmul(p6[:, i * 64:(i + 1) * 64],
                                 sg1[:, i * 128:(i + 1) * 128], s_i64[:],
                                 start=True, stop=True)
            bar()
            nc.scalar.copy(sltm[:], p6[:, 0:1024])
            bar()
            dump("scoretm", sltm[:])
            for i in range(16):
                lg = sltm[:, i * 64:(i + 1) * 64]
                nc.vector.reduce_max(gcol[:], lg, AX.X, negate=True)
                bar()
                nc.scalar.activation(gt0[:], lg, AF.Exp, bias=gcol[:])
                bar()
                nc.vector.reduce_sum(gcol[:], gt0[:], AX.X)
                bar()
                nc.vector.reciprocal(gcol[:], gcol[:])
                bar()
                nc.vector.tensor_scalar_mul(gt0[:], gt0[:], gcol[:])
                bar()
                nc.vector.max(out=gt2[:], in_=gt0[:])
                bar()
                nc.vector.tensor_scalar(gt1[:], gt0[:], gt2[:, 7:8], None,
                                        OP.is_ge)
                bar()
                nc.vector.tensor_mul(gt1[:], gt1[:], gt0[:])
                bar()
                nc.vector.tensor_reduce(
                    gt3[:], gt1[:].rearrange("p (g i) -> p g i", i=8), AX.X,
                    OP.add)
                bar()
                nc.vector.tensor_scalar(gt1[:], gt1[:], 0.0, None, OP.is_gt)
                bar()
                nc.vector.scalar_tensor_tensor(gt0[:], gt0[:], 100.0, gt1[:],
                                               OP.add, OP.mult)
                bar()
                nc.vector.tensor_scalar(gt0[:], gt0[:], 100.0, None,
                                        OP.subtract)
                bar()
                nc.vector.tensor_reduce(
                    gt4[:], gt0[:].rearrange("p (g i) -> p g i", i=8), AX.X,
                    OP.max, negate=True)
                bar()
                nc.vector.tensor_tensor(
                    gt0[:].rearrange("p (g i) -> p g i", i=8),
                    gt0[:].rearrange("p (g i) -> p g i", i=8),
                    gt4[:].rearrange("p (g o) -> p g o", o=1)
                    .broadcast_to([128, 8, 8]), OP.add)
                bar()
                nc.scalar.activation(gt0[:], gt0[:], AF.Exp)
                bar()
                nc.vector.tensor_reduce(
                    gt4[:], gt0[:].rearrange("p (g i) -> p g i", i=8), AX.X,
                    OP.add)
                bar()
                nc.vector.reciprocal(gt4[:], gt4[:])
                bar()
                nc.vector.tensor_tensor(
                    gt0[:].rearrange("p (g i) -> p g i", i=8),
                    gt0[:].rearrange("p (g i) -> p g i", i=8),
                    gt4[:].rearrange("p (g o) -> p g o", o=1)
                    .broadcast_to([128, 8, 8]), OP.mult)
                bar()
                nc.tensor.matmul(p6b[0:64, 0:128], gt0[:], s_i128[:],
                                 start=True, stop=True)
                bar()
                nc.scalar.copy(gstg[:], p6b[0:64, 0:128])
                bar()
                for g in range(NG):
                    dma(nc.sync, d_soft8[:, g, i * 128:(i + 1) * 128],
                        gstg[g * 8:(g + 1) * 8, :])
                nc.tensor.matmul(p6b[0:8, 0:128], gt3[:], s_i128[:],
                                 start=True, stop=True)
                bar()
                nc.vector.tensor_copy(gstgf[0:8, :], p6b[0:8, 0:128])
                bar()
                for g in range(NG):
                    dma(nc.sync, d_scrow[g:g + 1, i * 128:(i + 1) * 128],
                        gstgf[g:g + 1, :])
                dwait()
                bar()
            dump("soft8", d_soft8.ap())
            dump("scrow", d_scrow.ap())
        _sc.__exit__(None, None, None)

        if STAGE >= 7:
            _sc = nc.named_scope("P7"); _sc.__enter__()
            with MultiCtx(
                nc.sbuf_tensor("fw0", [128, 16 * 128], BF16),
                nc.sbuf_tensor("fw1", [128, 16 * 128], BF16),
                nc.sbuf_tensor("sgg", [128, 2048], F32),
                nc.sbuf_tensor("sab", [128, 11 * 2048], BF16),
                nc.sbuf_tensor("sSgb", [128, 2048], F32),
                nc.sbuf_tensor("swpt", [8, 128], BF16),
                nc.sbuf_tensor("syd0", [128, 2048], F32),
                nc.sbuf_tensor("syd1", [128, 2048], F32),
                nc.sbuf_tensor("s7s", [8, 2048], BF16),
                nc.sbuf_tensor("s7r", [1, 2048], F32),
                nc.psum_tensor("f7a", [128, 1024], F32),
                nc.psum_tensor("f7b", [128, 1024], F32),
                nc.psum_tensor("f7t", [8, 128], F32),
            ) as (fw0, fw1, sgg, sab, sSgb, swpt, syd0, syd1, s7s, s7r,
                  f7a, f7b, f7t):
                fws = (fw0, fw1)
                f7ms = (f7a, f7b)
                ydg = {}
                fwmk = {0: None, 1: None}
                for g in range(NG):
                    dma(nc.sync, s7s[:], d_soft8[:, g, :])
                    dma(nc.sync, s7r[:], d_scrow[g:g + 1, :])
                    dwait()
                    bar()
                    for t in range(4):
                        nc.tensor.matmul(
                            f7ms[t // 2][:, (t % 2) * 512:(t % 2 + 1) * 512],
                            s_ones1[:], s7r[0:1, t * 512:(t + 1) * 512],
                            start=True, stop=True)
                    bar()
                    nc.scalar.copy(sSgb[:, 0:1024], f7a[:])
                    nc.scalar.copy(sSgb[:, 1024:2048], f7b[:])
                    bar()
                    emk = {0: es.n, 1: es.n}
                    cmk = {0: cs.n, 1: cs.n}
                    zmk = zs.n
                    gvw = d_gpT[g].rearrange("(k p) i -> p k i", p=128)
                    uvw = d_upT[g].rearrange("(k p) i -> p k i", p=128)
                    wvs = (gvw, uvw)
                    wmarks = []
                    if fwmk[0] is not None:
                        nc.gpsimd.wait_ge(ps.h, fwmk[0])
                    dma(nc.gpsimd, fw0[:].rearrange("p (k m) -> p k m", m=128),
                        gvw[:, :, 0:128], sem=wsp[0])
                    wmarks.append((wsp[0], wsp[0].n))
                    for step in range(22):
                        m, proj = step // 2, step % 2
                        buf = fws[step % 2]
                        if step + 1 < 22:
                            m2, p2 = (step + 1) // 2, (step + 1) % 2
                            wsm = wsp[(step + 1) % 2]
                            if fwmk[(step + 1) % 2] is not None:
                                nc.gpsimd.wait_ge(ps.h, fwmk[(step + 1) % 2])
                            dma(nc.gpsimd,
                                fws[(step + 1) % 2][:]
                                .rearrange("p (k m) -> p k m", m=128),
                                wvs[p2][:, :, m2 * 128:(m2 + 1) * 128],
                                sem=wsm)
                            wmarks.append((wsm, wsm.n))
                        nc.tensor.wait_ge(wmarks[step][0].h, wmarks[step][1])
                        nc.tensor.wait_ge(zs.h, zmk)
                        for k in range(16):
                            _mm = nc.tensor.matmul(
                                f7t[:], s_mgT[:, k * 64 + g * 8:
                                              k * 64 + g * 8 + 8],
                                buf[:, k * 128:(k + 1) * 128],
                                start=(k == 0), stop=(k == 15))
                        ps.inc(_mm)
                        nc.vector.wait_ge(ps.h, ps.n)
                        zs.inc(nc.vector.tensor_copy(swpt[:], f7t[:]))
                        zmk = zs.n
                        for half in range(2):
                            pm = f7ms[half]
                            nc.tensor.wait_ge(es.h, emk[half])
                            for k in range(16):
                                for t2 in range(2):
                                    nc.tensor.matmul(
                                        pm[:, t2 * 512:(t2 + 1) * 512],
                                        buf[:, k * 128:(k + 1) * 128],
                                        s_xp1[:, k * 2048 + (half * 2 + t2)
                                              * 512:k * 2048 +
                                              (half * 2 + t2 + 1) * 512],
                                        start=(k == 0), stop=False)
                            if half == 0:
                                nc.tensor.wait_ge(zs.h, zs.n)
                            for t2 in range(2):
                                _mm = nc.tensor.matmul(
                                    pm[:, t2 * 512:(t2 + 1) * 512],
                                    swpt[:],
                                    s7s[:, (half * 2 + t2) * 512:
                                        (half * 2 + t2 + 1) * 512],
                                    start=False, stop=True)
                            ps.inc(_mm)
                            if proj == 0:
                                nc.scalar.wait_ge(ps.h, ps.n)
                                nc.scalar.wait_ge(cs.h, cmk[half])
                                es.inc(nc.scalar.activation(
                                    sgg[:, half * 1024:(half + 1) * 1024],
                                    pm[:], AF.Silu))
                                emk[half] = es.n
                            else:
                                nc.vector.wait_ge(ps.h, ps.n)
                                es.inc(nc.vector.tensor_mul(
                                    sgg[:, half * 1024:(half + 1) * 1024],
                                    sgg[:, half * 1024:(half + 1) * 1024],
                                    pm[:]))
                                emk[half] = es.n
                                nc.gpsimd.wait_ge(es.h, es.n)
                                cs.inc(nc.gpsimd.tensor_tensor(
                                    sab[:, m * 2048 + half * 1024:
                                        m * 2048 + (half + 1) * 1024],
                                    sgg[:, half * 1024:(half + 1) * 1024],
                                    sSgb[:, half * 1024:(half + 1) * 1024],
                                    OP.mult))
                                cmk[half] = cs.n
                        fwmk[step % 2] = ps.n
                    nc.tensor.wait_ge(cs.h, cs.n)
                    wv = d_dnT[g].rearrange("(k p) m -> p k m", p=128)
                    wmarks = []
                    if fwmk[0] is not None:
                        nc.gpsimd.wait_ge(ps.h, fwmk[0])
                    dma(nc.gpsimd,
                        fw0[:, 0:11 * 128].rearrange("p (k m) -> p k m", m=128),
                        wv[:, :, 0:128], sem=wsp[0])
                    wmarks.append((wsp[0], wsp[0].n))
                    for m in range(16):
                        buf = fws[m % 2]
                        syd = (syd0, syd1)[m % 2]
                        if m + 1 < 16:
                            wsm = wsp[(m + 1) % 2]
                            if fwmk[(m + 1) % 2] is not None:
                                nc.gpsimd.wait_ge(ps.h, fwmk[(m + 1) % 2])
                            dma(nc.gpsimd,
                                fws[(m + 1) % 2][:, 0:11 * 128]
                                .rearrange("p (k m) -> p k m", m=128),
                                wv[:, :, (m + 1) * 128:(m + 2) * 128],
                                sem=wsm)
                            wmarks.append((wsm, wsm.n))
                        nc.tensor.wait_ge(wmarks[m][0].h, wmarks[m][1])
                        for half in range(2):
                            pm = f7ms[half]
                            nc.tensor.wait_ge(es.h, emk[half])
                            for k in range(11):
                                for t2 in range(2):
                                    _mm = nc.tensor.matmul(
                                        pm[:, t2 * 512:(t2 + 1) * 512],
                                        buf[:, k * 128:(k + 1) * 128],
                                        sab[:, k * 2048 + (half * 2 + t2)
                                            * 512:k * 2048 +
                                            (half * 2 + t2 + 1) * 512],
                                        start=(k == 0), stop=(k == 10))
                            ps.inc(_mm)
                            nc.scalar.wait_ge(ps.h, ps.n)
                            if m % 2 in ydg:
                                nc.scalar.wait_ge(ya.h, ydg[m % 2])
                            es.inc(nc.scalar.copy(
                                syd[:, half * 1024:(half + 1) * 1024],
                                pm[:]))
                            emk[half] = es.n
                        fwmk[m % 2] = ps.n
                        nc.gpsimd.wait_ge(es.h, es.n)
                        dma(nc.gpsimd, d_y2[m * 128:(m + 1) * 128, :],
                            syd[:], sem=ya,
                            accum_op=(OP.bypass if g == 0 else OP.add))
                        ydg[m % 2] = ya.n
                nc.sync.wait_ge(ya.h, ya.n)
                bar()
                dump("y2", d_y2.ap())
            _sc.__exit__(None, None, None)
        _p67.__exit__(None, None, None)

    if STAGE >= 8:
        _sc = nc.named_scope("P8"); _sc.__enter__()
        with MultiCtx(
            nc.sbuf_tensor("sxr8", [128, 2048], F32),
            nc.sbuf_tensor("sy28", [128, 2048], F32),
            nc.sbuf_tensor("sxb8", [128, 2048], BF16),
            nc.sbuf_tensor("sxb8b", [128, 2048], BF16),
            nc.sbuf_tensor("s_m1b", [128, 2048], F32),
            nc.sbuf_tensor("s_r1b", [128, 2048], F32),
            nc.sbuf_tensor("s_m2b", [128, 2048], F32),
            nc.sbuf_tensor("s_r2b", [128, 2048], F32),
            nc.sbuf_tensor("srow8", [1, 2 * 2048], F32),
        ) as (sxr8, sy28, sxb8, sxb8b, s_m1b, s_r1b, s_m2b, s_r2b, srow8):
            dma(nc.sync, srow8[0:1, 0:2048], d_murs[0:1, :])
            dma(nc.sync, srow8[0:1, 2048:4096], d_murs[1:2, :])
            dwait()
            bar()
            with nc.psum_tensor("p8c", [128, 2048], F32) as p8c:
                for t in range(4):
                    nc.tensor.matmul(p8c[:, t * 512:(t + 1) * 512],
                                     s_ones1[:],
                                     srow8[0:1, t * 512:(t + 1) * 512],
                                     start=True, stop=True)
                bar()
                nc.scalar.copy(s_m1b[:], p8c[:])
                bar()
                for t in range(4):
                    nc.tensor.matmul(p8c[:, t * 512:(t + 1) * 512],
                                     s_ones1[:],
                                     srow8[0:1, 2048 + t * 512:
                                           2048 + (t + 1) * 512],
                                     start=True, stop=True)
                bar()
                nc.scalar.copy(s_r1b[:], p8c[:])
                bar()

            def t2_chunk(m):
                dma(nc.sync, sxr8[:], d_xres[m * 128:(m + 1) * 128, :])
                dma(nc.scalar, sy28[:], d_y2[m * 128:(m + 1) * 128, :])
                dwait()
                dwait(nc.scalar)
                bar()
                nc.vector.tensor_sub(sxr8[:], sxr8[:], s_m1b[:])
                bar()
                nc.vector.tensor_mul(sxr8[:], sxr8[:], s_r1b[:])
                bar()
                nc.vector.tensor_add(sxr8[:], sxr8[:], sy28[:])
                bar()

            with MultiCtx(
                nc.psum_tensor("p8r1", [1, 2048], F32),
                nc.psum_tensor("p8r2", [1, 2048], F32),
            ) as (p8r1, p8r2):
                for m in range(16):
                    t2_chunk(m)
                    nc.scalar.activation(sxb8[:], sxr8[:], AF.Copy)
                    nc.scalar.activation(sxb8b[:], sxr8[:], AF.Square)
                    bar()
                    for t in range(4):
                        nc.tensor.matmul(
                            p8r1[0:1, t * 512:(t + 1) * 512], s_onesb[:],
                            sxb8[:, t * 512:(t + 1) * 512],
                            start=(m == 0), stop=(m == 15))
                        nc.tensor.matmul(
                            p8r2[0:1, t * 512:(t + 1) * 512], s_onesb[:],
                            sxb8b[:, t * 512:(t + 1) * 512],
                            start=(m == 0), stop=(m == 15))
                    bar()
                nc.vector.tensor_copy(srow8[0:1, 0:2048], p8r1[0:1, :])
                bar()
                nc.vector.tensor_copy(srow8[0:1, 2048:4096], p8r2[0:1, :])
                bar()
            nc.vector.tensor_scalar_mul(srow8[0:1, 0:2048],
                                        srow8[0:1, 0:2048], 1.0 / DM)
            bar()
            nc.vector.tensor_scalar_mul(srow8[0:1, 2048:4096],
                                        srow8[0:1, 2048:4096], 1.0 / DM)
            bar()
            nc.vector.tensor_tensor(sy28[0:1, 0:2048], srow8[0:1, 0:2048],
                                    srow8[0:1, 0:2048], OP.mult)
            bar()
            nc.vector.tensor_sub(srow8[0:1, 2048:4096],
                                 srow8[0:1, 2048:4096], sy28[0:1, 0:2048])
            bar()
            nc.vector.tensor_scalar(srow8[0:1, 2048:4096],
                                    srow8[0:1, 2048:4096], EPS, None, OP.add)
            bar()
            nc.scalar.activation(srow8[0:1, 2048:4096],
                                 srow8[0:1, 2048:4096], AF.Sqrt)
            bar()
            nc.vector.reciprocal(srow8[0:1, 2048:4096],
                                 srow8[0:1, 2048:4096])
            bar()
            with nc.psum_tensor("p8b", [128, 2048], F32) as p8b:
                for t in range(4):
                    nc.tensor.matmul(p8b[:, t * 512:(t + 1) * 512],
                                     s_ones1[:],
                                     srow8[0:1, t * 512:(t + 1) * 512],
                                     start=True, stop=True)
                bar()
                nc.scalar.copy(s_m2b[:], p8b[:])
                bar()
                for t in range(4):
                    nc.tensor.matmul(p8b[:, t * 512:(t + 1) * 512],
                                     s_ones1[:],
                                     srow8[0:1, 2048 + t * 512:
                                           2048 + (t + 1) * 512],
                                     start=True, stop=True)
                bar()
                nc.scalar.copy(s_r2b[:], p8b[:])
                bar()
            for m in range(16):
                t2_chunk(m)
                nc.vector.tensor_sub(sxr8[:], sxr8[:], s_m2b[:])
                bar()
                nc.vector.tensor_mul(sxr8[:], sxr8[:], s_r2b[:])
                bar()
                dma(nc.sync, d_out[m * 128:(m + 1) * 128, :], sxr8[:],
                    sem=ss)
                nc.sync.wait_ge(ss.h, ss.n)
                bar()

        _sc.__exit__(None, None, None)
    if STAGE < 8:
        with nc.sbuf_tensor("sfin", [128, TOK], F32) as sfin:
            nc.vector.memset(sfin[:], 0.0)
            bar()
            for m in range(16):
                dma(nc.sync, d_out[m * 128:(m + 1) * 128, :], sfin[:],
                    sem=ss)
            nc.sync.wait_ge(ss.h, ss.n)
            bar()
    nc.sync.wait_ge(ds.h, ds.n)
    nc.sync.wait_ge(ss.h, ss.n)
    bar()
    return nc



def _prep(inputs):
    ident = np.eye(128, dtype=ml_dtypes.bfloat16)
    i64 = np.eye(64, dtype=np.float32)
    i128 = np.eye(128, dtype=np.float32)
    onesb = np.ones((128, 1), dtype=ml_dtypes.bfloat16)
    ones1 = np.ones((1, 128), dtype=np.float32)
    W_inT = np.ascontiguousarray(np.asarray(inputs["L0_attn_W_in"]).T)
    woutT = np.ascontiguousarray(np.asarray(inputs["L0_attn_W_out"]).T)
    gatewT = np.ascontiguousarray(np.asarray(inputs["L0_ffn_gate_w"]).T)
    mupT = np.ascontiguousarray(
        np.asarray(inputs["L0_ffn_mask_up_w"]).transpose(1, 0, 2)
        .reshape(DM, NG * 8))
    gpT = np.ascontiguousarray(np.asarray(inputs["L0_ffn_gate_proj_w"])
                               .transpose(0, 2, 1))
    upT = np.ascontiguousarray(np.asarray(inputs["L0_ffn_up_proj_w"])
                               .transpose(0, 2, 1))
    dnT = np.ascontiguousarray(np.asarray(inputs["L0_ffn_down_proj_w"])
                               .transpose(0, 2, 1))
    drep = np.ascontiguousarray(np.repeat(np.asarray(inputs["L0_attn_D"]),
                                          PH))
    invmap = np.ascontiguousarray(
        np.asarray(inputs["L0_ffn_inv_map"]).reshape(-1).astype(np.int32))
    u = np.asarray(inputs["L0_attn_u"])
    shared = dict(
        winT=W_inT, convw=np.ascontiguousarray(inputs["L0_attn_conv_w"]),
        convb=np.ascontiguousarray(inputs["L0_attn_conv_b"]),
        dtbias=np.ascontiguousarray(inputs["L0_attn_dt_bias"]),
        alog=np.ascontiguousarray(inputs["L0_attn_A_log"]),
        drep=drep, nw=np.ascontiguousarray(inputs["L0_attn_norm_w"]),
        woutT=woutT, gatewT=gatewT, invmap=invmap, mupT=mupT,
        gpT=gpT, upT=upT, dnT=dnT, ibf=ident, i64=i64, i128=i128,
        onesb=onesb, ones1=ones1)
    in_maps = []
    for core in range(8):
        b, qq = core // 4, core % 4
        t0 = qq * TOK
        uslice = np.zeros((TOKH, DM), dtype=np.float32)
        lo = t0 - 3
        src = u[b, max(lo, 0):t0 + TOK]
        uslice[TOKH - src.shape[0]:] = src
        uT = np.ascontiguousarray(uslice.T)
        mq = np.zeros((5, 4), dtype=np.float32)
        mq[4, :] = NEG
        for pq in range(4):
            if pq < qq:
                mq[4, pq] = 0.0
                for rq in range(4):
                    if pq < rq < qq:
                        mq[rq, pq] = 1.0
        m = dict(shared)
        m["uT"] = uT
        m["mq"] = mq
        in_maps.append(m)
    return in_maps


def kernel(**inputs):
    in_maps = _prep(inputs)
    nc = build_nc()
    res = run_bass_kernel_spmd(nc, in_maps, core_ids=list(range(8)))
    out = np.zeros((2, 8192, DM), dtype=np.float32)
    for core in range(8):
        b, qq = core // 4, core % 4
        out[b, qq * TOK:(qq + 1) * TOK] = res.results[core]["out"].T
    return out


# version 04
# speedup vs baseline: 1.3417x; 1.0143x over previous
import numpy as np
import ml_dtypes
import concourse.bass as bass
import concourse.mybir as mybir
from concourse.bass_utils import run_bass_kernel_spmd

F32 = mybir.dt.float32
BF16 = mybir.dt.bfloat16
I32 = mybir.dt.int32
OP = mybir.AluOpType
AF = mybir.ActivationFunctionType
AX = mybir.AxisListType

DM = 2048
DIN = 4096
DST = 128
H = 64
PH = 64
CONVD = 4352
DIP = 8512
TOK = 2048
TOKH = 2051
CH = 128
NCH = 16
IF = 1408
NG = 8
E = 64
NEG = -1.0e30
EPS = 1e-5
NM_Z = 32
NM_XBC = 34
M_DT = 66
NM = 67

STAGE = 99
P1CAP = 999


class MultiCtx:

    def __init__(self, *cms):
        self.cms = cms

    def __enter__(self):
        from contextlib import ExitStack
        self.st = ExitStack()
        return tuple(self.st.enter_context(c) for c in self.cms)

    def __exit__(self, *a):
        return self.st.__exit__(*a)


class Sem:
    def __init__(self, nc, name):
        self.h = nc.alloc_semaphore(name)
        self.n = 0

    def inc(self, instr, k=1):
        instr.then_inc(self.h, k)
        self.n += k
        return instr


def build_nc(dbg=()):
    nc = bass.Bass("TRN2", target_bir_lowering=False,
                   detect_race_conditions=False)

    d_uT = nc.dram_tensor("uT", [DM, TOKH], F32, kind="ExternalInput")
    d_winT = nc.dram_tensor("winT", [DM, DIP], F32, kind="ExternalInput")
    d_convw = nc.dram_tensor("convw", [CONVD, 4], F32, kind="ExternalInput")
    d_convb = nc.dram_tensor("convb", [CONVD], F32, kind="ExternalInput")
    d_dtbias = nc.dram_tensor("dtbias", [H], F32, kind="ExternalInput")
    d_alog = nc.dram_tensor("alog", [H], F32, kind="ExternalInput")
    d_drep = nc.dram_tensor("drep", [DIN], F32, kind="ExternalInput")
    d_nw = nc.dram_tensor("nw", [DIN], F32, kind="ExternalInput")
    d_woutT = nc.dram_tensor("woutT", [DIN, DM], F32, kind="ExternalInput")
    d_gatewT = nc.dram_tensor("gatewT", [DM, E], F32, kind="ExternalInput")
    d_invmap = nc.dram_tensor("invmap", [E], I32, kind="ExternalInput")
    d_mupT = nc.dram_tensor("mupT", [DM, NG * 8], F32, kind="ExternalInput")
    d_gpT = nc.dram_tensor("gpT", [NG, DM, IF], F32, kind="ExternalInput")
    d_upT = nc.dram_tensor("upT", [NG, DM, IF], F32, kind="ExternalInput")
    d_dnT = nc.dram_tensor("dnT", [NG, IF, DM], F32, kind="ExternalInput")
    d_mq = nc.dram_tensor("mq", [5, 4], F32, kind="ExternalInput")
    d_ibf = nc.dram_tensor("ibf", [128, 128], BF16, kind="ExternalInput")
    d_i64 = nc.dram_tensor("i64", [64, 64], F32, kind="ExternalInput")
    d_i128 = nc.dram_tensor("i128", [128, 128], F32, kind="ExternalInput")
    d_onesb = nc.dram_tensor("onesb", [128, 1], BF16, kind="ExternalInput")
    d_ones1 = nc.dram_tensor("ones1", [1, 128], F32, kind="ExternalInput")

    d_out = nc.dram_tensor("out", [DM, TOK], F32, kind="ExternalOutput")

    d_zsc = nc.dram_tensor("zsc", [DIN, TOK], F32, kind="Internal")
    d_xsc = nc.dram_tensor("xsc", [NCH, DIN, CH], BF16, kind="Internal")
    d_xdsc = nc.dram_tensor("xdsc", [NCH, DIN, CH], BF16, kind="Internal")
    d_scsc = nc.dram_tensor("scsc", [NCH, 128, DIN], F32, kind="Internal")
    d_acrow = nc.dram_tensor("acrow", [NCH, H * CH], F32, kind="Internal")
    d_dcrow = nc.dram_tensor("dcrow", [NCH, H], F32, kind="Internal")
    d_wrow = nc.dram_tensor("wrow", [4, H], F32, kind="Internal")
    d_xres = nc.dram_tensor("xres", [DM, TOK], F32, kind="Internal")
    d_y2 = nc.dram_tensor("y2", [DM, TOK], F32, kind="Internal")
    d_woutbf = nc.dram_tensor("woutbf", [DIN, DM], BF16, kind="Internal")
    d_agin = nc.dram_tensor("agin", [128 * DIN + 64], F32, kind="Internal")
    d_agout = nc.dram_tensor("agout", [4, 128 * DIN + 64], F32,
                             kind="Internal")
    d_dtsc = nc.dram_tensor("dtsc", [2, H, TOK], F32, kind="Internal")
    d_lnrow = nc.dram_tensor("lnrow", [2, TOK], F32, kind="Internal")
    d_murs = nc.dram_tensor("murs", [2, TOK], F32, kind="Internal")
    d_soft8 = nc.dram_tensor("soft8", [8, NG, TOK], BF16, kind="Internal")
    d_scrow = nc.dram_tensor("scrow", [NG, TOK], F32, kind="Internal")

    dbg_spec = dict(
        z=([DIN, TOK], F32), x=([NCH, DIN, CH], BF16), dt=([H, TOK], F32),
        acum=([H, TOK], F32), sc=([NCH, 128, DIN], F32),
        slocal=([128, DIN], F32), carry=([128, DIN], F32),
        yfm=([128, DIN], F32), ynorm=([128, 32 * 512], BF16),
        xres=([DM, TOK], F32), xp1=([128, 16 * 2048], BF16),
        scoretm=([128, 16 * 64], F32), soft8=([8, NG, TOK], BF16),
        scrow=([NG, TOK], F32), y2=([DM, TOK], F32),
        wt=([128, 16 * CH], BF16), xtm=([128, DIN], BF16),
        r8=([1, 2 * 2048], F32),
        yfm5=([128, DIN], F32), woutbf=([DIN, DM], BF16),
        prev5=([128, DIN], BF16), a0=([128, 11 * 2048], BF16),
        gg0=([128, 2048], F32), wpt0=([8, 11 * 128], BF16),
        sgb0=([128, 2048], F32),
    )
    d_dbg = {}
    for name in dbg:
        shape, dt_ = dbg_spec[name]
        d_dbg[name] = nc.dram_tensor("dbg_" + name, shape, dt_,
                                     kind="ExternalOutput")

    ds = Sem(nc, "ds")
    ws = Sem(nc, "ws")
    ps = Sem(nc, "ps")
    es = Sem(nc, "es")
    cs = Sem(nc, "cs")
    zs = Sem(nc, "zs")
    ss = Sem(nc, "ss")
    cc = Sem(nc, "cc")
    ya = Sem(nc, "ya")
    su = Sem(nc, "su")
    ws1 = Sem(nc, "ws1")
    sz0 = Sem(nc, "sz0")
    sz1 = Sem(nc, "sz1")
    sx0 = Sem(nc, "sx0")
    sx1 = Sem(nc, "sx1")
    srx = Sem(nc, "srx")
    sh0 = Sem(nc, "sh0")
    sh1 = Sem(nc, "sh1")
    uq0 = Sem(nc, "uq0")
    uq1 = Sem(nc, "uq1")
    gsm = Sem(nc, "gsm")
    sxc = Sem(nc, "sxc")
    wsp = (ws, ws1)

    def bar():
        nc.all_engine_barrier()

    def dma(q, out, in_, sem=ds, **kw):
        sem.inc(q.dma_start(out=out, in_=in_, **kw), 16)

    def dwait(eng=None):
        (eng or nc.sync).wait_ge(ds.h, ds.n)

    def dump(name, src):
        if name in d_dbg:
            dma(nc.sync, d_dbg[name].ap(), src)
            dwait()
            bar()

    sb = nc.alloc_sbuf_tensor
    s_convw = sb("s_convw", [128, NM_XBC * 4], F32)
    s_convb = sb("s_convb", [128, NM_XBC], F32)
    s_dtbias = sb("s_dtbias", [H, 1], F32)
    s_acol = sb("s_acol", [H, 1], F32)
    s_drep = sb("s_drep", [128, 32], F32)
    s_nw = sb("s_nw", [128, 32], F32)
    s_ibf = sb("s_ibf", [128, 128], BF16)
    s_i64 = sb("s_i64", [64, 64], F32)
    s_i128 = sb("s_i128", [128, 128], F32)
    s_onesb = sb("s_onesb", [128, 1], BF16)
    s_ones1 = sb("s_ones1", [1, 128], F32)
    s_mq = sb("s_mq", [5, 4], F32)
    s_atote = sb("s_atote", [5, H], F32)
    s_mgT = sb("s_mgT", [128, 16 * NG * 8], BF16)
    s_gateT = sb("s_gateT", [128, 16 * E], BF16)
    s_PT = sb("s_PT", [E, E], F32)
    s_tmp64 = sb("s_tmp64", [H, CH], F32)
    s_tmp64b = sb("s_tmp64b", [H, CH], F32)

    q = nc.sync
    dma(q, s_convw[:].rearrange("p (m t) -> p m t", t=4),
        d_convw.ap().rearrange("(m p) t -> p m t", p=128))
    dma(q, s_convb[:], d_convb.ap().rearrange("(m p) -> p m", p=128),
        allow_slow_non_contiguous=True)
    dma(q, s_dtbias[:], d_dtbias.ap().rearrange("(h o) -> h o", o=1))
    dma(q, s_acol[:], d_alog.ap().rearrange("(h o) -> h o", o=1))
    dma(q, s_drep[:], d_drep.ap().rearrange("(m p) -> p m", p=128),
        allow_slow_non_contiguous=True)
    dma(q, s_nw[:], d_nw.ap().rearrange("(m p) -> p m", p=128),
        allow_slow_non_contiguous=True)
    dma(q, s_ibf[:], d_ibf.ap())
    dma(q, s_i64[:], d_i64.ap())
    dma(q, s_i128[:], d_i128.ap())
    dma(q, s_onesb[:], d_onesb.ap())
    dma(q, s_ones1[:], d_ones1.ap())
    dma(q, s_mq[:], d_mq.ap())
    dma(nc.gpsimd, s_mgT[:].rearrange("p (k e) -> p k e", e=NG * 8),
        d_mupT.ap().rearrange("(k p) e -> p k e", p=128), sem=ws)
    dma(nc.gpsimd, s_gateT[:].rearrange("p (k e) -> p k e", e=E),
        d_gatewT.ap().rearrange("(k p) e -> p k e", p=128), sem=ws)
    with MultiCtx(
        nc.sbuf_tensor("s_invi", [E, 1], I32),
        nc.sbuf_tensor("s_invf", [E, 1], F32),
        nc.sbuf_tensor("s_iotac", [E, 1], F32),
        nc.sbuf_tensor("s_row0", [1, E], F32),
        nc.psum_tensor("pp0", [64, 64], F32),
    ) as (s_invi, s_invf, s_iotac, s_row0, pp0):
        dma(q, s_invi[:], d_invmap.ap().rearrange("(e o) -> e o", o=1))
        dwait()
        bar()
        nc.vector.tensor_copy(s_invf[:], s_invi[:])
        nc.gpsimd.iota(s_iotac[:], [[1, 1]], base=0, channel_multiplier=1,
                       allow_small_or_imprecise_dtypes=True)
        bar()
        nc.tensor.matmul(pp0[0:1, 0:64], s_invf[:], s_i64[:], start=True,
                         stop=True)
        bar()
        nc.vector.tensor_copy(s_row0[:], pp0[0:1, 0:64])
        bar()
        nc.tensor.matmul(pp0[0:64, 0:64], s_ones1[0:1, 0:64], s_row0[:],
                         start=True, stop=True)
        bar()
        nc.vector.tensor_scalar(s_PT[:], pp0[0:64, 0:64], s_iotac[:], None,
                                OP.is_equal)
        bar()

    with MultiCtx(
        nc.sbuf_tensor("s_Bbf", [128, TOK], BF16),
        nc.sbuf_tensor("s_Cbf", [128, TOK], BF16),
        nc.sbuf_tensor("s_dcb", [128, NCH * H], F32),
        nc.sbuf_tensor("s_colTa", [128, NCH * H], F32),
        nc.sbuf_tensor("s_evTa", [128, NCH * H], F32),
    ) as (s_Bbf, s_Cbf, s_dcb, s_colTa, s_evTa):
        _sR = MultiCtx(nc.sbuf_tensor("s_R", [128, DIN], F32))
        (s_R,) = _sR.__enter__()
        _colt = MultiCtx(
            nc.sbuf_tensor("s_colt", [H, TOK], F32),
            nc.sbuf_tensor("s_acum", [H, TOK], F32),
            nc.sbuf_tensor("s_asum", [H, NCH], F32),
        )
        s_colt, s_acum, s_asum = _colt.__enter__()
        if STAGE >= 1:
            _sc = nc.named_scope("P1"); _sc.__enter__()
            with MultiCtx(
                nc.sbuf_tensor("suT", [128, 16 * TOKH], BF16),
                nc.sbuf_tensor("wb0", [128, 16 * 128], BF16),
                nc.sbuf_tensor("wb1", [128, 16 * 128], BF16),
                nc.sbuf_tensor("xbc0", [128, TOKH], F32),
                nc.sbuf_tensor("cv0", [128, TOK], F32),
                nc.sbuf_tensor("xr0", [128, TOK], BF16),
                nc.sbuf_tensor("xr1", [128, TOK], BF16),
                nc.sbuf_tensor("xd0", [128, TOK], BF16),
                nc.sbuf_tensor("xd1", [128, TOK], BF16),
                nc.sbuf_tensor("zb0", [128, TOK], F32),
                nc.sbuf_tensor("zb1", [128, TOK], F32),
                nc.sbuf_tensor("sdtev", [H, TOK], F32),
                nc.psum_tensor("pA", [128, 2048], F32),
                nc.psum_tensor("pH", [128, 8], F32),
            ) as (suT, wb0, wb1, xbc0, cv0, xr0, xr1, xd0, xd1, zb0, zb1,
                  sdtev, pA, pH):
                for k in range(16):
                    dma(nc.gpsimd, suT[:, k * TOKH:(k + 1) * TOKH],
                        d_uT.ap().rearrange("(k p) t -> k p t", p=128)[k],
                        sem=su)
                wview = d_winT.ap().rearrange("(k p) m -> p k m", p=128)

                def wload(mi, buf):
                    wdt = min(128, DIP - mi * 128)
                    sm = wsp[mi % 2]
                    dma(nc.gpsimd,
                        buf[:].rearrange("p (k m) -> p k m", m=128)
                        [:, :, 0:wdt],
                        wview[:, :, mi * 128:mi * 128 + wdt], sem=sm)
                    return (sm, sm.n)

                wmk = []
                wmk.append(wload(0, wb0))
                wbufs = (wb0, wb1)
                xrs, xds, zbs = (xr0, xr1), (xd0, xd1), (zb0, zb1)
                guard = {}

                for mi in range(min(NM, P1CAP)):
                    par = mi % 2
                    wb = wbufs[par]
                    rows = 64 if mi == M_DT else 128
                    if mi + 1 < NM:
                        nc.gpsimd.wait_ge(ps.h, mi)
                        wmk.append(wload(mi + 1, wbufs[(mi + 1) % 2]))
                    if mi == 0:
                        nc.tensor.wait_ge(su.h, su.n)
                    nc.tensor.wait_ge(wmk[mi][0].h, wmk[mi][1])
                    nc.tensor.wait_ge(es.h, mi)
                    is_xbc = NM_Z <= mi < NM_Z + NM_XBC
                    for t in range(4):
                        for k in range(16):
                            _mm = nc.tensor.matmul(
                                pA[0:rows, t * 512:(t + 1) * 512],
                                wb[:, k * 128:k * 128 + rows],
                                suT[:, k * TOKH + 3 + t * 512:
                                    k * TOKH + 3 + (t + 1) * 512],
                                start=(k == 0), stop=(k == 15))
                    if is_xbc:
                        for k in range(16):
                            _mm = nc.tensor.matmul(
                                pH[:, 0:3], wb[:, k * 128:(k + 1) * 128],
                                suT[:, k * TOKH:k * TOKH + 3],
                                start=(k == 0), stop=(k == 15))
                    ps.inc(_mm)

                    if mi < NM_Z:
                        zb = zbs[par]
                        szp = (sz0, sz1)[par]
                        nc.scalar.wait_ge(ps.h, mi + 1)
                        nc.scalar.wait_ge(szp.h, szp.n)
                        es.inc(nc.scalar.activation(zb[:], pA[:], AF.Silu))
                        nc.sync.wait_ge(es.h, es.n)
                        dma(nc.sync, d_zsc[mi * 128:(mi + 1) * 128, :], zb[:],
                            sem=szp)
                    elif is_xbc:
                        ci = mi - NM_Z
                        nc.vector.wait_ge(ps.h, mi + 1)
                        nc.vector.tensor_copy(xbc0[:, 0:3], pH[:, 0:3])
                        es.inc(nc.vector.tensor_copy(xbc0[:, 3:2051], pA[:]))
                        nc.vector.tensor_scalar(
                            cv0[:], xbc0[:, 0:2048],
                            s_convw[:, ci * 4:ci * 4 + 1],
                            s_convb[:, ci:ci + 1], OP.mult, OP.add)
                        for w in range(1, 4):
                            _cv = nc.vector.scalar_tensor_tensor(
                                cv0[:], xbc0[:, w:w + 2048],
                                s_convw[:, ci * 4 + w:ci * 4 + w + 1], cv0[:],
                                OP.mult, OP.add)
                        cs.inc(_cv)
                        nc.scalar.wait_ge(cs.h, cs.n)
                        if ci < 32:
                            xrb, xdb = xrs[par], xds[par]
                            sxq = (sx0, sx1)[par]
                            nc.scalar.wait_ge(sxq.h, sxq.n)
                            zs.inc(nc.scalar.activation(xrb[:], cv0[:],
                                                         AF.Silu))
                            nc.vector.wait_ge(zs.h, zs.n)
                            zs.inc(nc.vector.tensor_scalar_mul(
                                xdb[:], xrb[:], s_drep[:, ci:ci + 1]))
                            nc.sync.wait_ge(zs.h, zs.n)
                            nc.scalar.wait_ge(zs.h, zs.n)
                            dma(nc.sync,
                                d_xsc[:, ci * 128:(ci + 1) * 128, :]
                                .transpose([1, 0, 2]),
                                xrb[:].rearrange("p (c t) -> p c t", t=128),
                                sem=sxq)
                            dma(nc.scalar,
                                d_xdsc[:, ci * 128:(ci + 1) * 128, :]
                                .transpose([1, 0, 2]),
                                xdb[:].rearrange("p (c t) -> p c t", t=128),
                                sem=sxq)
                        elif ci == 32:
                            zs.inc(nc.scalar.activation(s_Bbf[:], cv0[:],
                                                        AF.Silu))
                        else:
                            zs.inc(nc.scalar.activation(s_Cbf[:], cv0[:],
                                                        AF.Silu))
                    else:
                        nc.vector.wait_ge(ps.h, mi + 1)
                        es.inc(nc.vector.tensor_copy(sdtev[:], pA[0:64, :]))
                for smx in (sz0, sz1, sx0, sx1, ss):
                    nc.sync.wait_ge(smx.h, smx.n)
                bar()
                if P1CAP >= NM:
                    dma(nc.sync, d_dtsc[0], sdtev[:])
                    dwait()
                bar()
            _sc.__exit__(None, None, None)

        if STAGE >= 2:
            _sc = nc.named_scope("P2"); _sc.__enter__()
            with MultiCtx(
                nc.sbuf_tensor("p2raw", [H, TOK], F32),
                nc.sbuf_tensor("p2dt", [H, TOK], F32),
            ) as (p2raw, p2dt):
                dma(nc.sync, p2raw[:], d_dtsc[0])
                dwait()
                bar()
                nc.scalar.activation(p2dt[:], p2raw[:], AF.Abs,
                                     bias=s_dtbias[:])
                bar()
                nc.scalar.activation(p2dt[:], p2dt[:], AF.Exp, scale=-1.0)
                bar()
                nc.scalar.activation(p2dt[:], p2dt[:], AF.Ln, bias=1.0)
                nc.vector.tensor_scalar(p2raw[:], p2raw[:], s_dtbias[:], 0.0,
                                        OP.add, OP.max)
                bar()
                nc.vector.tensor_add(p2dt[:], p2dt[:], p2raw[:])
                bar()
                nc.scalar.activation(s_colt[:], p2dt[:], AF.Ln)
                nc.scalar.activation(s_acol[:], s_acol[:], AF.Exp)
                bar()
                dma(nc.sync, d_dtsc[1], p2dt[:])
                nc.vector.tensor_scalar_mul(s_acol[:], s_acol[:], -1.0)
                bar()
                nc.vector.tensor_scalar_mul(p2raw[:], p2dt[:], s_acol[:])
                bar()
                for c in range(NCH):
                    nc.vector.tensor_tensor_scan(
                        s_acum[:, c * CH:(c + 1) * CH],
                        p2raw[:, c * CH:(c + 1) * CH],
                        p2raw[:, c * CH:(c + 1) * CH], 0.0, OP.add, OP.bypass)
                bar()
                nc.vector.tensor_sub(s_colt[:], s_colt[:], s_acum[:])
                nc.vector.tensor_copy(
                    s_asum[:],
                    s_acum[:].rearrange("h (c t) -> h c t", t=CH)
                    [:, :, CH - 1])
                bar()
                nc.scalar.activation(s_tmp64[:, 0:NCH], s_asum[:], AF.Exp)
                bar()
                with nc.psum_tensor("pdc", [64, 64], F32) as pdc:
                    nc.tensor.matmul(pdc[0:NCH, 0:64], s_tmp64[:, 0:NCH],
                                     s_i64[:], start=True, stop=True)
                    bar()
                    nc.vector.tensor_copy(s_tmp64b[0:NCH, 0:64],
                                          pdc[0:NCH, 0:64])
                    bar()
                dma(nc.sync, d_dcrow.ap(), s_tmp64b[0:NCH, 0:64])
                for c in range(NCH):
                    dma(nc.sync, d_acrow[c].rearrange("(h l) -> h l", h=H),
                        s_acum[:, c * CH:(c + 1) * CH])
                dwait()
                bar()
                for c in range(NCH):
                    dma(nc.sync, s_dcb[:, c * H:(c + 1) * H],
                        d_dcrow[c:c + 1, :].broadcast_to([128, H]))
                dwait()
                bar()
                with nc.psum_tensor("p2t", [128, 128], F32) as p2t:
                    for c in range(NCH):
                        nc.tensor.matmul(p2t[:, 0:64],
                                         s_colt[:, c * CH:(c + 1) * CH],
                                         s_i64[:], start=True, stop=True)
                        bar()
                        nc.scalar.copy(s_colTa[:, c * H:(c + 1) * H],
                                       p2t[:, 0:64])
                        bar()
                        nc.scalar.activation(s_tmp64[:],
                                             s_acum[:, c * CH:(c + 1) * CH],
                                             AF.Exp)
                        bar()
                        nc.tensor.matmul(p2t[:, 0:64], s_tmp64[:], s_i64[:],
                                         start=True, stop=True)
                        bar()
                        nc.scalar.copy(s_evTa[:, c * H:(c + 1) * H],
                                       p2t[:, 0:64])
                        bar()
                dump("dt", p2dt[:])
            dump("acum", s_acum[:])
            dump("z", d_zsc.ap())
            dump("x", d_xsc.ap())
            _sc.__exit__(None, None, None)

        if STAGE >= 3:
            _sc = nc.named_scope("P3"); _sc.__enter__()
            with MultiCtx(
                nc.sbuf_tensor("xtm0", [128, DIN], BF16),
                nc.sbuf_tensor("xtm1", [128, DIN], BF16),
                nc.sbuf_tensor("sxd", [128, DIN], BF16),
                nc.sbuf_tensor("sBt", [128, 128], BF16),
                nc.sbuf_tensor("sdtd", [128, 64], BF16),
                nc.sbuf_tensor("sSc", [128, DIN], F32),
                nc.sbuf_tensor("p3dt", [H, TOK], F32),
                nc.psum_tensor("p3a", [128, 2048], F32),
                nc.psum_tensor("p3s", [128, 128], F32),
                nc.psum_tensor("p3w", [128, 64], F32),
            ) as (xtm0, xtm1, sxd, sBt, sdtd, sSc, p3dt, p3a, p3s, p3w):
                dma(nc.sync, p3dt[:], d_dtsc[1])
                nc.vector.memset(s_R[:], 0.0)
                xtms = (xtm0, xtm1)
                dma(nc.sync, xtm0[:], d_xsc[0], transpose=True)
                dwait()
                bar()
                dump("xtm", xtm0[:])
                for c in range(NCH):
                    if c + 1 < NCH:
                        dma(nc.sync, xtms[(c + 1) % 2][:], d_xsc[c + 1],
                            transpose=True)
                    xtm = xtms[c % 2]
                    nc.tensor.matmul(p3s[:], s_Bbf[:, c * CH:(c + 1) * CH],
                                     s_ibf[:], start=True, stop=True)
                    nc.scalar.activation(
                        s_tmp64[:], s_acum[:, c * CH:(c + 1) * CH], AF.Exp,
                        scale=-1.0,
                        bias=s_acum[:, c * CH + CH - 1:c * CH + CH])
                    bar()
                    nc.vector.tensor_mul(s_tmp64[:], s_tmp64[:],
                                         p3dt[:, c * CH:(c + 1) * CH])
                    nc.scalar.copy(sBt[:], p3s[:])
                    bar()
                    nc.tensor.matmul(p3w[:], s_tmp64[:], s_i64[:], start=True,
                                     stop=True)
                    bar()
                    nc.vector.tensor_copy(sdtd[:], p3w[:])
                    bar()
                    nc.vector.tensor_tensor(
                        sxd[:].rearrange("s (h p) -> s h p", p=PH),
                        xtm[:].rearrange("s (h p) -> s h p", p=PH),
                        sdtd[:].rearrange("s (h o) -> s h o", o=1)
                        .broadcast_to([128, H, PH]), OP.mult)
                    bar()
                    for half in range(2):
                        for j in range(4):
                            nc.tensor.matmul(
                                p3a[:, j * 512:(j + 1) * 512], sBt[:],
                                sxd[:, half * 2048 + j * 512:
                                    half * 2048 + (j + 1) * 512],
                                start=True, stop=True)
                        bar()
                        nc.scalar.copy(sSc[:, half * 2048:(half + 1) * 2048],
                                       p3a[:])
                        bar()
                    dma(nc.sync, d_scsc[c], sSc[:], sem=ss)
                    nc.vector.tensor_tensor(
                        s_R[:].rearrange("n (h p) -> n h p", p=PH),
                        s_R[:].rearrange("n (h p) -> n h p", p=PH),
                        s_dcb[:, c * H:(c + 1) * H]
                        .rearrange("n (h o) -> n h o", o=1)
                        .broadcast_to([128, H, PH]), OP.mult)
                    bar()
                    nc.vector.tensor_add(s_R[:], s_R[:], sSc[:])
                    bar()
                nc.vector.reduce_sum(s_tmp64[:, 0:1], s_asum[:], AX.X)
                nc.sync.wait_ge(ss.h, ss.n)
                bar()
                dump("sc", d_scsc.ap())
                dump("slocal", s_R[:])
                dma(nc.sync,
                    d_agin.ap()[0:128 * DIN].rearrange("(p f) -> p f", p=128),
                    s_R[:])
                dma(nc.sync,
                    d_agin.ap()[128 * DIN:].rearrange("(h o) -> h o", o=1),
                    s_tmp64[:, 0:1])
                dwait()
                bar()
                _sc.__exit__(None, None, None)
                _sc = nc.named_scope("AG"); _sc.__enter__()
                cc.inc(nc.gpsimd.collective_compute(
                    "AllGather", OP.bypass,
                    replica_groups=[[0, 1, 2, 3], [4, 5, 6, 7]],
                    ins=[d_agin.ap()], outs=[d_agout.ap()]), 1)
                nc.gpsimd.wait_ge(cc.h, cc.n)
                bar()
                _sc.__exit__(None, None, None)

        _colt.__exit__(None, None, None)

        if STAGE >= 4:
            _sc = nc.named_scope("P4carry"); _sc.__enter__()
            with MultiCtx(
                nc.sbuf_tensor("sSg", [128, DIN], F32),
                nc.sbuf_tensor("sWb", [128, 4 * 64], F32),
                nc.psum_tensor("p4w", [64, 64], F32),
            ) as (sSg, sWb, p4w):
                nc.vector.memset(s_atote[:], 1.0)
                bar()
                for p in range(4):
                    dma(nc.sync, s_atote[p:p + 1, :],
                        d_agout[p, 128 * DIN:].rearrange("(o h) -> o h", o=1))
                dwait()
                bar()
                nc.tensor.matmul(p4w[0:64, 0:4], s_atote[:], s_mq.ap(),
                                 start=True, stop=True)
                bar()
                nc.scalar.activation(s_tmp64[:, 0:4], p4w[0:64, 0:4], AF.Exp)
                bar()
                nc.tensor.matmul(p4w[0:4, 0:64], s_tmp64[:, 0:4], s_i64[:],
                                 start=True, stop=True)
                bar()
                nc.vector.tensor_copy(s_tmp64b[0:4, 0:64], p4w[0:4, 0:64])
                bar()
                dma(nc.sync, d_wrow.ap(), s_tmp64b[0:4, 0:64])
                dwait()
                bar()
                for p in range(4):
                    dma(nc.sync, sWb[:, p * 64:(p + 1) * 64],
                        d_wrow[p:p + 1, :].broadcast_to([128, H]))
                dwait()
                bar()
                nc.vector.memset(s_R[:], 0.0)
                bar()
                for p in range(4):
                    dma(nc.sync, sSg[:],
                        d_agout[p, 0:128 * DIN]
                        .rearrange("(n f) -> n f", n=128))
                    dwait()
                    bar()
                    nc.vector.tensor_tensor(
                        sSg[:].rearrange("n (h p) -> n h p", p=PH),
                        sSg[:].rearrange("n (h p) -> n h p", p=PH),
                        sWb[:, p * 64:(p + 1) * 64]
                        .rearrange("n (h o) -> n h o", o=1)
                        .broadcast_to([128, H, PH]), OP.mult)
                    bar()
                    nc.vector.tensor_add(s_R[:], s_R[:], sSg[:])
                    bar()
                dump("carry", s_R[:])
            _sc.__exit__(None, None, None)
            _sc = nc.named_scope("P4main"); _sc.__enter__()
            with MultiCtx(
                nc.sbuf_tensor("s_prevbf", [128, DIN], BF16),
                nc.sbuf_tensor("s_yng", [128, 32 * 512], BF16),
                nc.sbuf_tensor("xtmA", [128, DIN], BF16),
                nc.sbuf_tensor("xtmB", [128, DIN], BF16),
                nc.sbuf_tensor("xdc", [128, DIN], BF16),
                nc.sbuf_tensor("srB0", [128, 16 * CH], F32),
                nc.sbuf_tensor("srB1", [128, 16 * CH], F32),
                nc.sbuf_tensor("sLW0", [128, 16 * CH], BF16),
                nc.sbuf_tensor("sLW1", [128, 16 * CH], BF16),
                nc.sbuf_tensor("sGt", [128, 128], F32),
                nc.sbuf_tensor("sYtm", [128, 2048], BF16),
                nc.sbuf_tensor("sYo", [128, 2048], BF16),
                nc.sbuf_tensor("syfm", [128, DIN], F32),
                nc.sbuf_tensor("shA", [128, 2048], F32),
                nc.sbuf_tensor("shB", [128, 2048], F32),
                nc.sbuf_tensor("srs", [1, 128], F32),
                nc.sbuf_tensor("swog0", [128, 32 * 128], BF16),
                nc.sbuf_tensor("swog1", [128, 32 * 128], BF16),
                nc.sbuf_tensor("sxrg", [128, 1024], F32),
                nc.sbuf_tensor("sxrb", [128, 1024], BF16),
                nc.sbuf_tensor("sxrb2", [128, 1024], BF16),
                nc.sbuf_tensor("srowt", [1, 1024], F32),
                nc.sbuf_tensor("s_mask", [128, 128], F32),
                nc.psum_tensor("pYd", [128, 1024], F32),
                nc.psum_tensor("pY2", [128, 1024], F32),
                nc.psum_tensor("pYo", [128, 1024], F32),
                nc.psum_tensor("p4s", [128, 128], F32),
            ) as (s_prevbf, s_yng, xtmA, xtmB, xdc, srB0, srB1, sLW0, sLW1,
                  sGt, sYtm, sYo, syfm, shA, shB, srs, swog0, swog1,
                  sxrg, sxrb, sxrb2, srowt, s_mask, pYd, pY2, pYo, p4s):
                nc.vector.memset(s_mask[:], 0.0)
                bar()
                nc.gpsimd.affine_select(s_mask[:], s_mask[:], [[1, 128]],
                                        OP.is_ge, NEG, base=0,
                                        channel_multiplier=-1)
                bar()
                wov = d_woutT.ap().rearrange("(k p) m -> p k m", p=128)
                smk = []
                for m in range(16):
                    buf = (swog0, swog1)[m % 2]
                    wsm = wsp[m % 2]
                    if m >= 2:
                        nc.gpsimd.wait_ge(ss.h, smk[m - 2])
                    dma(nc.gpsimd, buf[:].rearrange("p (k m) -> p k m", m=128),
                        wov[:, :, m * 128:(m + 1) * 128], sem=wsm)
                    nc.vector.wait_ge(wsm.h, wsm.n)
                    zs.inc(nc.vector.tensor_tensor(
                        buf[:].rearrange("p (k m) -> p k m", m=128),
                        buf[:].rearrange("p (k m) -> p k m", m=128),
                        s_nw[:].rearrange("p (k o) -> p k o", o=1)
                        .broadcast_to([128, 32, 128]), OP.mult))
                    nc.sync.wait_ge(zs.h, zs.n)
                    dma(nc.sync,
                        d_woutbf[:, m * 128:(m + 1) * 128]
                        .rearrange("(k p) m -> p k m", p=128),
                        buf[:].rearrange("p (k m) -> p k m", m=128), sem=ss)
                    smk.append(ss.n)
                    nc.vector.wait_ge(ss.h, ss.n)
                nc.sync.wait_ge(ss.h, ss.n)
                bar()
                dump("woutbf", d_woutbf.ap())

                m_wml, m_dad, m_aff, m_exp = {}, {}, {}, {}
                m_mm1, m_mm2, m_yev, m_yov = {}, {}, {}, {}
                m_tb, m_tba, m_gmm, m_sgt = {}, {}, {}, {}
                m_prev, m_radd, m_zml = {}, {}, {}
                m_sq, m_rms, m_ts, m_sqr = {}, {}, {}, {}
                m_rcp, m_bc, m_ynm = {}, {}, {}
                rb_ld, sc_ld, z_ld, xtm_ld, xdc_ld, u_ld = \
                    {}, {}, {}, {}, {}, {}
                m_main, m_sum, m_tta, m_xrb = {}, {}, {}, {}
                m_st, m_rowt, m_lnst = {}, {}, {}
                srBs = (srB0, srB1)
                sLWs = (sLW0, sLW1)
                szq = (sz0, sz1)
                shq = (sh0, sh1)
                shb = (shA, shB)
                uqq = (uq0, uq1)
                sxq = (sx0, sx1)
                xtms = (xtmA, xtmB)
                pYp = (pYd, pY2)

                def W(eng, sem, mark):
                    if mark is not None:
                        eng.wait_ge(sem.h, mark)

                def ld_srb(c, q):
                    W(nc.sync, cs, m_wml.get((c, q - 2)) if q >= 2
                      else m_wml.get((c - 1, q + 2)))
                    dma(nc.sync, srBs[q % 2][:],
                        d_acrow[c:c + 1, q * 16 * CH:(q + 1) * 16 * CH]
                        .broadcast_to([128, 16 * CH]), sem=szq[q % 2])
                    rb_ld[(c, q)] = szq[q % 2].n

                def ld_sc(c, h):
                    W(nc.sync, gsm, m_zml.get((c - 1, h)))
                    dma(nc.sync, shb[h][:],
                        d_scsc[c][:, h * 2048:(h + 1) * 2048], sem=shq[h])
                    sc_ld[(c, h)] = shq[h].n

                def ld_z(c, h):
                    W(nc.sync, cs, m_radd[(c, h)])
                    dma(nc.sync,
                        shb[h][:].rearrange("p (i t) -> p i t", t=128),
                        d_zsc[h * 2048:(h + 1) * 2048, c * CH:(c + 1) * CH]
                        .rearrange("(i p) t -> p i t", p=128), sem=shq[h])
                    z_ld[(c, h)] = shq[h].n

                def ld_xtm(c):
                    W(nc.scalar, ps, m_mm1.get((c - 2, 3)))
                    dma(nc.scalar, xtms[c % 2][:], d_xsc[c], transpose=True,
                        sem=sxq[c % 2])
                    xtm_ld[c] = sxq[c % 2].n

                def ld_xdc(c):
                    W(nc.scalar, cs, m_tba.get((c - 1, 1, 1)))
                    dma(nc.scalar,
                        xdc[:].rearrange("p (i t) -> p i t", t=128),
                        d_xdsc[c].rearrange("(i p) t -> p i t", p=128),
                        sem=sxc)
                    xdc_ld[c] = sxc.n

                def g_mm(c):
                    W(nc.tensor, es, m_sgt.get(c - 1))
                    ps.inc(nc.tensor.matmul(
                        p4s[:], s_Bbf[:, c * CH:(c + 1) * CH],
                        s_Cbf[:, c * CH:(c + 1) * CH], start=True, stop=True))
                    m_gmm[c] = ps.n
                    W(nc.scalar, ps, m_gmm[c])
                    W(nc.scalar, cs, m_wml.get((c - 1, 3)))
                    es.inc(nc.scalar.copy(sGt[:], p4s[:]))
                    m_sgt[c] = es.n

                def wchain(c, q):
                    srq, slw = srBs[q % 2], sLWs[q % 2]
                    W(nc.vector, szq[q % 2], rb_ld[(c, q)])
                    cs.inc(nc.vector.tensor_tensor(
                        srq[:].rearrange("s (h l) -> s h l", l=CH),
                        srq[:].rearrange("s (h l) -> s h l", l=CH),
                        s_colTa[:, c * H + q * 16:c * H + q * 16 + 16]
                        .rearrange("s (h o) -> s h o", o=1)
                        .broadcast_to([128, 16, CH]), OP.add))
                    m_dad[(c, q)] = cs.n
                    W(nc.gpsimd, cs, m_dad[(c, q)])
                    gsm.inc(nc.gpsimd.tensor_tensor(
                        srq[:].rearrange("s (h l) -> s h l", l=CH),
                        srq[:].rearrange("s (h l) -> s h l", l=CH),
                        s_mask[:].rearrange("s (o l) -> s o l", o=1)
                        .broadcast_to([128, 16, CH]), OP.add))
                    m_aff[(c, q)] = gsm.n
                    W(nc.scalar, gsm, m_aff[(c, q)])
                    es.inc(nc.scalar.activation(srq[:], srq[:], AF.Exp))
                    m_exp[(c, q)] = es.n
                    W(nc.vector, es, m_exp[(c, q)])
                    if q == 0:
                        W(nc.vector, es, m_sgt[c])
                    W(nc.vector, ps, m_mm1.get((c, q - 2)) if q >= 2
                      else m_mm1.get((c - 1, q + 2)))
                    cs.inc(nc.vector.tensor_tensor(
                        slw[:].rearrange("s (h l) -> s h l", l=CH),
                        srq[:].rearrange("s (h l) -> s h l", l=CH),
                        sGt[:].rearrange("s (o l) -> s o l", o=1)
                        .broadcast_to([128, 16, CH]), OP.mult))
                    m_wml[(c, q)] = cs.n
                    if "wt" in d_dbg and c == 0 and q == 0:
                        bar()
                        dump("wt", slw[:])
                    nq_c, nq_q = (c, q + 2) if q < 2 else (c + 1, q - 2)
                    if nq_c < NCH:
                        ld_srb(nq_c, nq_q)

                def mm_q(c, q):
                    grp, c4 = c // 4, c % 4
                    qq = q % 2
                    slw = sLWs[q % 2]
                    xtm = xtms[c % 2]
                    pYb = pYp[q % 2]
                    W(nc.tensor, cs, m_wml[(c, q)])
                    if q < 2:
                        W(nc.tensor, cs, m_tba.get((c - 1, 1, q)))
                    else:
                        W(nc.tensor, cs, m_tba.get((c, 0, q - 2)))
                    if q == 0:
                        W(nc.tensor, sxq[c % 2], xtm_ld[c])
                        if c4 == 0:
                            W(nc.tensor, cs, m_tta.get((grp - 1, 15)))
                        if q == 1 and c4 == 0:
                            pass
                    for hh in range(16):
                        h = q * 16 + hh
                        _mm = nc.tensor.matmul(
                            pYb[:, hh * 64:(hh + 1) * 64],
                            slw[:, hh * CH:(hh + 1) * CH],
                            xtm[:, h * PH:(h + 1) * PH],
                            start=True, stop=True)
                    ps.inc(_mm)
                    m_mm1[(c, q)] = ps.n
                    W(nc.tensor, cs, m_yov.get((c, q - 1)) if q > 0
                      else m_yov.get((c - 1, 3)))
                    if q == 0:
                        W(nc.tensor, es, m_prev[c])
                        W(nc.tensor, cs, m_ynm.get(c - 1))
                        if c4 == 0:
                            W(nc.tensor, cs, m_rowt.get(grp - 1))
                    for hh in range(16):
                        h = q * 16 + hh
                        _mm = nc.tensor.matmul(
                            pYo[:, hh * 64:(hh + 1) * 64],
                            s_Cbf[:, c * CH:(c + 1) * CH],
                            s_prevbf[:, h * PH:(h + 1) * PH],
                            start=True, stop=True)
                    ps.inc(_mm)
                    m_mm2[(c, q)] = ps.n
                    W(nc.scalar, ps, m_mm1[(c, q)])
                    W(nc.scalar, ps, m_tb[(c, 0, q - 2)] if q >= 2
                      else m_tb.get((c - 1, 1, q)))
                    es.inc(nc.scalar.copy(
                        sYtm[:, qq * 1024:(qq + 1) * 1024], pYb[:]))
                    m_yev[(c, q)] = es.n
                    W(nc.vector, ps, m_mm2[(c, q)])
                    W(nc.vector, ps, m_tb[(c, 0, q - 2)] if q >= 2
                      else m_tb.get((c - 1, 1, q)))
                    cs.inc(nc.vector.tensor_tensor(
                        sYo[:, qq * 1024:(qq + 1) * 1024]
                        .rearrange("s (hh p) -> s hh p", p=PH),
                        pYo[:].rearrange("s (hh p) -> s hh p", p=PH),
                        s_evTa[:, c * H + q * 16:c * H + q * 16 + 16]
                        .rearrange("s (h o) -> s h o", o=1)
                        .broadcast_to([128, 16, PH]), OP.mult))
                    m_yov[(c, q)] = cs.n

                def tb_hf(c, hf):
                    for b in range(2):
                        q = hf * 2 + b
                        W(nc.tensor, es, m_yev[(c, q)])
                        W(nc.tensor, cs, m_yov[(c, q)])
                        for i8 in range(8):
                            i = b * 8 + i8
                            nc.tensor.matmul(
                                pYp[b][:, i8 * 128:(i8 + 1) * 128],
                                sYtm[:, i * 128:(i + 1) * 128],
                                s_ibf[:], start=True, stop=False)
                            _mm = nc.tensor.matmul(
                                pYp[b][:, i8 * 128:(i8 + 1) * 128],
                                sYo[:, i * 128:(i + 1) * 128],
                                s_ibf[:], start=False, stop=True)
                        ps.inc(_mm)
                        m_tb[(c, hf, b)] = ps.n
                        W(nc.vector, ps, m_tb[(c, hf, b)])
                        if b == 0 and hf == 0:
                            W(nc.vector, sxc, xdc_ld[c])
                        cs.inc(nc.vector.tensor_add(
                            syfm[:, (hf * 16 + b * 8) * 128:
                                 (hf * 16 + b * 8 + 8) * 128],
                            pYp[b][:],
                            xdc[:, (hf * 16 + b * 8) * 128:
                                (hf * 16 + b * 8 + 8) * 128]))
                        m_tba[(c, hf, b)] = cs.n

                def zmul(c, h):
                    W(nc.gpsimd, shq[h], z_ld[(c, h)])
                    W(nc.gpsimd, cs, m_tba[(c, h, 1)])
                    gsm.inc(nc.gpsimd.tensor_tensor(
                        syfm[:, h * 2048:(h + 1) * 2048],
                        syfm[:, h * 2048:(h + 1) * 2048],
                        shb[h][:], OP.mult))
                    m_zml[(c, h)] = gsm.n

                def prev_copy(c):
                    if c > 0:
                        W(nc.scalar, ps, m_mm2[(c - 1, 3)])
                        W(nc.scalar, cs, m_radd[(c - 1, 1)])
                    es.inc(nc.scalar.copy(s_prevbf[:], s_R[:]))
                    m_prev[c] = es.n

                def r_update(c):
                    W(nc.vector, es, m_prev[c])
                    cs.inc(nc.vector.tensor_tensor(
                        s_R[:].rearrange("n (h p) -> n h p", p=PH),
                        s_R[:].rearrange("n (h p) -> n h p", p=PH),
                        s_dcb[:, c * H:(c + 1) * H]
                        .rearrange("n (h o) -> n h o", o=1)
                        .broadcast_to([128, H, PH]), OP.mult))
                    for h in range(2):
                        W(nc.vector, shq[h], sc_ld[(c, h)])
                        cs.inc(nc.vector.tensor_add(
                            s_R[:, h * 2048:(h + 1) * 2048],
                            s_R[:, h * 2048:(h + 1) * 2048], shb[h][:]))
                        m_radd[(c, h)] = cs.n
                        ld_z(c, h)

                def out_proj(g, c_tail):
                    wobv = d_woutbf.ap().rearrange("(k p) m -> p k m", p=128)
                    if "ynorm" in d_dbg and g == 0:
                        bar()
                        dump("ynorm", s_yng[:])
                    for mm in range(2):
                        W(nc.scalar, es, m_xrb.get((g - 1, 14 + mm)))
                        W(nc.scalar, srx, m_st.get((g - 1, 14 + mm)))
                        dma(nc.scalar, sxrg[:, mm * 512:(mm + 1) * 512],
                            d_uT[mm * 128:(mm + 1) * 128,
                                 3 + g * 512:3 + (g + 1) * 512],
                            sem=uqq[mm])
                        u_ld[(g, mm)] = uqq[mm].n
                    W(nc.gpsimd, ps, m_rms[c_tail])
                    dma(nc.gpsimd,
                        swog0[:].rearrange("p (k m) -> p k m", m=128),
                        wobv[:, :, 0:128], sem=wsp[0])
                    wmk = {0: wsp[0].n}
                    for m in range(16):
                        hb = m % 2
                        buf = (swog0, swog1)[hb]
                        if m + 1 < 16:
                            W(nc.gpsimd, ps, m_main.get((g, m - 1)))
                            dma(nc.gpsimd,
                                (swog0, swog1)[(m + 1) % 2][:]
                                .rearrange("p (k m) -> p k m", m=128),
                                wobv[:, :, (m + 1) * 128:(m + 2) * 128],
                                sem=wsp[(m + 1) % 2])
                            wmk[m + 1] = wsp[(m + 1) % 2].n
                        W(nc.tensor, wsp[hb], wmk[m])
                        W(nc.tensor, cs, m_tta.get((g, m - 2)))
                        if m <= 1:
                            W(nc.tensor, cs, m_ynm[c_tail])
                            W(nc.tensor, cs, m_tba[(c_tail, 1, 0)])
                        for k in range(32):
                            _mm = nc.tensor.matmul(
                                pYd[:, hb * 512:(hb + 1) * 512],
                                buf[:, k * 128:(k + 1) * 128],
                                s_yng[:, k * 512:(k + 1) * 512],
                                start=(k == 0), stop=(k == 31))
                        ps.inc(_mm)
                        m_main[(g, m)] = ps.n
                        if m >= 1:
                            pb = (m - 1) % 2
                            W(nc.tensor, es, m_xrb[(g, m - 1)])
                            nc.tensor.matmul(
                                pYo[0:1, 0:512], s_onesb[:],
                                sxrb[:, pb * 512:(pb + 1) * 512],
                                start=(m - 1 == 0), stop=False)
                            ps.inc(nc.tensor.matmul(
                                pYo[0:1, 512:1024], s_onesb[:],
                                sxrb2[:, pb * 512:(pb + 1) * 512],
                                start=(m - 1 == 0), stop=False))
                            m_sum[(g, m - 1)] = ps.n
                        W(nc.vector, uqq[hb], u_ld[(g, m)])
                        W(nc.vector, ps, m_main[(g, m)])
                        cs.inc(nc.vector.tensor_add(
                            sxrg[:, hb * 512:(hb + 1) * 512],
                            sxrg[:, hb * 512:(hb + 1) * 512],
                            pYd[:, hb * 512:(hb + 1) * 512]))
                        m_tta[(g, m)] = cs.n
                        W(nc.sync, cs, m_tta[(g, m)])
                        dma(nc.sync,
                            d_xres[m * 128:(m + 1) * 128,
                                   g * 512:(g + 1) * 512],
                            sxrg[:, hb * 512:(hb + 1) * 512], sem=srx)
                        m_st[(g, m)] = srx.n
                        W(nc.scalar, cs, m_tta[(g, m)])
                        nc.scalar.activation(
                            sxrb[:, hb * 512:(hb + 1) * 512],
                            sxrg[:, hb * 512:(hb + 1) * 512], AF.Copy)
                        es.inc(nc.scalar.activation(
                            sxrb2[:, hb * 512:(hb + 1) * 512],
                            sxrg[:, hb * 512:(hb + 1) * 512], AF.Square))
                        m_xrb[(g, m)] = es.n
                        if m + 2 < 16:
                            W(nc.scalar, es, m_xrb[(g, m)])
                            W(nc.scalar, srx, m_st[(g, m)])
                            dma(nc.scalar,
                                sxrg[:, hb * 512:(hb + 1) * 512],
                                d_uT[(m + 2) * 128:(m + 3) * 128,
                                     3 + g * 512:3 + (g + 1) * 512],
                                sem=uqq[hb])
                            u_ld[(g, m + 2)] = uqq[hb].n
                    W(nc.tensor, es, m_xrb[(g, 15)])
                    nc.tensor.matmul(pYo[0:1, 0:512], s_onesb[:],
                                     sxrb[:, 512:1024], start=False,
                                     stop=True)
                    ps.inc(nc.tensor.matmul(pYo[0:1, 512:1024], s_onesb[:],
                                            sxrb2[:, 512:1024], start=False,
                                            stop=True))
                    m_sum[(g, 15)] = ps.n
                    W(nc.vector, ps, m_sum[(g, 15)])
                    W(nc.vector, ss, m_lnst.get(g - 1))
                    nc.vector.tensor_copy(srowt[0:1, 0:512],
                                          pYo[0:1, 0:512])
                    cs.inc(nc.vector.tensor_copy(srowt[0:1, 512:1024],
                                                 pYo[0:1, 512:1024]))
                    m_rowt[g] = cs.n
                    W(nc.sync, cs, m_rowt[g])
                    dma(nc.sync, d_lnrow[0:1, g * 512:(g + 1) * 512],
                        srowt[0:1, 0:512], sem=ss)
                    dma(nc.sync, d_lnrow[1:2, g * 512:(g + 1) * 512],
                        srowt[0:1, 512:1024], sem=ss)
                    m_lnst[g] = ss.n

                ld_xtm(0)
                ld_xtm(1)
                ld_xdc(0)
                ld_sc(0, 0)
                ld_sc(0, 1)
                ld_srb(0, 0)
                ld_srb(0, 1)
                prev_copy(0)
                g_mm(0)
                wchain(0, 0)
                wchain(0, 1)
                r_update(0)

                for c in range(NCH):
                    grp, c4 = c // 4, c % 4
                    dbgy = ("yfm" in d_dbg and c == 0) or \
                        ("yfm5" in d_dbg and c == 5)
                    mm_q(c, 0)
                    wchain(c, 2)
                    mm_q(c, 1)
                    tb_hf(c, 0)
                    wchain(c, 3)
                    mm_q(c, 2)
                    if not dbgy:
                        zmul(c, 0)
                    if c + 1 < NCH:
                        ld_sc(c + 1, 0)
                    mm_q(c, 3)
                    if c + 1 < NCH:
                        prev_copy(c + 1)
                    tb_hf(c, 1)
                    if dbgy:
                        bar()
                        dump("yfm" if c == 0 else "yfm5", syfm[:])
                        zmul(c, 0)
                    if "prev5" in d_dbg and c == 4:
                        bar()
                        dump("prev5", s_prevbf[:])
                    zmul(c, 1)
                    if c + 1 < NCH:
                        ld_sc(c + 1, 1)
                    W(nc.scalar, gsm, m_zml[(c, 1)])
                    W(nc.scalar, gsm, m_zml[(c, 0)])
                    if c4 == 0:
                        W(nc.scalar, ps, m_main.get((grp - 1, 15)))
                    else:
                        W(nc.scalar, ps, m_rms.get(c - 1))
                    es.inc(nc.scalar.activation(swog0[:], syfm[:],
                                                AF.Square))
                    m_sq[c] = es.n
                    W(nc.tensor, es, m_sq[c])
                    W(nc.tensor, cs, m_yov[(c, 3)])
                    if c4 == 0:
                        W(nc.tensor, cs, m_rowt.get(grp - 1))
                    for i in range(32):
                        _mm = nc.tensor.matmul(
                            pYo[0:1, 128:256], s_onesb[:],
                            swog0[:, i * 128:(i + 1) * 128],
                            start=(i == 0), stop=(i == 31))
                    ps.inc(_mm)
                    m_rms[c] = ps.n
                    W(nc.vector, ps, m_rms[c])
                    W(nc.vector, ps, m_bc.get(c - 1))
                    cs.inc(nc.vector.tensor_scalar(
                        srs[:], pYo[0:1, 128:256], 1.0 / DIN, EPS,
                        OP.mult, OP.add))
                    m_ts[c] = cs.n
                    W(nc.scalar, cs, m_ts[c])
                    es.inc(nc.scalar.activation(srs[:], srs[:], AF.Sqrt))
                    m_sqr[c] = es.n
                    W(nc.vector, es, m_sqr[c])
                    cs.inc(nc.vector.reciprocal(srs[:], srs[:]))
                    m_rcp[c] = cs.n
                    W(nc.tensor, cs, m_rcp[c])
                    ps.inc(nc.tensor.matmul(pYo[:, 0:128], s_ones1[:],
                                            srs[:], start=True, stop=True))
                    m_bc[c] = ps.n
                    W(nc.vector, ps, m_bc[c])
                    if c4 == 0:
                        W(nc.vector, ps, m_main.get((grp - 1, 15)))
                    cs.inc(nc.vector.tensor_tensor(
                        s_yng[:].rearrange("p (i t) -> p i t", t=512)
                        [:, :, c4 * 128:(c4 + 1) * 128],
                        syfm[:].rearrange("p (i t) -> p i t", t=128),
                        pYo[:, 0:128].rearrange("p (o t) -> p o t", o=1)
                        .broadcast_to([128, 32, 128]), OP.mult))
                    m_ynm[c] = cs.n
                    if c + 1 < NCH:
                        g_mm(c + 1)
                        wchain(c + 1, 0)
                        wchain(c + 1, 1)
                        if c + 2 < NCH:
                            ld_xtm(c + 2)
                        ld_xdc(c + 1)
                        r_update(c + 1)
                    if c4 == 3:
                        out_proj(grp, c)
                nc.sync.wait_ge(srx.h, srx.n)
                nc.sync.wait_ge(ss.h, ss.n)
                bar()
                dump("xres", d_xres.ap())
            _sc.__exit__(None, None, None)

        _sR.__exit__(None, None, None)

    if STAGE >= 6:
        _sc = nc.named_scope("P6"); _sc.__enter__()
        _p67 = MultiCtx(
            nc.sbuf_tensor("s_xp1", [128, 16 * 2048], BF16),
            nc.sbuf_tensor("s_mu1", [1, TOK], F32),
            nc.sbuf_tensor("s_rs1", [1, TOK], F32),
        )
        s_xp1, s_mu1, s_rs1 = _p67.__enter__()
        with MultiCtx(
            nc.sbuf_tensor("s_mu1b", [128, 2048], F32),
            nc.sbuf_tensor("s_rs1b", [128, 2048], F32),
            nc.sbuf_tensor("sxr", [128, 2048], F32),
            nc.sbuf_tensor("srowa", [1, 2048], F32),
            nc.sbuf_tensor("sg1", [64, 2048], F32),
            nc.sbuf_tensor("sltm", [128, 16 * 64], F32),
            nc.sbuf_tensor("sxr2", [128, 2048], F32),
            nc.sbuf_tensor("gt0a", [128, 1024], F32),
            nc.sbuf_tensor("gt1a", [128, 1024], F32),
            nc.sbuf_tensor("gmsk", [128, 1024], F32),
            nc.sbuf_tensor("gt2a", [128, 128], F32),
            nc.sbuf_tensor("gt3a", [128, 128], F32),
            nc.sbuf_tensor("gt4a", [128, 128], F32),
            nc.sbuf_tensor("gcola", [128, 16], F32),
            nc.sbuf_tensor("gstga", [64, 2048], BF16),
            nc.sbuf_tensor("gscfa", [64, 2048], F32),
            nc.psum_tensor("p6", [128, 2048], F32),
        ) as (s_mu1b, s_rs1b, sxr, srowa, sg1, sltm, sxr2, gt0a, gt1a,
              gmsk, gt2a, gt3a, gt4a, gcola, gstga, gscfa, p6):
            dma(nc.sync, s_mu1[:], d_lnrow[0:1, :])
            dma(nc.sync, s_rs1[:], d_lnrow[1:2, :])
            dwait()
            bar()
            nc.vector.tensor_scalar_mul(s_mu1[:], s_mu1[:], 1.0 / DM)
            bar()
            nc.vector.tensor_scalar_mul(s_rs1[:], s_rs1[:], 1.0 / DM)
            bar()
            nc.vector.tensor_tensor(srowa[:], s_mu1[:], s_mu1[:], OP.mult)
            bar()
            nc.vector.tensor_sub(s_rs1[:], s_rs1[:], srowa[:])
            bar()
            nc.vector.tensor_scalar(s_rs1[:], s_rs1[:], EPS, None, OP.add)
            bar()
            nc.scalar.activation(s_rs1[:], s_rs1[:], AF.Sqrt)
            bar()
            nc.vector.reciprocal(s_rs1[:], s_rs1[:])
            bar()
            dma(nc.sync, d_murs[0:1, :], s_mu1[:])
            dma(nc.sync, d_murs[1:2, :], s_rs1[:])
            dwait()
            bar()
            for t in range(4):
                nc.tensor.matmul(p6[:, t * 512:(t + 1) * 512], s_ones1[:],
                                 s_mu1[0:1, t * 512:(t + 1) * 512],
                                 start=True, stop=True)
            bar()
            nc.scalar.copy(s_mu1b[:], p6[:])
            bar()
            for t in range(4):
                nc.tensor.matmul(p6[:, t * 512:(t + 1) * 512], s_ones1[:],
                                 s_rs1[0:1, t * 512:(t + 1) * 512],
                                 start=True, stop=True)
            bar()
            nc.scalar.copy(s_rs1b[:], p6[:])
            bar()
            sxr6_ = (sxr, sxr2)
            shq6 = (sh0, sh1)
            xl6, ml6 = {}, {}

            def ld6(m):
                p = m % 2
                if (m - 2) in ml6:
                    nc.sync.wait_ge(cs.h, ml6[m - 2])
                dma(nc.sync, sxr6_[p][:],
                    d_xres[m * 128:(m + 1) * 128, :], sem=shq6[p])
                xl6[m] = shq6[p].n

            ld6(0)
            ld6(1)
            for m in range(16):
                p = m % 2
                nc.vector.wait_ge(shq6[p].h, xl6[m])
                nc.vector.tensor_sub(sxr6_[p][:], sxr6_[p][:], s_mu1b[:])
                cs.inc(nc.vector.tensor_tensor(
                    s_xp1[:, m * 2048:(m + 1) * 2048],
                    sxr6_[p][:], s_rs1b[:], OP.mult))
                ml6[m] = cs.n
                if m + 2 < 16:
                    ld6(m + 2)
            bar()
            dump("xp1", s_xp1[:])
            for t in range(4):
                for k in range(16):
                    nc.tensor.matmul(
                        p6[0:64, t * 512:(t + 1) * 512],
                        s_gateT[:, k * 64:(k + 1) * 64],
                        s_xp1[:, k * 2048 + t * 512:k * 2048 + (t + 1) * 512],
                        start=(k == 0), stop=(k == 15))
            bar()
            nc.vector.tensor_copy(sg1[:], p6[0:64, :])
            bar()
            for t in range(4):
                nc.tensor.matmul(p6[0:64, t * 512:(t + 1) * 512], s_PT[:],
                                 sg1[:, t * 512:(t + 1) * 512],
                                 start=True, stop=True)
            bar()
            nc.vector.tensor_copy(sg1[:], p6[0:64, :])
            bar()
            for i in range(16):
                nc.tensor.matmul(p6[:, i * 64:(i + 1) * 64],
                                 sg1[:, i * 128:(i + 1) * 128], s_i64[:],
                                 start=True, stop=True)
            bar()
            nc.scalar.copy(sltm[:], p6[:, 0:1024])
            bar()
            dump("scoretm", sltm[:])
            sl3 = sltm[:].rearrange("p (i e) -> p i e", e=64)
            g03 = gt0a[:].rearrange("p (i e) -> p i e", e=64)
            g13 = gt1a[:].rearrange("p (i e) -> p i e", e=64)
            gc3 = gcola[:].rearrange("p (i o) -> p i o", o=1)
            nc.vector.tensor_reduce(gcola[:], sl3, AX.X, OP.max, negate=True)
            bar()
            nc.vector.tensor_tensor(g03, sl3,
                                    gc3.broadcast_to([128, 16, 64]), OP.add)
            bar()
            nc.scalar.activation(gt0a[:], gt0a[:], AF.Exp)
            bar()
            nc.vector.tensor_reduce(gcola[:], g03, AX.X, OP.add)
            bar()
            nc.vector.reciprocal(gcola[:], gcola[:])
            bar()
            nc.vector.tensor_tensor(g03, g03,
                                    gc3.broadcast_to([128, 16, 64]), OP.mult)
            bar()
            for i in range(16):
                nc.vector.max(out=gt2a[:, i * 8:(i + 1) * 8],
                              in_=gt0a[:, i * 64:(i + 1) * 64])
            bar()
            nc.vector.tensor_tensor(
                g13, g03,
                gt2a[:].rearrange("p (i e) -> p i e", e=8)[:, :, 7:8]
                .broadcast_to([128, 16, 64]), OP.is_ge)
            bar()
            nc.vector.tensor_mul(gt1a[:], gt1a[:], gt0a[:])
            bar()
            nc.vector.tensor_reduce(
                gt3a[:], gt1a[:].rearrange("p (x e) -> p x e", e=8), AX.X,
                OP.add)
            bar()
            nc.vector.tensor_scalar(gmsk[:], gt1a[:], 0.0, None, OP.is_gt)
            bar()
            nc.vector.scalar_tensor_tensor(gt1a[:], gt1a[:], 100.0, gmsk[:],
                                           OP.add, OP.mult)
            bar()
            nc.vector.tensor_scalar(gt1a[:], gt1a[:], 100.0, None,
                                    OP.subtract)
            bar()
            nc.vector.tensor_reduce(
                gt4a[:], gt1a[:].rearrange("p (x e) -> p x e", e=8), AX.X,
                OP.max, negate=True)
            bar()
            nc.vector.tensor_tensor(
                gt1a[:].rearrange("p (x e) -> p x e", e=8),
                gt1a[:].rearrange("p (x e) -> p x e", e=8),
                gt4a[:].rearrange("p (x o) -> p x o", o=1)
                .broadcast_to([128, 128, 8]), OP.add)
            bar()
            nc.scalar.activation(gt1a[:], gt1a[:], AF.Exp)
            bar()
            nc.vector.tensor_reduce(
                gt4a[:], gt1a[:].rearrange("p (x e) -> p x e", e=8), AX.X,
                OP.add)
            bar()
            nc.vector.reciprocal(gt4a[:], gt4a[:])
            bar()
            nc.vector.tensor_tensor(
                gt1a[:].rearrange("p (x e) -> p x e", e=8),
                gt1a[:].rearrange("p (x e) -> p x e", e=8),
                gt4a[:].rearrange("p (x o) -> p x o", o=1)
                .broadcast_to([128, 128, 8]), OP.mult)
            bar()
            for i in range(16):
                nc.tensor.matmul(p6[0:64, i * 128:(i + 1) * 128],
                                 gt1a[:, i * 64:(i + 1) * 64], s_i128[:],
                                 start=True, stop=True)
            bar()
            nc.scalar.copy(gstga[:], p6[0:64, :])
            bar()
            for i in range(16):
                nc.tensor.matmul(p6[0:8, i * 128:(i + 1) * 128],
                                 gt3a[:, i * 8:(i + 1) * 8], s_i128[:],
                                 start=True, stop=True)
            bar()
            nc.vector.tensor_copy(gscfa[0:8, :], p6[0:8, :])
            bar()
            for g in range(NG):
                dma(nc.sync, d_soft8[:, g, :], gstga[g * 8:(g + 1) * 8, :])
                dma(nc.sync, d_scrow[g:g + 1, :], gscfa[g:g + 1, :])
            dwait()
            bar()
            dump("soft8", d_soft8.ap())
            dump("scrow", d_scrow.ap())
        _sc.__exit__(None, None, None)

        if STAGE >= 7:
            _sc = nc.named_scope("P7"); _sc.__enter__()
            with MultiCtx(
                nc.sbuf_tensor("fw0", [128, 16 * 128], BF16),
                nc.sbuf_tensor("fw1", [128, 16 * 128], BF16),
                nc.sbuf_tensor("sgg", [128, 2048], F32),
                nc.sbuf_tensor("sab", [128, 11 * 2048], BF16),
                nc.sbuf_tensor("sSgb", [128, 2048], F32),
                nc.sbuf_tensor("swpt", [8, 128], BF16),
                nc.sbuf_tensor("syd0", [128, 2048], F32),
                nc.sbuf_tensor("syd1", [128, 2048], F32),
                nc.sbuf_tensor("sydp0", [128, 2048], F32),
                nc.sbuf_tensor("sydp1", [128, 2048], F32),
                nc.sbuf_tensor("s7s", [8, 2048], BF16),
                nc.sbuf_tensor("s7r", [1, 2048], F32),
                nc.psum_tensor("f7a", [128, 1024], F32),
                nc.psum_tensor("f7b", [128, 1024], F32),
                nc.psum_tensor("f7t", [8, 128], F32),
            ) as (fw0, fw1, sgg, sab, sSgb, swpt, syd0, syd1,
                  sydp0, sydp1, s7s, s7r, f7a, f7b, f7t):
                fws = (fw0, fw1)
                f7ms = (f7a, f7b)
                sydps = (sydp0, sydp1)
                y2st, y2pl, y2ev = {}, {}, {}
                fwmk = {0: None, 1: None}
                for g in range(NG):
                    dma(nc.sync, s7s[:], d_soft8[:, g, :])
                    dma(nc.sync, s7r[:], d_scrow[g:g + 1, :])
                    dwait()
                    bar()
                    for t in range(4):
                        nc.tensor.matmul(
                            f7ms[t // 2][:, (t % 2) * 512:(t % 2 + 1) * 512],
                            s_ones1[:], s7r[0:1, t * 512:(t + 1) * 512],
                            start=True, stop=True)
                    bar()
                    nc.scalar.copy(sSgb[:, 0:1024], f7a[:])
                    nc.scalar.copy(sSgb[:, 1024:2048], f7b[:])
                    bar()
                    emk = {0: es.n, 1: es.n}
                    cmk = {0: cs.n, 1: cs.n}
                    zmk = zs.n
                    gvw = d_gpT[g].rearrange("(k p) i -> p k i", p=128)
                    uvw = d_upT[g].rearrange("(k p) i -> p k i", p=128)
                    wvs = (gvw, uvw)
                    wmarks = []
                    if fwmk[0] is not None:
                        nc.gpsimd.wait_ge(ps.h, fwmk[0])
                    dma(nc.gpsimd, fw0[:].rearrange("p (k m) -> p k m", m=128),
                        gvw[:, :, 0:128], sem=wsp[0])
                    wmarks.append((wsp[0], wsp[0].n))
                    for step in range(22):
                        m, proj = step // 2, step % 2
                        buf = fws[step % 2]
                        if step + 1 < 22:
                            m2, p2 = (step + 1) // 2, (step + 1) % 2
                            wsm = wsp[(step + 1) % 2]
                            if fwmk[(step + 1) % 2] is not None:
                                nc.gpsimd.wait_ge(ps.h, fwmk[(step + 1) % 2])
                            dma(nc.gpsimd,
                                fws[(step + 1) % 2][:]
                                .rearrange("p (k m) -> p k m", m=128),
                                wvs[p2][:, :, m2 * 128:(m2 + 1) * 128],
                                sem=wsm)
                            wmarks.append((wsm, wsm.n))
                        nc.tensor.wait_ge(wmarks[step][0].h, wmarks[step][1])
                        nc.tensor.wait_ge(zs.h, zmk)
                        for k in range(16):
                            _mm = nc.tensor.matmul(
                                f7t[:], s_mgT[:, k * 64 + g * 8:
                                              k * 64 + g * 8 + 8],
                                buf[:, k * 128:(k + 1) * 128],
                                start=(k == 0), stop=(k == 15))
                        ps.inc(_mm)
                        nc.vector.wait_ge(ps.h, ps.n)
                        zs.inc(nc.vector.tensor_copy(swpt[:], f7t[:]))
                        zmk = zs.n
                        for half in range(2):
                            pm = f7ms[half]
                            nc.tensor.wait_ge(es.h, emk[half])
                            for k in range(16):
                                for t2 in range(2):
                                    nc.tensor.matmul(
                                        pm[:, t2 * 512:(t2 + 1) * 512],
                                        buf[:, k * 128:(k + 1) * 128],
                                        s_xp1[:, k * 2048 + (half * 2 + t2)
                                              * 512:k * 2048 +
                                              (half * 2 + t2 + 1) * 512],
                                        start=(k == 0), stop=False)
                            if half == 0:
                                nc.tensor.wait_ge(zs.h, zs.n)
                            for t2 in range(2):
                                _mm = nc.tensor.matmul(
                                    pm[:, t2 * 512:(t2 + 1) * 512],
                                    swpt[:],
                                    s7s[:, (half * 2 + t2) * 512:
                                        (half * 2 + t2 + 1) * 512],
                                    start=False, stop=True)
                            ps.inc(_mm)
                            if proj == 0:
                                nc.scalar.wait_ge(ps.h, ps.n)
                                nc.scalar.wait_ge(cs.h, cmk[half])
                                es.inc(nc.scalar.activation(
                                    sgg[:, half * 1024:(half + 1) * 1024],
                                    pm[:], AF.Silu))
                                emk[half] = es.n
                            else:
                                nc.vector.wait_ge(ps.h, ps.n)
                                es.inc(nc.vector.tensor_mul(
                                    sgg[:, half * 1024:(half + 1) * 1024],
                                    sgg[:, half * 1024:(half + 1) * 1024],
                                    pm[:]))
                                emk[half] = es.n
                                nc.gpsimd.wait_ge(es.h, es.n)
                                cs.inc(nc.gpsimd.tensor_tensor(
                                    sab[:, m * 2048 + half * 1024:
                                        m * 2048 + (half + 1) * 1024],
                                    sgg[:, half * 1024:(half + 1) * 1024],
                                    sSgb[:, half * 1024:(half + 1) * 1024],
                                    OP.mult))
                                cmk[half] = cs.n
                        fwmk[step % 2] = ps.n
                    nc.tensor.wait_ge(cs.h, cs.n)
                    wv = d_dnT[g].rearrange("(k p) m -> p k m", p=128)
                    wmarks = []
                    if fwmk[0] is not None:
                        nc.gpsimd.wait_ge(ps.h, fwmk[0])
                    dma(nc.gpsimd,
                        fw0[:, 0:11 * 128].rearrange("p (k m) -> p k m", m=128),
                        wv[:, :, 0:128], sem=wsp[0])
                    wmarks.append((wsp[0], wsp[0].n))
                    def ld_y2p(g, m):
                        if g == 0 or m >= 16:
                            return
                        W7 = nc.sync.wait_ge
                        if (g, m - 2) in y2ev:
                            W7(es.h, y2ev[(g, m - 2)])
                        W7(srx.h, y2st[(g - 1, m)])
                        dma(nc.sync, sydps[m % 2][:],
                            d_y2[m * 128:(m + 1) * 128, :], sem=(sh0, sh1)[m % 2])
                        y2pl[(g, m)] = (sh0, sh1)[m % 2].n

                    ld_y2p(g, 0)
                    ld_y2p(g, 1)
                    for m in range(16):
                        buf = fws[m % 2]
                        syd = (syd0, syd1)[m % 2]
                        sydp = sydps[m % 2]
                        if m + 1 < 16:
                            wsm = wsp[(m + 1) % 2]
                            if fwmk[(m + 1) % 2] is not None:
                                nc.gpsimd.wait_ge(ps.h, fwmk[(m + 1) % 2])
                            dma(nc.gpsimd,
                                fws[(m + 1) % 2][:, 0:11 * 128]
                                .rearrange("p (k m) -> p k m", m=128),
                                wv[:, :, (m + 1) * 128:(m + 2) * 128],
                                sem=wsm)
                            wmarks.append((wsm, wsm.n))
                        nc.tensor.wait_ge(wmarks[m][0].h, wmarks[m][1])
                        for half in range(2):
                            pm = f7ms[half]
                            nc.tensor.wait_ge(es.h, emk[half])
                            for k in range(11):
                                for t2 in range(2):
                                    _mm = nc.tensor.matmul(
                                        pm[:, t2 * 512:(t2 + 1) * 512],
                                        buf[:, k * 128:(k + 1) * 128],
                                        sab[:, k * 2048 + (half * 2 + t2)
                                            * 512:k * 2048 +
                                            (half * 2 + t2 + 1) * 512],
                                        start=(k == 0), stop=(k == 10))
                            ps.inc(_mm)
                            if g == 0:
                                nc.scalar.wait_ge(ps.h, ps.n)
                                if (g, m - 2) in y2st:
                                    nc.scalar.wait_ge(srx.h,
                                                      y2st[(g, m - 2)])
                                es.inc(nc.scalar.copy(
                                    syd[:, half * 1024:(half + 1) * 1024],
                                    pm[:]))
                                emk[half] = es.n
                            else:
                                nc.vector.wait_ge(ps.h, ps.n)
                                if half == 0:
                                    nc.vector.wait_ge(
                                        (sh0, sh1)[m % 2].h, y2pl[(g, m)])
                                    if (g, m - 2) in y2st:
                                        nc.vector.wait_ge(
                                            srx.h, y2st[(g, m - 2)])
                                es.inc(nc.vector.tensor_add(
                                    syd[:, half * 1024:(half + 1) * 1024],
                                    pm[:],
                                    sydp[:, half * 1024:(half + 1) * 1024]))
                                emk[half] = es.n
                                if half == 1:
                                    y2ev[(g, m)] = es.n
                        fwmk[m % 2] = ps.n
                        nc.sync.wait_ge(es.h, es.n)
                        dma(nc.sync, d_y2[m * 128:(m + 1) * 128, :],
                            syd[:], sem=srx)
                        y2st[(g, m)] = srx.n
                        ld_y2p(g, m + 2)
                nc.sync.wait_ge(srx.h, srx.n)
                bar()
                dump("y2", d_y2.ap())
            _sc.__exit__(None, None, None)
        _p67.__exit__(None, None, None)

    if STAGE >= 8:
        _sc = nc.named_scope("P8"); _sc.__enter__()
        with MultiCtx(
            nc.sbuf_tensor("sxrA", [128, 2048], F32),
            nc.sbuf_tensor("sxrB", [128, 2048], F32),
            nc.sbuf_tensor("sy2A", [128, 2048], F32),
            nc.sbuf_tensor("sy2B", [128, 2048], F32),
            nc.sbuf_tensor("st2A", [128, 2048], F32),
            nc.sbuf_tensor("st2B", [128, 2048], F32),
            nc.sbuf_tensor("sxbA", [128, 2048], BF16),
            nc.sbuf_tensor("sxbB", [128, 2048], BF16),
            nc.sbuf_tensor("sxqA", [128, 2048], BF16),
            nc.sbuf_tensor("sxqB", [128, 2048], BF16),
            nc.sbuf_tensor("s_m1b", [128, 2048], F32),
            nc.sbuf_tensor("s_r1b", [128, 2048], F32),
            nc.sbuf_tensor("s_m2b", [128, 2048], F32),
            nc.sbuf_tensor("s_r2b", [128, 2048], F32),
            nc.sbuf_tensor("srow8", [1, 2 * 2048], F32),
        ) as (sxrA, sxrB, sy2A, sy2B, st2A, st2B, sxbA, sxbB, sxqA, sxqB,
              s_m1b, s_r1b, s_m2b, s_r2b, srow8):
            sxr_ = (sxrA, sxrB)
            sy2_ = (sy2A, sy2B)
            st2_ = (st2A, st2B)
            sxb_ = (sxbA, sxbB)
            sxq_ = (sxqA, sxqB)
            shq8 = (sh0, sh1)
            uqq8 = (uq0, uq1)
            xl8, yl8 = {}, {}
            m_ml8, m_ad8, m_cp8, m_sum8, m_st8 = {}, {}, {}, {}, {}

            def W8(eng, sem, mark):
                if mark is not None:
                    eng.wait_ge(sem.h, mark)

            def ld8(pp, m):
                p = m % 2
                W8(nc.sync, gsm, m_ad8.get((pp, m - 2)))
                dma(nc.sync, sxr_[p][:],
                    d_xres[m * 128:(m + 1) * 128, :], sem=shq8[p])
                xl8[(pp, m)] = shq8[p].n
                W8(nc.scalar, gsm, m_ad8.get((pp, m - 2)))
                dma(nc.scalar, sy2_[p][:],
                    d_y2[m * 128:(m + 1) * 128, :], sem=uqq8[p])
                yl8[(pp, m)] = uqq8[p].n

            dma(nc.sync, srow8[0:1, 0:2048], d_murs[0:1, :])
            dma(nc.sync, srow8[0:1, 2048:4096], d_murs[1:2, :])
            dwait()
            bar()
            with nc.psum_tensor("p8c", [128, 2048], F32) as p8c:
                for t in range(4):
                    nc.tensor.matmul(p8c[:, t * 512:(t + 1) * 512],
                                     s_ones1[:],
                                     srow8[0:1, t * 512:(t + 1) * 512],
                                     start=True, stop=True)
                bar()
                nc.scalar.copy(s_m1b[:], p8c[:])
                bar()
                for t in range(4):
                    nc.tensor.matmul(p8c[:, t * 512:(t + 1) * 512],
                                     s_ones1[:],
                                     srow8[0:1, 2048 + t * 512:
                                           2048 + (t + 1) * 512],
                                     start=True, stop=True)
                bar()
                nc.scalar.copy(s_r1b[:], p8c[:])
                bar()
            nc.vector.tensor_mul(s_m1b[:], s_m1b[:], s_r1b[:])
            bar()

            with MultiCtx(
                nc.psum_tensor("p8r1", [1, 2048], F32),
                nc.psum_tensor("p8r2", [1, 2048], F32),
            ) as (p8r1, p8r2):
                ld8(0, 0)
                ld8(0, 1)
                for m in range(16):
                    p = m % 2
                    W8(nc.vector, shq8[p], xl8[(0, m)])
                    nc.vector.tensor_mul(sxr_[p][:], sxr_[p][:], s_r1b[:])
                    cs.inc(nc.vector.tensor_sub(sxr_[p][:], sxr_[p][:],
                                                s_m1b[:]))
                    m_ml8[(0, m)] = cs.n
                    W8(nc.gpsimd, cs, m_ml8[(0, m)])
                    W8(nc.gpsimd, uqq8[p], yl8[(0, m)])
                    W8(nc.gpsimd, es, m_cp8.get((0, m - 2)))
                    gsm.inc(nc.gpsimd.tensor_tensor(
                        st2_[p][:], sxr_[p][:], sy2_[p][:], OP.add))
                    m_ad8[(0, m)] = gsm.n
                    W8(nc.scalar, gsm, m_ad8[(0, m)])
                    W8(nc.scalar, ps, m_sum8.get((0, m - 2)))
                    nc.scalar.activation(sxb_[p][:], st2_[p][:], AF.Copy)
                    es.inc(nc.scalar.activation(sxq_[p][:], st2_[p][:],
                                                AF.Square))
                    m_cp8[(0, m)] = es.n
                    if m + 2 < 16:
                        ld8(0, m + 2)
                    W8(nc.tensor, es, m_cp8[(0, m)])
                    for t in range(4):
                        nc.tensor.matmul(
                            p8r1[0:1, t * 512:(t + 1) * 512], s_onesb[:],
                            sxb_[p][:, t * 512:(t + 1) * 512],
                            start=(m == 0), stop=(m == 15))
                        _mm = nc.tensor.matmul(
                            p8r2[0:1, t * 512:(t + 1) * 512], s_onesb[:],
                            sxq_[p][:, t * 512:(t + 1) * 512],
                            start=(m == 0), stop=(m == 15))
                    ps.inc(_mm)
                    m_sum8[(0, m)] = ps.n
                nc.vector.wait_ge(ps.h, ps.n)
                nc.vector.tensor_copy(srow8[0:1, 0:2048], p8r1[0:1, :])
                bar()
                nc.vector.tensor_copy(srow8[0:1, 2048:4096], p8r2[0:1, :])
                bar()
            nc.vector.tensor_scalar_mul(srow8[0:1, 0:2048],
                                        srow8[0:1, 0:2048], 1.0 / DM)
            bar()
            nc.vector.tensor_scalar_mul(srow8[0:1, 2048:4096],
                                        srow8[0:1, 2048:4096], 1.0 / DM)
            bar()
            nc.vector.tensor_tensor(st2A[0:1, 0:2048], srow8[0:1, 0:2048],
                                    srow8[0:1, 0:2048], OP.mult)
            bar()
            nc.vector.tensor_sub(srow8[0:1, 2048:4096],
                                 srow8[0:1, 2048:4096], st2A[0:1, 0:2048])
            bar()
            nc.vector.tensor_scalar(srow8[0:1, 2048:4096],
                                    srow8[0:1, 2048:4096], EPS, None, OP.add)
            bar()
            nc.scalar.activation(srow8[0:1, 2048:4096],
                                 srow8[0:1, 2048:4096], AF.Sqrt)
            bar()
            nc.vector.reciprocal(srow8[0:1, 2048:4096],
                                 srow8[0:1, 2048:4096])
            bar()
            dump("r8", srow8[:])
            with nc.psum_tensor("p8b", [128, 2048], F32) as p8b:
                for t in range(4):
                    nc.tensor.matmul(p8b[:, t * 512:(t + 1) * 512],
                                     s_ones1[:],
                                     srow8[0:1, t * 512:(t + 1) * 512],
                                     start=True, stop=True)
                bar()
                nc.scalar.copy(s_m2b[:], p8b[:])
                bar()
                for t in range(4):
                    nc.tensor.matmul(p8b[:, t * 512:(t + 1) * 512],
                                     s_ones1[:],
                                     srow8[0:1, 2048 + t * 512:
                                           2048 + (t + 1) * 512],
                                     start=True, stop=True)
                bar()
                nc.scalar.copy(s_r2b[:], p8b[:])
                bar()
            nc.vector.tensor_add(s_m2b[:], s_m2b[:], s_m1b[:])
            bar()

            ld8(1, 0)
            ld8(1, 1)
            for m in range(16):
                p = m % 2
                W8(nc.vector, shq8[p], xl8[(1, m)])
                nc.vector.tensor_mul(sxr_[p][:], sxr_[p][:], s_r1b[:])
                cs.inc(nc.vector.tensor_sub(sxr_[p][:], sxr_[p][:],
                                            s_m2b[:]))
                m_ml8[(1, m)] = cs.n
                W8(nc.gpsimd, cs, m_ml8[(1, m)])
                W8(nc.gpsimd, uqq8[p], yl8[(1, m)])
                W8(nc.gpsimd, ss, m_st8.get((1, m - 2)))
                gsm.inc(nc.gpsimd.tensor_tensor(
                    st2_[p][:], sxr_[p][:], sy2_[p][:], OP.add))
                m_ad8[(1, m)] = gsm.n
                W8(nc.vector, gsm, m_ad8[(1, m)])
                cs.inc(nc.vector.tensor_mul(st2_[p][:], st2_[p][:],
                                            s_r2b[:]))
                m_ml8[(1, m)] = cs.n
                W8(nc.sync, cs, m_ml8[(1, m)])
                dma(nc.sync, d_out[m * 128:(m + 1) * 128, :], st2_[p][:],
                    sem=ss)
                m_st8[(1, m)] = ss.n
                if m + 2 < 16:
                    ld8(1, m + 2)
            nc.sync.wait_ge(ss.h, ss.n)
            bar()
        _sc.__exit__(None, None, None)
    if STAGE < 8:
        with nc.sbuf_tensor("sfin", [128, TOK], F32) as sfin:
            nc.vector.memset(sfin[:], 0.0)
            bar()
            for m in range(16):
                dma(nc.sync, d_out[m * 128:(m + 1) * 128, :], sfin[:],
                    sem=ss)
            nc.sync.wait_ge(ss.h, ss.n)
            bar()
    nc.sync.wait_ge(ds.h, ds.n)
    nc.sync.wait_ge(ss.h, ss.n)
    bar()
    return nc



def _prep(inputs):
    ident = np.eye(128, dtype=ml_dtypes.bfloat16)
    i64 = np.eye(64, dtype=np.float32)
    i128 = np.eye(128, dtype=np.float32)
    onesb = np.ones((128, 1), dtype=ml_dtypes.bfloat16)
    ones1 = np.ones((1, 128), dtype=np.float32)
    W_inT = np.ascontiguousarray(np.asarray(inputs["L0_attn_W_in"]).T)
    woutT = np.ascontiguousarray(np.asarray(inputs["L0_attn_W_out"]).T)
    gatewT = np.ascontiguousarray(np.asarray(inputs["L0_ffn_gate_w"]).T)
    mupT = np.ascontiguousarray(
        np.asarray(inputs["L0_ffn_mask_up_w"]).transpose(1, 0, 2)
        .reshape(DM, NG * 8))
    gpT = np.ascontiguousarray(np.asarray(inputs["L0_ffn_gate_proj_w"])
                               .transpose(0, 2, 1))
    upT = np.ascontiguousarray(np.asarray(inputs["L0_ffn_up_proj_w"])
                               .transpose(0, 2, 1))
    dnT = np.ascontiguousarray(np.asarray(inputs["L0_ffn_down_proj_w"])
                               .transpose(0, 2, 1))
    drep = np.ascontiguousarray(np.repeat(np.asarray(inputs["L0_attn_D"]),
                                          PH))
    invmap = np.ascontiguousarray(
        np.asarray(inputs["L0_ffn_inv_map"]).reshape(-1).astype(np.int32))
    u = np.asarray(inputs["L0_attn_u"])
    shared = dict(
        winT=W_inT, convw=np.ascontiguousarray(inputs["L0_attn_conv_w"]),
        convb=np.ascontiguousarray(inputs["L0_attn_conv_b"]),
        dtbias=np.ascontiguousarray(inputs["L0_attn_dt_bias"]),
        alog=np.ascontiguousarray(inputs["L0_attn_A_log"]),
        drep=drep, nw=np.ascontiguousarray(inputs["L0_attn_norm_w"]),
        woutT=woutT, gatewT=gatewT, invmap=invmap, mupT=mupT,
        gpT=gpT, upT=upT, dnT=dnT, ibf=ident, i64=i64, i128=i128,
        onesb=onesb, ones1=ones1)
    in_maps = []
    for core in range(8):
        b, qq = core // 4, core % 4
        t0 = qq * TOK
        uslice = np.zeros((TOKH, DM), dtype=np.float32)
        lo = t0 - 3
        src = u[b, max(lo, 0):t0 + TOK]
        uslice[TOKH - src.shape[0]:] = src
        uT = np.ascontiguousarray(uslice.T)
        mq = np.zeros((5, 4), dtype=np.float32)
        mq[4, :] = NEG
        for pq in range(4):
            if pq < qq:
                mq[4, pq] = 0.0
                for rq in range(4):
                    if pq < rq < qq:
                        mq[rq, pq] = 1.0
        m = dict(shared)
        m["uT"] = uT
        m["mq"] = mq
        in_maps.append(m)
    return in_maps


def kernel(**inputs):
    in_maps = _prep(inputs)
    nc = build_nc()
    res = run_bass_kernel_spmd(nc, in_maps, core_ids=list(range(8)))
    out = np.zeros((2, 8192, DM), dtype=np.float32)
    for core in range(8):
        b, qq = core // 4, core % 4
        out[b, qq * TOK:(qq + 1) * TOK] = res.results[core]["out"].T
    return out


# version 05
# speedup vs baseline: 1.4182x; 1.0570x over previous
import numpy as np
import ml_dtypes
import concourse.bass as bass
import concourse.mybir as mybir
from concourse.bass_utils import run_bass_kernel_spmd

F32 = mybir.dt.float32
BF16 = mybir.dt.bfloat16
I32 = mybir.dt.int32
OP = mybir.AluOpType
AF = mybir.ActivationFunctionType
AX = mybir.AxisListType

DM = 2048
DIN = 4096
DST = 128
H = 64
PH = 64
CONVD = 4352
DIP = 8512
TOK = 2048
TOKH = 2051
CH = 128
NCH = 16
IF = 1408
NG = 8
E = 64
NEG = -1.0e30
EPS = 1e-5
NM_Z = 32
NM_XBC = 34
M_DT = 66
NM = 67

STAGE = 99
P1CAP = 999


class MultiCtx:

    def __init__(self, *cms):
        self.cms = cms

    def __enter__(self):
        from contextlib import ExitStack
        self.st = ExitStack()
        return tuple(self.st.enter_context(c) for c in self.cms)

    def __exit__(self, *a):
        return self.st.__exit__(*a)


class Sem:
    def __init__(self, nc, name):
        self.h = nc.alloc_semaphore(name)
        self.n = 0

    def inc(self, instr, k=1):
        instr.then_inc(self.h, k)
        self.n += k
        return instr


def build_nc(dbg=()):
    nc = bass.Bass("TRN2", target_bir_lowering=False,
                   detect_race_conditions=False)

    d_uT = nc.dram_tensor("uT", [DM, TOKH], F32, kind="ExternalInput")
    d_winT = nc.dram_tensor("winT", [DM, DIP], F32, kind="ExternalInput")
    d_convw = nc.dram_tensor("convw", [CONVD, 4], F32, kind="ExternalInput")
    d_convb = nc.dram_tensor("convb", [CONVD], F32, kind="ExternalInput")
    d_dtbias = nc.dram_tensor("dtbias", [H], F32, kind="ExternalInput")
    d_alog = nc.dram_tensor("alog", [H], F32, kind="ExternalInput")
    d_drep = nc.dram_tensor("drep", [DIN], F32, kind="ExternalInput")
    d_nw = nc.dram_tensor("nw", [DIN], F32, kind="ExternalInput")
    d_woutT = nc.dram_tensor("woutT", [DIN, DM], F32, kind="ExternalInput")
    d_gatewT = nc.dram_tensor("gatewT", [DM, E], F32, kind="ExternalInput")
    d_invmap = nc.dram_tensor("invmap", [E], I32, kind="ExternalInput")
    d_mupT = nc.dram_tensor("mupT", [DM, NG * 8], F32, kind="ExternalInput")
    d_gpT = nc.dram_tensor("gpT", [NG, DM, IF], F32, kind="ExternalInput")
    d_upT = nc.dram_tensor("upT", [NG, DM, IF], F32, kind="ExternalInput")
    d_dnT = nc.dram_tensor("dnT", [NG, IF, DM], F32, kind="ExternalInput")
    d_mq = nc.dram_tensor("mq", [5, 4], F32, kind="ExternalInput")
    d_ibf = nc.dram_tensor("ibf", [128, 128], BF16, kind="ExternalInput")
    d_i64 = nc.dram_tensor("i64", [64, 64], F32, kind="ExternalInput")
    d_i128 = nc.dram_tensor("i128", [128, 128], F32, kind="ExternalInput")
    d_onesb = nc.dram_tensor("onesb", [128, 1], BF16, kind="ExternalInput")
    d_ones1 = nc.dram_tensor("ones1", [1, 128], F32, kind="ExternalInput")

    d_out = nc.dram_tensor("out", [DM, TOK], F32, kind="ExternalOutput")

    d_zsc = nc.dram_tensor("zsc", [DIN, TOK], F32, kind="Internal")
    d_xsc = nc.dram_tensor("xsc", [NCH, DIN, CH], BF16, kind="Internal")
    d_xdsc = nc.dram_tensor("xdsc", [NCH, DIN, CH], BF16, kind="Internal")
    d_scsc = nc.dram_tensor("scsc", [NCH, 128, DIN], F32, kind="Internal")
    d_acrow = nc.dram_tensor("acrow", [NCH, H * CH], F32, kind="Internal")
    d_dcrow = nc.dram_tensor("dcrow", [NCH, H], F32, kind="Internal")
    d_wrow = nc.dram_tensor("wrow", [4, H], F32, kind="Internal")
    d_xres = nc.dram_tensor("xres", [DM, TOK], F32, kind="Internal")
    d_y2 = nc.dram_tensor("y2", [DM, TOK], F32, kind="Internal")
    d_woutbf = nc.dram_tensor("woutbf", [DIN, DM], BF16, kind="Internal")
    d_agin = nc.dram_tensor("agin", [128 * DIN + 64], F32, kind="Internal")
    d_agout = nc.dram_tensor("agout", [4, 128 * DIN + 64], F32,
                             kind="Internal")
    d_dtsc = nc.dram_tensor("dtsc", [2, H, TOK], F32, kind="Internal")
    d_lnrow = nc.dram_tensor("lnrow", [2, TOK], F32, kind="Internal")
    d_murs = nc.dram_tensor("murs", [2, TOK], F32, kind="Internal")
    d_soft8 = nc.dram_tensor("soft8", [8, NG, TOK], BF16, kind="Internal")
    d_scrow = nc.dram_tensor("scrow", [NG, TOK], F32, kind="Internal")

    dbg_spec = dict(
        z=([DIN, TOK], F32), x=([NCH, DIN, CH], BF16), dt=([H, TOK], F32),
        acum=([H, TOK], F32), sc=([NCH, 128, DIN], F32),
        slocal=([128, DIN], F32), carry=([128, DIN], F32),
        yfm=([128, DIN], F32), ynorm=([128, 32 * 512], BF16),
        xres=([DM, TOK], F32), xp1=([128, 16 * 2048], BF16),
        scoretm=([128, 16 * 64], F32), soft8=([8, NG, TOK], BF16),
        scrow=([NG, TOK], F32), y2=([DM, TOK], F32),
        wt=([128, 16 * CH], BF16), xtm=([128, DIN], BF16),
        r8=([1, 2 * 2048], F32),
        yfm5=([128, DIN], F32), woutbf=([DIN, DM], BF16),
        prev5=([128, DIN], BF16), a0=([128, 11 * 2048], BF16),
        gg0=([128, 2048], F32), wpt0=([8, 11 * 128], BF16),
        sgb0=([128, 2048], F32),
    )
    d_dbg = {}
    for name in dbg:
        shape, dt_ = dbg_spec[name]
        d_dbg[name] = nc.dram_tensor("dbg_" + name, shape, dt_,
                                     kind="ExternalOutput")

    ds = Sem(nc, "ds")
    ws = Sem(nc, "ws")
    ps = Sem(nc, "ps")
    es = Sem(nc, "es")
    cs = Sem(nc, "cs")
    zs = Sem(nc, "zs")
    ss = Sem(nc, "ss")
    cc = Sem(nc, "cc")
    ya = Sem(nc, "ya")
    su = Sem(nc, "su")
    ws1 = Sem(nc, "ws1")
    sz0 = Sem(nc, "sz0")
    sz1 = Sem(nc, "sz1")
    sx0 = Sem(nc, "sx0")
    sx1 = Sem(nc, "sx1")
    srx = Sem(nc, "srx")
    sh0 = Sem(nc, "sh0")
    sh1 = Sem(nc, "sh1")
    uq0 = Sem(nc, "uq0")
    uq1 = Sem(nc, "uq1")
    gsm = Sem(nc, "gsm")
    sxc = Sem(nc, "sxc")
    wsp = (ws, ws1)

    def bar():
        nc.all_engine_barrier()

    def dma(q, out, in_, sem=ds, **kw):
        sem.inc(q.dma_start(out=out, in_=in_, **kw), 16)

    def dwait(eng=None):
        (eng or nc.sync).wait_ge(ds.h, ds.n)

    def dump(name, src):
        if name in d_dbg:
            dma(nc.sync, d_dbg[name].ap(), src)
            dwait()
            bar()

    sb = nc.alloc_sbuf_tensor
    s_convw = sb("s_convw", [128, NM_XBC * 4], F32)
    s_convb = sb("s_convb", [128, NM_XBC], F32)
    s_dtbias = sb("s_dtbias", [H, 1], F32)
    s_acol = sb("s_acol", [H, 1], F32)
    s_drep = sb("s_drep", [128, 32], F32)
    s_nw = sb("s_nw", [128, 32], F32)
    s_ibf = sb("s_ibf", [128, 128], BF16)
    s_i64 = sb("s_i64", [64, 64], F32)
    s_i128 = sb("s_i128", [128, 128], F32)
    s_onesb = sb("s_onesb", [128, 1], BF16)
    s_ones1 = sb("s_ones1", [1, 128], F32)
    s_mq = sb("s_mq", [5, 4], F32)
    s_atote = sb("s_atote", [5, H], F32)
    s_mgT = sb("s_mgT", [128, 16 * NG * 8], BF16)
    s_gateT = sb("s_gateT", [128, 16 * E], BF16)
    s_PT = sb("s_PT", [E, E], F32)
    s_tmp64 = sb("s_tmp64", [H, CH], F32)
    s_tmp64b = sb("s_tmp64b", [H, CH], F32)

    q = nc.sync
    dma(q, s_convw[:].rearrange("p (m t) -> p m t", t=4),
        d_convw.ap().rearrange("(m p) t -> p m t", p=128))
    dma(q, s_convb[:], d_convb.ap().rearrange("(m p) -> p m", p=128),
        allow_slow_non_contiguous=True)
    dma(q, s_dtbias[:], d_dtbias.ap().rearrange("(h o) -> h o", o=1))
    dma(q, s_acol[:], d_alog.ap().rearrange("(h o) -> h o", o=1))
    dma(q, s_drep[:], d_drep.ap().rearrange("(m p) -> p m", p=128),
        allow_slow_non_contiguous=True)
    dma(q, s_nw[:], d_nw.ap().rearrange("(m p) -> p m", p=128),
        allow_slow_non_contiguous=True)
    dma(q, s_ibf[:], d_ibf.ap())
    dma(q, s_i64[:], d_i64.ap())
    dma(q, s_i128[:], d_i128.ap())
    dma(q, s_onesb[:], d_onesb.ap())
    dma(q, s_ones1[:], d_ones1.ap())
    dma(q, s_mq[:], d_mq.ap())
    dma(nc.gpsimd, s_mgT[:].rearrange("p (k e) -> p k e", e=NG * 8),
        d_mupT.ap().rearrange("(k p) e -> p k e", p=128), sem=ws)
    dma(nc.gpsimd, s_gateT[:].rearrange("p (k e) -> p k e", e=E),
        d_gatewT.ap().rearrange("(k p) e -> p k e", p=128), sem=ws)
    with MultiCtx(
        nc.sbuf_tensor("s_invi", [E, 1], I32),
        nc.sbuf_tensor("s_invf", [E, 1], F32),
        nc.sbuf_tensor("s_iotac", [E, 1], F32),
        nc.sbuf_tensor("s_row0", [1, E], F32),
        nc.psum_tensor("pp0", [64, 64], F32),
    ) as (s_invi, s_invf, s_iotac, s_row0, pp0):
        dma(q, s_invi[:], d_invmap.ap().rearrange("(e o) -> e o", o=1))
        dwait()
        bar()
        nc.vector.tensor_copy(s_invf[:], s_invi[:])
        nc.gpsimd.iota(s_iotac[:], [[1, 1]], base=0, channel_multiplier=1,
                       allow_small_or_imprecise_dtypes=True)
        bar()
        nc.tensor.matmul(pp0[0:1, 0:64], s_invf[:], s_i64[:], start=True,
                         stop=True)
        bar()
        nc.vector.tensor_copy(s_row0[:], pp0[0:1, 0:64])
        bar()
        nc.tensor.matmul(pp0[0:64, 0:64], s_ones1[0:1, 0:64], s_row0[:],
                         start=True, stop=True)
        bar()
        nc.vector.tensor_scalar(s_PT[:], pp0[0:64, 0:64], s_iotac[:], None,
                                OP.is_equal)
        bar()

    with MultiCtx(
        nc.sbuf_tensor("s_Bbf", [128, TOK], BF16),
        nc.sbuf_tensor("s_Cbf", [128, TOK], BF16),
        nc.sbuf_tensor("s_dcb", [128, NCH * H], F32),
        nc.sbuf_tensor("s_colTa", [128, NCH * H], F32),
        nc.sbuf_tensor("s_evTa", [128, NCH * H], F32),
    ) as (s_Bbf, s_Cbf, s_dcb, s_colTa, s_evTa):
        _sR = MultiCtx(nc.sbuf_tensor("s_R", [128, DIN], F32))
        (s_R,) = _sR.__enter__()
        _colt = MultiCtx(
            nc.sbuf_tensor("s_colt", [H, TOK], F32),
            nc.sbuf_tensor("s_acum", [H, TOK], F32),
            nc.sbuf_tensor("s_asum", [H, NCH], F32),
        )
        s_colt, s_acum, s_asum = _colt.__enter__()
        if STAGE >= 1:
            _sc = nc.named_scope("P1"); _sc.__enter__()
            with MultiCtx(
                nc.sbuf_tensor("suT", [128, 16 * TOKH], BF16),
                nc.sbuf_tensor("wb0", [128, 16 * 128], BF16),
                nc.sbuf_tensor("wb1", [128, 16 * 128], BF16),
                nc.sbuf_tensor("xbc0", [128, TOKH], F32),
                nc.sbuf_tensor("cv0", [128, TOK], F32),
                nc.sbuf_tensor("xr0", [128, TOK], BF16),
                nc.sbuf_tensor("xr1", [128, TOK], BF16),
                nc.sbuf_tensor("xd0", [128, TOK], BF16),
                nc.sbuf_tensor("xd1", [128, TOK], BF16),
                nc.sbuf_tensor("zb0", [128, TOK], F32),
                nc.sbuf_tensor("zb1", [128, TOK], F32),
                nc.sbuf_tensor("sdtev", [H, TOK], F32),
                nc.psum_tensor("pA", [128, 2048], F32),
                nc.psum_tensor("pH", [128, 8], F32),
            ) as (suT, wb0, wb1, xbc0, cv0, xr0, xr1, xd0, xd1, zb0, zb1,
                  sdtev, pA, pH):
                for k in range(16):
                    dma(nc.gpsimd, suT[:, k * TOKH:(k + 1) * TOKH],
                        d_uT.ap().rearrange("(k p) t -> k p t", p=128)[k],
                        sem=su)
                wview = d_winT.ap().rearrange("(k p) m -> p k m", p=128)

                def wload(mi, buf):
                    wdt = min(128, DIP - mi * 128)
                    sm = wsp[mi % 2]
                    dma(nc.gpsimd,
                        buf[:].rearrange("p (k m) -> p k m", m=128)
                        [:, :, 0:wdt],
                        wview[:, :, mi * 128:mi * 128 + wdt], sem=sm)
                    return (sm, sm.n)

                wmk = []
                wmk.append(wload(0, wb0))
                wbufs = (wb0, wb1)
                xrs, xds, zbs = (xr0, xr1), (xd0, xd1), (zb0, zb1)
                guard = {}

                for mi in range(min(NM, P1CAP)):
                    par = mi % 2
                    wb = wbufs[par]
                    rows = 64 if mi == M_DT else 128
                    if mi + 1 < NM:
                        nc.gpsimd.wait_ge(ps.h, mi)
                        wmk.append(wload(mi + 1, wbufs[(mi + 1) % 2]))
                    if mi == 0:
                        nc.tensor.wait_ge(su.h, su.n)
                    nc.tensor.wait_ge(wmk[mi][0].h, wmk[mi][1])
                    nc.tensor.wait_ge(es.h, mi)
                    is_xbc = NM_Z <= mi < NM_Z + NM_XBC
                    for t in range(4):
                        for k in range(16):
                            _mm = nc.tensor.matmul(
                                pA[0:rows, t * 512:(t + 1) * 512],
                                wb[:, k * 128:k * 128 + rows],
                                suT[:, k * TOKH + 3 + t * 512:
                                    k * TOKH + 3 + (t + 1) * 512],
                                start=(k == 0), stop=(k == 15))
                    if is_xbc:
                        for k in range(16):
                            _mm = nc.tensor.matmul(
                                pH[:, 0:3], wb[:, k * 128:(k + 1) * 128],
                                suT[:, k * TOKH:k * TOKH + 3],
                                start=(k == 0), stop=(k == 15))
                    ps.inc(_mm)

                    if mi < NM_Z:
                        zb = zbs[par]
                        szp = (sz0, sz1)[par]
                        nc.scalar.wait_ge(ps.h, mi + 1)
                        nc.scalar.wait_ge(szp.h, szp.n)
                        es.inc(nc.scalar.activation(zb[:], pA[:], AF.Silu))
                        nc.sync.wait_ge(es.h, es.n)
                        dma(nc.sync, d_zsc[mi * 128:(mi + 1) * 128, :], zb[:],
                            sem=szp)
                    elif is_xbc:
                        ci = mi - NM_Z
                        nc.vector.wait_ge(ps.h, mi + 1)
                        nc.vector.tensor_copy(xbc0[:, 0:3], pH[:, 0:3])
                        es.inc(nc.vector.tensor_copy(xbc0[:, 3:2051], pA[:]))
                        nc.vector.tensor_scalar(
                            cv0[:], xbc0[:, 0:2048],
                            s_convw[:, ci * 4:ci * 4 + 1],
                            s_convb[:, ci:ci + 1], OP.mult, OP.add)
                        for w in range(1, 4):
                            _cv = nc.vector.scalar_tensor_tensor(
                                cv0[:], xbc0[:, w:w + 2048],
                                s_convw[:, ci * 4 + w:ci * 4 + w + 1], cv0[:],
                                OP.mult, OP.add)
                        cs.inc(_cv)
                        nc.scalar.wait_ge(cs.h, cs.n)
                        if ci < 32:
                            xrb, xdb = xrs[par], xds[par]
                            sxq = (sx0, sx1)[par]
                            nc.scalar.wait_ge(sxq.h, sxq.n)
                            zs.inc(nc.scalar.activation(xrb[:], cv0[:],
                                                         AF.Silu))
                            nc.vector.wait_ge(zs.h, zs.n)
                            zs.inc(nc.vector.tensor_scalar_mul(
                                xdb[:], xrb[:], s_drep[:, ci:ci + 1]))
                            nc.sync.wait_ge(zs.h, zs.n)
                            nc.scalar.wait_ge(zs.h, zs.n)
                            dma(nc.sync,
                                d_xsc[:, ci * 128:(ci + 1) * 128, :]
                                .transpose([1, 0, 2]),
                                xrb[:].rearrange("p (c t) -> p c t", t=128),
                                sem=sxq)
                            dma(nc.scalar,
                                d_xdsc[:, ci * 128:(ci + 1) * 128, :]
                                .transpose([1, 0, 2]),
                                xdb[:].rearrange("p (c t) -> p c t", t=128),
                                sem=sxq)
                        elif ci == 32:
                            zs.inc(nc.scalar.activation(s_Bbf[:], cv0[:],
                                                        AF.Silu))
                        else:
                            zs.inc(nc.scalar.activation(s_Cbf[:], cv0[:],
                                                        AF.Silu))
                    else:
                        nc.vector.wait_ge(ps.h, mi + 1)
                        es.inc(nc.vector.tensor_copy(sdtev[:], pA[0:64, :]))
                for smx in (sz0, sz1, sx0, sx1, ss):
                    nc.sync.wait_ge(smx.h, smx.n)
                bar()
                if P1CAP >= NM:
                    dma(nc.sync, d_dtsc[0], sdtev[:])
                    dwait()
                bar()
            _sc.__exit__(None, None, None)

        if STAGE >= 2:
            _sc = nc.named_scope("P2"); _sc.__enter__()
            with MultiCtx(
                nc.sbuf_tensor("p2raw", [H, TOK], F32),
                nc.sbuf_tensor("p2dt", [H, TOK], F32),
            ) as (p2raw, p2dt):
                dma(nc.sync, p2raw[:], d_dtsc[0])
                dwait()
                bar()
                nc.scalar.activation(p2dt[:], p2raw[:], AF.Abs,
                                     bias=s_dtbias[:])
                bar()
                nc.scalar.activation(p2dt[:], p2dt[:], AF.Exp, scale=-1.0)
                bar()
                nc.scalar.activation(p2dt[:], p2dt[:], AF.Ln, bias=1.0)
                nc.vector.tensor_scalar(p2raw[:], p2raw[:], s_dtbias[:], 0.0,
                                        OP.add, OP.max)
                bar()
                nc.vector.tensor_add(p2dt[:], p2dt[:], p2raw[:])
                bar()
                nc.scalar.activation(s_colt[:], p2dt[:], AF.Ln)
                nc.scalar.activation(s_acol[:], s_acol[:], AF.Exp)
                bar()
                dma(nc.sync, d_dtsc[1], p2dt[:])
                nc.vector.tensor_scalar_mul(s_acol[:], s_acol[:], -1.0)
                bar()
                nc.vector.tensor_scalar_mul(p2raw[:], p2dt[:], s_acol[:])
                bar()
                for c in range(NCH):
                    nc.vector.tensor_tensor_scan(
                        s_acum[:, c * CH:(c + 1) * CH],
                        p2raw[:, c * CH:(c + 1) * CH],
                        p2raw[:, c * CH:(c + 1) * CH], 0.0, OP.add, OP.bypass)
                bar()
                nc.vector.tensor_sub(s_colt[:], s_colt[:], s_acum[:])
                nc.vector.tensor_copy(
                    s_asum[:],
                    s_acum[:].rearrange("h (c t) -> h c t", t=CH)
                    [:, :, CH - 1])
                bar()
                nc.scalar.activation(s_tmp64[:, 0:NCH], s_asum[:], AF.Exp)
                bar()
                with nc.psum_tensor("pdc", [64, 64], F32) as pdc:
                    nc.tensor.matmul(pdc[0:NCH, 0:64], s_tmp64[:, 0:NCH],
                                     s_i64[:], start=True, stop=True)
                    bar()
                    nc.vector.tensor_copy(s_tmp64b[0:NCH, 0:64],
                                          pdc[0:NCH, 0:64])
                    bar()
                dma(nc.sync, d_dcrow.ap(), s_tmp64b[0:NCH, 0:64])
                for c in range(NCH):
                    dma(nc.sync, d_acrow[c].rearrange("(h l) -> h l", h=H),
                        s_acum[:, c * CH:(c + 1) * CH])
                dwait()
                bar()
                for c in range(NCH):
                    dma(nc.sync, s_dcb[:, c * H:(c + 1) * H],
                        d_dcrow[c:c + 1, :].broadcast_to([128, H]))
                dwait()
                bar()
                with nc.psum_tensor("p2t", [128, 128], F32) as p2t:
                    for c in range(NCH):
                        nc.tensor.matmul(p2t[:, 0:64],
                                         s_colt[:, c * CH:(c + 1) * CH],
                                         s_i64[:], start=True, stop=True)
                        bar()
                        nc.scalar.copy(s_colTa[:, c * H:(c + 1) * H],
                                       p2t[:, 0:64])
                        bar()
                        nc.scalar.activation(s_tmp64[:],
                                             s_acum[:, c * CH:(c + 1) * CH],
                                             AF.Exp)
                        bar()
                        nc.tensor.matmul(p2t[:, 0:64], s_tmp64[:], s_i64[:],
                                         start=True, stop=True)
                        bar()
                        nc.scalar.copy(s_evTa[:, c * H:(c + 1) * H],
                                       p2t[:, 0:64])
                        bar()
                dump("dt", p2dt[:])
            dump("acum", s_acum[:])
            dump("z", d_zsc.ap())
            dump("x", d_xsc.ap())
            _sc.__exit__(None, None, None)

        if STAGE >= 3:
            _sc = nc.named_scope("P3"); _sc.__enter__()
            with MultiCtx(
                nc.sbuf_tensor("xtm0", [128, DIN], BF16),
                nc.sbuf_tensor("xtm1", [128, DIN], BF16),
                nc.sbuf_tensor("sxd", [128, DIN], BF16),
                nc.sbuf_tensor("sBt", [128, 128], BF16),
                nc.sbuf_tensor("sdtd", [128, 64], BF16),
                nc.sbuf_tensor("sSc", [128, DIN], F32),
                nc.sbuf_tensor("p3dt", [H, TOK], F32),
                nc.psum_tensor("p3a", [128, 2048], F32),
                nc.psum_tensor("p3s", [128, 128], F32),
                nc.psum_tensor("p3w", [128, 64], F32),
            ) as (xtm0, xtm1, sxd, sBt, sdtd, sSc, p3dt, p3a, p3s, p3w):
                dma(nc.sync, p3dt[:], d_dtsc[1])
                nc.vector.memset(s_R[:], 0.0)
                xtms = (xtm0, xtm1)
                dma(nc.sync, xtm0[:], d_xsc[0], transpose=True)
                dwait()
                bar()
                dump("xtm", xtm0[:])
                for c in range(NCH):
                    if c + 1 < NCH:
                        dma(nc.sync, xtms[(c + 1) % 2][:], d_xsc[c + 1],
                            transpose=True)
                    xtm = xtms[c % 2]
                    nc.tensor.matmul(p3s[:], s_Bbf[:, c * CH:(c + 1) * CH],
                                     s_ibf[:], start=True, stop=True)
                    nc.scalar.activation(
                        s_tmp64[:], s_acum[:, c * CH:(c + 1) * CH], AF.Exp,
                        scale=-1.0,
                        bias=s_acum[:, c * CH + CH - 1:c * CH + CH])
                    bar()
                    nc.vector.tensor_mul(s_tmp64[:], s_tmp64[:],
                                         p3dt[:, c * CH:(c + 1) * CH])
                    nc.scalar.copy(sBt[:], p3s[:])
                    bar()
                    nc.tensor.matmul(p3w[:], s_tmp64[:], s_i64[:], start=True,
                                     stop=True)
                    bar()
                    nc.vector.tensor_copy(sdtd[:], p3w[:])
                    bar()
                    nc.vector.tensor_tensor(
                        sxd[:].rearrange("s (h p) -> s h p", p=PH),
                        xtm[:].rearrange("s (h p) -> s h p", p=PH),
                        sdtd[:].rearrange("s (h o) -> s h o", o=1)
                        .broadcast_to([128, H, PH]), OP.mult)
                    bar()
                    for half in range(2):
                        for j in range(4):
                            nc.tensor.matmul(
                                p3a[:, j * 512:(j + 1) * 512], sBt[:],
                                sxd[:, half * 2048 + j * 512:
                                    half * 2048 + (j + 1) * 512],
                                start=True, stop=True)
                        bar()
                        nc.scalar.copy(sSc[:, half * 2048:(half + 1) * 2048],
                                       p3a[:])
                        bar()
                    dma(nc.sync, d_scsc[c], sSc[:], sem=ss)
                    nc.vector.tensor_tensor(
                        s_R[:].rearrange("n (h p) -> n h p", p=PH),
                        s_R[:].rearrange("n (h p) -> n h p", p=PH),
                        s_dcb[:, c * H:(c + 1) * H]
                        .rearrange("n (h o) -> n h o", o=1)
                        .broadcast_to([128, H, PH]), OP.mult)
                    bar()
                    nc.vector.tensor_add(s_R[:], s_R[:], sSc[:])
                    bar()
                nc.vector.reduce_sum(s_tmp64[:, 0:1], s_asum[:], AX.X)
                nc.sync.wait_ge(ss.h, ss.n)
                bar()
                dump("sc", d_scsc.ap())
                dump("slocal", s_R[:])
                dma(nc.sync,
                    d_agin.ap()[0:128 * DIN].rearrange("(p f) -> p f", p=128),
                    s_R[:])
                dma(nc.sync,
                    d_agin.ap()[128 * DIN:].rearrange("(h o) -> h o", o=1),
                    s_tmp64[:, 0:1])
                dwait()
                bar()
                _sc.__exit__(None, None, None)
                _sc = nc.named_scope("AG"); _sc.__enter__()
                cc.inc(nc.gpsimd.collective_compute(
                    "AllGather", OP.bypass,
                    replica_groups=[[0, 1, 2, 3], [4, 5, 6, 7]],
                    ins=[d_agin.ap()], outs=[d_agout.ap()]), 1)
                nc.gpsimd.wait_ge(cc.h, cc.n)
                bar()
                _sc.__exit__(None, None, None)

        _colt.__exit__(None, None, None)

        if STAGE >= 4:
            _sc = nc.named_scope("P4carry"); _sc.__enter__()
            with MultiCtx(
                nc.sbuf_tensor("sSg", [128, DIN], F32),
                nc.sbuf_tensor("sWb", [128, 4 * 64], F32),
                nc.psum_tensor("p4w", [64, 64], F32),
            ) as (sSg, sWb, p4w):
                nc.vector.memset(s_atote[:], 1.0)
                bar()
                for p in range(4):
                    dma(nc.sync, s_atote[p:p + 1, :],
                        d_agout[p, 128 * DIN:].rearrange("(o h) -> o h", o=1))
                dwait()
                bar()
                nc.tensor.matmul(p4w[0:64, 0:4], s_atote[:], s_mq.ap(),
                                 start=True, stop=True)
                bar()
                nc.scalar.activation(s_tmp64[:, 0:4], p4w[0:64, 0:4], AF.Exp)
                bar()
                nc.tensor.matmul(p4w[0:4, 0:64], s_tmp64[:, 0:4], s_i64[:],
                                 start=True, stop=True)
                bar()
                nc.vector.tensor_copy(s_tmp64b[0:4, 0:64], p4w[0:4, 0:64])
                bar()
                dma(nc.sync, d_wrow.ap(), s_tmp64b[0:4, 0:64])
                dwait()
                bar()
                for p in range(4):
                    dma(nc.sync, sWb[:, p * 64:(p + 1) * 64],
                        d_wrow[p:p + 1, :].broadcast_to([128, H]))
                dwait()
                bar()
                nc.vector.memset(s_R[:], 0.0)
                bar()
                for p in range(4):
                    dma(nc.sync, sSg[:],
                        d_agout[p, 0:128 * DIN]
                        .rearrange("(n f) -> n f", n=128))
                    dwait()
                    bar()
                    nc.vector.tensor_tensor(
                        sSg[:].rearrange("n (h p) -> n h p", p=PH),
                        sSg[:].rearrange("n (h p) -> n h p", p=PH),
                        sWb[:, p * 64:(p + 1) * 64]
                        .rearrange("n (h o) -> n h o", o=1)
                        .broadcast_to([128, H, PH]), OP.mult)
                    bar()
                    nc.vector.tensor_add(s_R[:], s_R[:], sSg[:])
                    bar()
                dump("carry", s_R[:])
            _sc.__exit__(None, None, None)
            _sc = nc.named_scope("P4main"); _sc.__enter__()
            with MultiCtx(
                nc.sbuf_tensor("s_prevbf", [128, DIN], BF16),
                nc.sbuf_tensor("s_yng", [128, 32 * 512], BF16),
                nc.sbuf_tensor("xtmA", [128, DIN], BF16),
                nc.sbuf_tensor("xtmB", [128, DIN], BF16),
                nc.sbuf_tensor("xdc", [128, DIN], BF16),
                nc.sbuf_tensor("srB0", [128, 16 * CH], F32),
                nc.sbuf_tensor("srB1", [128, 16 * CH], F32),
                nc.sbuf_tensor("sLW0", [128, 16 * CH], BF16),
                nc.sbuf_tensor("sLW1", [128, 16 * CH], BF16),
                nc.sbuf_tensor("sGt", [128, 128], F32),
                nc.sbuf_tensor("sYtm", [128, 2048], BF16),
                nc.sbuf_tensor("sYo", [128, 2048], BF16),
                nc.sbuf_tensor("syfm", [128, DIN], F32),
                nc.sbuf_tensor("shA", [128, 2048], F32),
                nc.sbuf_tensor("shB", [128, 2048], F32),
                nc.sbuf_tensor("srs", [1, 128], F32),
                nc.sbuf_tensor("swog0", [128, 32 * 128], BF16),
                nc.sbuf_tensor("swog1", [128, 32 * 128], BF16),
                nc.sbuf_tensor("sxrg", [128, 1024], F32),
                nc.sbuf_tensor("sxrb", [128, 1024], BF16),
                nc.sbuf_tensor("sxrb2", [128, 1024], BF16),
                nc.sbuf_tensor("srowt", [1, 1024], F32),
                nc.sbuf_tensor("s_mask", [128, 128], F32),
                nc.psum_tensor("pYd", [128, 1024], F32),
                nc.psum_tensor("pY2", [128, 1024], F32),
                nc.psum_tensor("pYo", [128, 1024], F32),
                nc.psum_tensor("p4s", [128, 128], F32),
            ) as (s_prevbf, s_yng, xtmA, xtmB, xdc, srB0, srB1, sLW0, sLW1,
                  sGt, sYtm, sYo, syfm, shA, shB, srs, swog0, swog1,
                  sxrg, sxrb, sxrb2, srowt, s_mask, pYd, pY2, pYo, p4s):
                nc.vector.memset(s_mask[:], 0.0)
                bar()
                nc.gpsimd.affine_select(s_mask[:], s_mask[:], [[1, 128]],
                                        OP.is_ge, NEG, base=0,
                                        channel_multiplier=-1)
                bar()
                wov = d_woutT.ap().rearrange("(k p) m -> p k m", p=128)
                smk = []
                for m in range(16):
                    buf = (swog0, swog1)[m % 2]
                    wsm = wsp[m % 2]
                    if m >= 2:
                        nc.gpsimd.wait_ge(ss.h, smk[m - 2])
                    dma(nc.gpsimd, buf[:].rearrange("p (k m) -> p k m", m=128),
                        wov[:, :, m * 128:(m + 1) * 128], sem=wsm)
                    nc.vector.wait_ge(wsm.h, wsm.n)
                    zs.inc(nc.vector.tensor_tensor(
                        buf[:].rearrange("p (k m) -> p k m", m=128),
                        buf[:].rearrange("p (k m) -> p k m", m=128),
                        s_nw[:].rearrange("p (k o) -> p k o", o=1)
                        .broadcast_to([128, 32, 128]), OP.mult))
                    nc.sync.wait_ge(zs.h, zs.n)
                    dma(nc.sync,
                        d_woutbf[:, m * 128:(m + 1) * 128]
                        .rearrange("(k p) m -> p k m", p=128),
                        buf[:].rearrange("p (k m) -> p k m", m=128), sem=ss)
                    smk.append(ss.n)
                    nc.vector.wait_ge(ss.h, ss.n)
                pc_ss = ss.n
                if "woutbf" in d_dbg:
                    nc.sync.wait_ge(ss.h, ss.n)
                    bar()
                    dump("woutbf", d_woutbf.ap())

                m_wml, m_dad, m_aff, m_exp = {}, {}, {}, {}
                m_mm1, m_mm2, m_yev, m_yov = {}, {}, {}, {}
                m_tb, m_tba, m_gmm, m_sgt = {}, {}, {}, {}
                m_prev, m_radd, m_zml = {}, {}, {}
                m_sq, m_rms, m_ts, m_sqr = {}, {}, {}, {}
                m_rcp, m_bc, m_ynm = {}, {}, {}
                rb_ld, sc_ld, z_ld, xtm_ld, xdc_ld, u_ld = \
                    {}, {}, {}, {}, {}, {}
                m_main, m_sum, m_tta, m_xrb = {}, {}, {}, {}
                m_st, m_rowt, m_lnst = {}, {}, {}
                srBs = (srB0, srB1)
                sLWs = (sLW0, sLW1)
                szq = (sz0, sz1)
                shq = (sh0, sh1)
                shb = (shA, shB)
                uqq = (uq0, uq1)
                sxq = (sx0, sx1)
                xtms = (xtmA, xtmB)
                pYp = (pYd, pY2)

                def W(eng, sem, mark):
                    if mark is not None:
                        eng.wait_ge(sem.h, mark)

                def ld_srb(c, q):
                    W(nc.sync, cs, m_wml.get((c, q - 2)) if q >= 2
                      else m_wml.get((c - 1, q + 2)))
                    dma(nc.sync, srBs[q % 2][:],
                        d_acrow[c:c + 1, q * 16 * CH:(q + 1) * 16 * CH]
                        .broadcast_to([128, 16 * CH]), sem=szq[q % 2])
                    rb_ld[(c, q)] = szq[q % 2].n

                def ld_sc(c, h):
                    W(nc.sync, gsm, m_zml.get((c - 1, h)))
                    dma(nc.sync, shb[h][:],
                        d_scsc[c][:, h * 2048:(h + 1) * 2048], sem=shq[h])
                    sc_ld[(c, h)] = shq[h].n

                def ld_z(c, h):
                    W(nc.sync, cs, m_radd[(c, h)])
                    dma(nc.sync,
                        shb[h][:].rearrange("p (i t) -> p i t", t=128),
                        d_zsc[h * 2048:(h + 1) * 2048, c * CH:(c + 1) * CH]
                        .rearrange("(i p) t -> p i t", p=128), sem=shq[h])
                    z_ld[(c, h)] = shq[h].n

                def ld_xtm(c):
                    W(nc.scalar, ps, m_mm1.get((c - 2, 3)))
                    dma(nc.scalar, xtms[c % 2][:], d_xsc[c], transpose=True,
                        sem=sxq[c % 2])
                    xtm_ld[c] = sxq[c % 2].n

                def ld_xdc(c):
                    W(nc.scalar, cs, m_tba.get((c - 1, 1, 1)))
                    dma(nc.scalar,
                        xdc[:].rearrange("p (i t) -> p i t", t=128),
                        d_xdsc[c].rearrange("(i p) t -> p i t", p=128),
                        sem=sxc)
                    xdc_ld[c] = sxc.n

                def g_mm(c):
                    W(nc.tensor, es, m_sgt.get(c - 1))
                    ps.inc(nc.tensor.matmul(
                        p4s[:], s_Bbf[:, c * CH:(c + 1) * CH],
                        s_Cbf[:, c * CH:(c + 1) * CH], start=True, stop=True))
                    m_gmm[c] = ps.n
                    W(nc.scalar, ps, m_gmm[c])
                    W(nc.scalar, cs, m_wml.get((c - 1, 3)))
                    es.inc(nc.scalar.copy(sGt[:], p4s[:]))
                    m_sgt[c] = es.n

                def wchain(c, q):
                    srq, slw = srBs[q % 2], sLWs[q % 2]
                    W(nc.vector, szq[q % 2], rb_ld[(c, q)])
                    cs.inc(nc.vector.tensor_tensor(
                        srq[:].rearrange("s (h l) -> s h l", l=CH),
                        srq[:].rearrange("s (h l) -> s h l", l=CH),
                        s_colTa[:, c * H + q * 16:c * H + q * 16 + 16]
                        .rearrange("s (h o) -> s h o", o=1)
                        .broadcast_to([128, 16, CH]), OP.add))
                    m_dad[(c, q)] = cs.n
                    W(nc.gpsimd, cs, m_dad[(c, q)])
                    gsm.inc(nc.gpsimd.tensor_tensor(
                        srq[:].rearrange("s (h l) -> s h l", l=CH),
                        srq[:].rearrange("s (h l) -> s h l", l=CH),
                        s_mask[:].rearrange("s (o l) -> s o l", o=1)
                        .broadcast_to([128, 16, CH]), OP.add))
                    m_aff[(c, q)] = gsm.n
                    W(nc.scalar, gsm, m_aff[(c, q)])
                    es.inc(nc.scalar.activation(srq[:], srq[:], AF.Exp))
                    m_exp[(c, q)] = es.n
                    W(nc.vector, es, m_exp[(c, q)])
                    if q == 0:
                        W(nc.vector, es, m_sgt[c])
                    W(nc.vector, ps, m_mm1.get((c, q - 2)) if q >= 2
                      else m_mm1.get((c - 1, q + 2)))
                    cs.inc(nc.vector.tensor_tensor(
                        slw[:].rearrange("s (h l) -> s h l", l=CH),
                        srq[:].rearrange("s (h l) -> s h l", l=CH),
                        sGt[:].rearrange("s (o l) -> s o l", o=1)
                        .broadcast_to([128, 16, CH]), OP.mult))
                    m_wml[(c, q)] = cs.n
                    if "wt" in d_dbg and c == 0 and q == 0:
                        bar()
                        dump("wt", slw[:])
                    nq_c, nq_q = (c, q + 2) if q < 2 else (c + 1, q - 2)
                    if nq_c < NCH:
                        ld_srb(nq_c, nq_q)

                def mm_q(c, q):
                    grp, c4 = c // 4, c % 4
                    qq = q % 2
                    slw = sLWs[q % 2]
                    xtm = xtms[c % 2]
                    pYb = pYp[q % 2]
                    W(nc.tensor, cs, m_wml[(c, q)])
                    if q < 2:
                        W(nc.tensor, cs, m_tba.get((c - 1, 1, q)))
                    else:
                        W(nc.tensor, cs, m_tba.get((c, 0, q - 2)))
                    if q == 0:
                        W(nc.tensor, sxq[c % 2], xtm_ld[c])
                        if c4 == 0:
                            W(nc.tensor, cs, m_tta.get((grp - 1, 15)))
                        if q == 1 and c4 == 0:
                            pass
                    for hh in range(16):
                        h = q * 16 + hh
                        _mm = nc.tensor.matmul(
                            pYb[:, hh * 64:(hh + 1) * 64],
                            slw[:, hh * CH:(hh + 1) * CH],
                            xtm[:, h * PH:(h + 1) * PH],
                            start=True, stop=True)
                    ps.inc(_mm)
                    m_mm1[(c, q)] = ps.n
                    W(nc.tensor, cs, m_yov.get((c, q - 1)) if q > 0
                      else m_yov.get((c - 1, 3)))
                    if q == 0:
                        W(nc.tensor, es, m_prev[c])
                        W(nc.tensor, cs, m_ynm.get(c - 1))
                        if c4 == 0:
                            W(nc.tensor, cs, m_rowt.get(grp - 1))
                    for hh in range(16):
                        h = q * 16 + hh
                        _mm = nc.tensor.matmul(
                            pYo[:, hh * 64:(hh + 1) * 64],
                            s_Cbf[:, c * CH:(c + 1) * CH],
                            s_prevbf[:, h * PH:(h + 1) * PH],
                            start=True, stop=True)
                    ps.inc(_mm)
                    m_mm2[(c, q)] = ps.n
                    W(nc.scalar, ps, m_mm1[(c, q)])
                    W(nc.scalar, ps, m_tb[(c, 0, q - 2)] if q >= 2
                      else m_tb.get((c - 1, 1, q)))
                    es.inc(nc.scalar.copy(
                        sYtm[:, qq * 1024:(qq + 1) * 1024], pYb[:]))
                    m_yev[(c, q)] = es.n
                    W(nc.vector, ps, m_mm2[(c, q)])
                    W(nc.vector, ps, m_tb[(c, 0, q - 2)] if q >= 2
                      else m_tb.get((c - 1, 1, q)))
                    cs.inc(nc.vector.tensor_tensor(
                        sYo[:, qq * 1024:(qq + 1) * 1024]
                        .rearrange("s (hh p) -> s hh p", p=PH),
                        pYo[:].rearrange("s (hh p) -> s hh p", p=PH),
                        s_evTa[:, c * H + q * 16:c * H + q * 16 + 16]
                        .rearrange("s (h o) -> s h o", o=1)
                        .broadcast_to([128, 16, PH]), OP.mult))
                    m_yov[(c, q)] = cs.n

                def tb_hf(c, hf):
                    for b in range(2):
                        q = hf * 2 + b
                        W(nc.tensor, es, m_yev[(c, q)])
                        W(nc.tensor, cs, m_yov[(c, q)])
                        for i8 in range(8):
                            i = b * 8 + i8
                            nc.tensor.matmul(
                                pYp[b][:, i8 * 128:(i8 + 1) * 128],
                                sYtm[:, i * 128:(i + 1) * 128],
                                s_ibf[:], start=True, stop=False)
                            _mm = nc.tensor.matmul(
                                pYp[b][:, i8 * 128:(i8 + 1) * 128],
                                sYo[:, i * 128:(i + 1) * 128],
                                s_ibf[:], start=False, stop=True)
                        ps.inc(_mm)
                        m_tb[(c, hf, b)] = ps.n
                        W(nc.vector, ps, m_tb[(c, hf, b)])
                        if b == 0 and hf == 0:
                            W(nc.vector, sxc, xdc_ld[c])
                        cs.inc(nc.vector.tensor_add(
                            syfm[:, (hf * 16 + b * 8) * 128:
                                 (hf * 16 + b * 8 + 8) * 128],
                            pYp[b][:],
                            xdc[:, (hf * 16 + b * 8) * 128:
                                (hf * 16 + b * 8 + 8) * 128]))
                        m_tba[(c, hf, b)] = cs.n

                def zmul(c, h):
                    W(nc.gpsimd, shq[h], z_ld[(c, h)])
                    W(nc.gpsimd, cs, m_tba[(c, h, 1)])
                    gsm.inc(nc.gpsimd.tensor_tensor(
                        syfm[:, h * 2048:(h + 1) * 2048],
                        syfm[:, h * 2048:(h + 1) * 2048],
                        shb[h][:], OP.mult))
                    m_zml[(c, h)] = gsm.n

                def prev_copy(c):
                    if c > 0:
                        W(nc.scalar, ps, m_mm2[(c - 1, 3)])
                        W(nc.scalar, cs, m_radd[(c - 1, 1)])
                    es.inc(nc.scalar.copy(s_prevbf[:], s_R[:]))
                    m_prev[c] = es.n

                def r_update(c):
                    W(nc.vector, es, m_prev[c])
                    cs.inc(nc.vector.tensor_tensor(
                        s_R[:].rearrange("n (h p) -> n h p", p=PH),
                        s_R[:].rearrange("n (h p) -> n h p", p=PH),
                        s_dcb[:, c * H:(c + 1) * H]
                        .rearrange("n (h o) -> n h o", o=1)
                        .broadcast_to([128, H, PH]), OP.mult))
                    for h in range(2):
                        W(nc.vector, shq[h], sc_ld[(c, h)])
                        cs.inc(nc.vector.tensor_add(
                            s_R[:, h * 2048:(h + 1) * 2048],
                            s_R[:, h * 2048:(h + 1) * 2048], shb[h][:]))
                        m_radd[(c, h)] = cs.n
                        ld_z(c, h)

                def out_proj(g, c_tail):
                    wobv = d_woutbf.ap().rearrange("(k p) m -> p k m", p=128)
                    if "ynorm" in d_dbg and g == 0:
                        bar()
                        dump("ynorm", s_yng[:])
                    for mm in range(2):
                        W(nc.scalar, es, m_xrb.get((g - 1, 14 + mm)))
                        W(nc.scalar, srx, m_st.get((g - 1, 14 + mm)))
                        dma(nc.scalar, sxrg[:, mm * 512:(mm + 1) * 512],
                            d_uT[mm * 128:(mm + 1) * 128,
                                 3 + g * 512:3 + (g + 1) * 512],
                            sem=uqq[mm])
                        u_ld[(g, mm)] = uqq[mm].n
                    W(nc.gpsimd, ps, m_rms[c_tail])
                    if g == 0:
                        nc.gpsimd.wait_ge(ss.h, pc_ss)
                    dma(nc.gpsimd,
                        swog0[:].rearrange("p (k m) -> p k m", m=128),
                        wobv[:, :, 0:128], sem=wsp[0])
                    wmk = {0: wsp[0].n}
                    for m in range(16):
                        hb = m % 2
                        buf = (swog0, swog1)[hb]
                        if m + 1 < 16:
                            W(nc.gpsimd, ps, m_main.get((g, m - 1)))
                            dma(nc.gpsimd,
                                (swog0, swog1)[(m + 1) % 2][:]
                                .rearrange("p (k m) -> p k m", m=128),
                                wobv[:, :, (m + 1) * 128:(m + 2) * 128],
                                sem=wsp[(m + 1) % 2])
                            wmk[m + 1] = wsp[(m + 1) % 2].n
                        W(nc.tensor, wsp[hb], wmk[m])
                        W(nc.tensor, cs, m_tta.get((g, m - 2)))
                        if m <= 1:
                            W(nc.tensor, cs, m_ynm[c_tail])
                            W(nc.tensor, cs, m_tba[(c_tail, 1, 0)])
                        for k in range(32):
                            _mm = nc.tensor.matmul(
                                pYd[:, hb * 512:(hb + 1) * 512],
                                buf[:, k * 128:(k + 1) * 128],
                                s_yng[:, k * 512:(k + 1) * 512],
                                start=(k == 0), stop=(k == 31))
                        ps.inc(_mm)
                        m_main[(g, m)] = ps.n
                        if m >= 1:
                            pb = (m - 1) % 2
                            W(nc.tensor, es, m_xrb[(g, m - 1)])
                            nc.tensor.matmul(
                                pYo[0:1, 0:512], s_onesb[:],
                                sxrb[:, pb * 512:(pb + 1) * 512],
                                start=(m - 1 == 0), stop=False)
                            ps.inc(nc.tensor.matmul(
                                pYo[0:1, 512:1024], s_onesb[:],
                                sxrb2[:, pb * 512:(pb + 1) * 512],
                                start=(m - 1 == 0), stop=False))
                            m_sum[(g, m - 1)] = ps.n
                        W(nc.vector, uqq[hb], u_ld[(g, m)])
                        W(nc.vector, ps, m_main[(g, m)])
                        cs.inc(nc.vector.tensor_add(
                            sxrg[:, hb * 512:(hb + 1) * 512],
                            sxrg[:, hb * 512:(hb + 1) * 512],
                            pYd[:, hb * 512:(hb + 1) * 512]))
                        m_tta[(g, m)] = cs.n
                        W(nc.sync, cs, m_tta[(g, m)])
                        dma(nc.sync,
                            d_xres[m * 128:(m + 1) * 128,
                                   g * 512:(g + 1) * 512],
                            sxrg[:, hb * 512:(hb + 1) * 512], sem=srx)
                        m_st[(g, m)] = srx.n
                        W(nc.scalar, cs, m_tta[(g, m)])
                        nc.scalar.activation(
                            sxrb[:, hb * 512:(hb + 1) * 512],
                            sxrg[:, hb * 512:(hb + 1) * 512], AF.Copy)
                        es.inc(nc.scalar.activation(
                            sxrb2[:, hb * 512:(hb + 1) * 512],
                            sxrg[:, hb * 512:(hb + 1) * 512], AF.Square))
                        m_xrb[(g, m)] = es.n
                        if m + 2 < 16:
                            W(nc.scalar, es, m_xrb[(g, m)])
                            W(nc.scalar, srx, m_st[(g, m)])
                            dma(nc.scalar,
                                sxrg[:, hb * 512:(hb + 1) * 512],
                                d_uT[(m + 2) * 128:(m + 3) * 128,
                                     3 + g * 512:3 + (g + 1) * 512],
                                sem=uqq[hb])
                            u_ld[(g, m + 2)] = uqq[hb].n
                    W(nc.tensor, es, m_xrb[(g, 15)])
                    nc.tensor.matmul(pYo[0:1, 0:512], s_onesb[:],
                                     sxrb[:, 512:1024], start=False,
                                     stop=True)
                    ps.inc(nc.tensor.matmul(pYo[0:1, 512:1024], s_onesb[:],
                                            sxrb2[:, 512:1024], start=False,
                                            stop=True))
                    m_sum[(g, 15)] = ps.n
                    W(nc.vector, ps, m_sum[(g, 15)])
                    W(nc.vector, ss, m_lnst.get(g - 1))
                    nc.vector.tensor_copy(srowt[0:1, 0:512],
                                          pYo[0:1, 0:512])
                    cs.inc(nc.vector.tensor_copy(srowt[0:1, 512:1024],
                                                 pYo[0:1, 512:1024]))
                    m_rowt[g] = cs.n
                    W(nc.sync, cs, m_rowt[g])
                    dma(nc.sync, d_lnrow[0:1, g * 512:(g + 1) * 512],
                        srowt[0:1, 0:512], sem=ss)
                    dma(nc.sync, d_lnrow[1:2, g * 512:(g + 1) * 512],
                        srowt[0:1, 512:1024], sem=ss)
                    m_lnst[g] = ss.n

                ld_xtm(0)
                ld_xtm(1)
                ld_xdc(0)
                ld_sc(0, 0)
                ld_sc(0, 1)
                ld_srb(0, 0)
                ld_srb(0, 1)
                prev_copy(0)
                g_mm(0)
                wchain(0, 0)
                wchain(0, 1)
                r_update(0)

                for c in range(NCH):
                    grp, c4 = c // 4, c % 4
                    dbgy = ("yfm" in d_dbg and c == 0) or \
                        ("yfm5" in d_dbg and c == 5)
                    mm_q(c, 0)
                    wchain(c, 2)
                    mm_q(c, 1)
                    tb_hf(c, 0)
                    wchain(c, 3)
                    mm_q(c, 2)
                    if not dbgy:
                        zmul(c, 0)
                    if c + 1 < NCH:
                        ld_sc(c + 1, 0)
                    mm_q(c, 3)
                    if c + 1 < NCH:
                        prev_copy(c + 1)
                    tb_hf(c, 1)
                    if dbgy:
                        bar()
                        dump("yfm" if c == 0 else "yfm5", syfm[:])
                        zmul(c, 0)
                    if "prev5" in d_dbg and c == 4:
                        bar()
                        dump("prev5", s_prevbf[:])
                    zmul(c, 1)
                    if c + 1 < NCH:
                        ld_sc(c + 1, 1)
                    W(nc.scalar, gsm, m_zml[(c, 1)])
                    W(nc.scalar, gsm, m_zml[(c, 0)])
                    if c4 == 0:
                        W(nc.scalar, ps, m_main.get((grp - 1, 15)))
                    else:
                        W(nc.scalar, ps, m_rms.get(c - 1))
                    if c == 0:
                        nc.scalar.wait_ge(ss.h, pc_ss)
                    es.inc(nc.scalar.activation(swog0[:], syfm[:],
                                                AF.Square))
                    m_sq[c] = es.n
                    W(nc.tensor, es, m_sq[c])
                    W(nc.tensor, cs, m_yov[(c, 3)])
                    if c4 == 0:
                        W(nc.tensor, cs, m_rowt.get(grp - 1))
                    for i in range(32):
                        _mm = nc.tensor.matmul(
                            pYo[0:1, 128:256], s_onesb[:],
                            swog0[:, i * 128:(i + 1) * 128],
                            start=(i == 0), stop=(i == 31))
                    ps.inc(_mm)
                    m_rms[c] = ps.n
                    W(nc.vector, ps, m_rms[c])
                    W(nc.vector, ps, m_bc.get(c - 1))
                    cs.inc(nc.vector.tensor_scalar(
                        srs[:], pYo[0:1, 128:256], 1.0 / DIN, EPS,
                        OP.mult, OP.add))
                    m_ts[c] = cs.n
                    W(nc.scalar, cs, m_ts[c])
                    es.inc(nc.scalar.activation(srs[:], srs[:], AF.Sqrt))
                    m_sqr[c] = es.n
                    W(nc.vector, es, m_sqr[c])
                    cs.inc(nc.vector.reciprocal(srs[:], srs[:]))
                    m_rcp[c] = cs.n
                    W(nc.tensor, cs, m_rcp[c])
                    ps.inc(nc.tensor.matmul(pYo[:, 0:128], s_ones1[:],
                                            srs[:], start=True, stop=True))
                    m_bc[c] = ps.n
                    W(nc.vector, ps, m_bc[c])
                    if c4 == 0:
                        W(nc.vector, ps, m_main.get((grp - 1, 15)))
                    cs.inc(nc.vector.tensor_tensor(
                        s_yng[:].rearrange("p (i t) -> p i t", t=512)
                        [:, :, c4 * 128:(c4 + 1) * 128],
                        syfm[:].rearrange("p (i t) -> p i t", t=128),
                        pYo[:, 0:128].rearrange("p (o t) -> p o t", o=1)
                        .broadcast_to([128, 32, 128]), OP.mult))
                    m_ynm[c] = cs.n
                    if c + 1 < NCH:
                        g_mm(c + 1)
                        wchain(c + 1, 0)
                        wchain(c + 1, 1)
                        if c + 2 < NCH:
                            ld_xtm(c + 2)
                        ld_xdc(c + 1)
                        r_update(c + 1)
                    if c4 == 3:
                        out_proj(grp, c)
                nc.sync.wait_ge(srx.h, srx.n)
                nc.sync.wait_ge(ss.h, ss.n)
                bar()
                dump("xres", d_xres.ap())
            _sc.__exit__(None, None, None)

        _sR.__exit__(None, None, None)

    if STAGE >= 6:
        _sc = nc.named_scope("P6"); _sc.__enter__()
        _p67 = MultiCtx(
            nc.sbuf_tensor("s_xp1", [128, 16 * 2048], BF16),
            nc.sbuf_tensor("s_mu1", [1, TOK], F32),
            nc.sbuf_tensor("s_rs1", [1, TOK], F32),
        )
        s_xp1, s_mu1, s_rs1 = _p67.__enter__()
        with MultiCtx(
            nc.sbuf_tensor("s_mu1b", [128, 2048], F32),
            nc.sbuf_tensor("s_rs1b", [128, 2048], F32),
            nc.sbuf_tensor("sxr", [128, 2048], F32),
            nc.sbuf_tensor("srowa", [1, 2048], F32),
            nc.sbuf_tensor("sg1", [64, 2048], F32),
            nc.sbuf_tensor("sltm", [128, 16 * 64], F32),
            nc.sbuf_tensor("sxr2", [128, 2048], F32),
            nc.sbuf_tensor("gt0a", [128, 1024], F32),
            nc.sbuf_tensor("gt1a", [128, 1024], F32),
            nc.sbuf_tensor("gmsk", [128, 1024], F32),
            nc.sbuf_tensor("gt2a", [128, 128], F32),
            nc.sbuf_tensor("gt3a", [128, 128], F32),
            nc.sbuf_tensor("gt4a", [128, 128], F32),
            nc.sbuf_tensor("gcola", [128, 16], F32),
            nc.sbuf_tensor("gstga", [64, 2048], BF16),
            nc.sbuf_tensor("gscfa", [64, 2048], F32),
            nc.psum_tensor("p6", [128, 2048], F32),
        ) as (s_mu1b, s_rs1b, sxr, srowa, sg1, sltm, sxr2, gt0a, gt1a,
              gmsk, gt2a, gt3a, gt4a, gcola, gstga, gscfa, p6):
            dma(nc.sync, s_mu1[:], d_lnrow[0:1, :])
            dma(nc.sync, s_rs1[:], d_lnrow[1:2, :])
            dwait()
            bar()
            nc.vector.tensor_scalar_mul(s_mu1[:], s_mu1[:], 1.0 / DM)
            bar()
            nc.vector.tensor_scalar_mul(s_rs1[:], s_rs1[:], 1.0 / DM)
            bar()
            nc.vector.tensor_tensor(srowa[:], s_mu1[:], s_mu1[:], OP.mult)
            bar()
            nc.vector.tensor_sub(s_rs1[:], s_rs1[:], srowa[:])
            bar()
            nc.vector.tensor_scalar(s_rs1[:], s_rs1[:], EPS, None, OP.add)
            bar()
            nc.scalar.activation(s_rs1[:], s_rs1[:], AF.Sqrt)
            bar()
            nc.vector.reciprocal(s_rs1[:], s_rs1[:])
            bar()
            dma(nc.sync, d_murs[0:1, :], s_mu1[:])
            dma(nc.sync, d_murs[1:2, :], s_rs1[:])
            dwait()
            bar()
            for t in range(4):
                nc.tensor.matmul(p6[:, t * 512:(t + 1) * 512], s_ones1[:],
                                 s_mu1[0:1, t * 512:(t + 1) * 512],
                                 start=True, stop=True)
            bar()
            nc.scalar.copy(s_mu1b[:], p6[:])
            bar()
            for t in range(4):
                nc.tensor.matmul(p6[:, t * 512:(t + 1) * 512], s_ones1[:],
                                 s_rs1[0:1, t * 512:(t + 1) * 512],
                                 start=True, stop=True)
            bar()
            nc.scalar.copy(s_rs1b[:], p6[:])
            bar()
            sxr6_ = (sxr, sxr2)
            shq6 = (sh0, sh1)
            xl6, ml6 = {}, {}

            def ld6(m):
                p = m % 2
                if (m - 2) in ml6:
                    nc.sync.wait_ge(cs.h, ml6[m - 2])
                dma(nc.sync, sxr6_[p][:],
                    d_xres[m * 128:(m + 1) * 128, :], sem=shq6[p])
                xl6[m] = shq6[p].n

            ld6(0)
            ld6(1)
            for m in range(16):
                p = m % 2
                nc.vector.wait_ge(shq6[p].h, xl6[m])
                nc.vector.tensor_sub(sxr6_[p][:], sxr6_[p][:], s_mu1b[:])
                cs.inc(nc.vector.tensor_tensor(
                    s_xp1[:, m * 2048:(m + 1) * 2048],
                    sxr6_[p][:], s_rs1b[:], OP.mult))
                ml6[m] = cs.n
                if m + 2 < 16:
                    ld6(m + 2)
            bar()
            dump("xp1", s_xp1[:])
            for t in range(4):
                for k in range(16):
                    nc.tensor.matmul(
                        p6[0:64, t * 512:(t + 1) * 512],
                        s_gateT[:, k * 64:(k + 1) * 64],
                        s_xp1[:, k * 2048 + t * 512:k * 2048 + (t + 1) * 512],
                        start=(k == 0), stop=(k == 15))
            bar()
            nc.vector.tensor_copy(sg1[:], p6[0:64, :])
            bar()
            for t in range(4):
                nc.tensor.matmul(p6[0:64, t * 512:(t + 1) * 512], s_PT[:],
                                 sg1[:, t * 512:(t + 1) * 512],
                                 start=True, stop=True)
            bar()
            nc.vector.tensor_copy(sg1[:], p6[0:64, :])
            bar()
            for i in range(16):
                nc.tensor.matmul(p6[:, i * 64:(i + 1) * 64],
                                 sg1[:, i * 128:(i + 1) * 128], s_i64[:],
                                 start=True, stop=True)
            bar()
            nc.scalar.copy(sltm[:], p6[:, 0:1024])
            bar()
            dump("scoretm", sltm[:])
            sl3 = sltm[:].rearrange("p (i e) -> p i e", e=64)
            g03 = gt0a[:].rearrange("p (i e) -> p i e", e=64)
            g13 = gt1a[:].rearrange("p (i e) -> p i e", e=64)
            gc3 = gcola[:].rearrange("p (i o) -> p i o", o=1)
            nc.vector.tensor_reduce(gcola[:], sl3, AX.X, OP.max, negate=True)
            bar()
            nc.vector.tensor_tensor(g03, sl3,
                                    gc3.broadcast_to([128, 16, 64]), OP.add)
            bar()
            nc.scalar.activation(gt0a[:], gt0a[:], AF.Exp)
            bar()
            nc.vector.tensor_reduce(gcola[:], g03, AX.X, OP.add)
            bar()
            nc.vector.reciprocal(gcola[:], gcola[:])
            bar()
            nc.vector.tensor_tensor(g03, g03,
                                    gc3.broadcast_to([128, 16, 64]), OP.mult)
            bar()
            for i in range(16):
                nc.vector.max(out=gt2a[:, i * 8:(i + 1) * 8],
                              in_=gt0a[:, i * 64:(i + 1) * 64])
            bar()
            nc.vector.tensor_tensor(
                g13, g03,
                gt2a[:].rearrange("p (i e) -> p i e", e=8)[:, :, 7:8]
                .broadcast_to([128, 16, 64]), OP.is_ge)
            bar()
            nc.vector.tensor_mul(gt1a[:], gt1a[:], gt0a[:])
            bar()
            nc.vector.tensor_reduce(
                gt3a[:], gt1a[:].rearrange("p (x e) -> p x e", e=8), AX.X,
                OP.add)
            bar()
            nc.vector.tensor_scalar(gmsk[:], gt1a[:], 0.0, None, OP.is_gt)
            bar()
            nc.vector.scalar_tensor_tensor(gt1a[:], gt1a[:], 100.0, gmsk[:],
                                           OP.add, OP.mult)
            bar()
            nc.vector.tensor_scalar(gt1a[:], gt1a[:], 100.0, None,
                                    OP.subtract)
            bar()
            nc.vector.tensor_reduce(
                gt4a[:], gt1a[:].rearrange("p (x e) -> p x e", e=8), AX.X,
                OP.max, negate=True)
            bar()
            nc.vector.tensor_tensor(
                gt1a[:].rearrange("p (x e) -> p x e", e=8),
                gt1a[:].rearrange("p (x e) -> p x e", e=8),
                gt4a[:].rearrange("p (x o) -> p x o", o=1)
                .broadcast_to([128, 128, 8]), OP.add)
            bar()
            nc.scalar.activation(gt1a[:], gt1a[:], AF.Exp)
            bar()
            nc.vector.tensor_reduce(
                gt4a[:], gt1a[:].rearrange("p (x e) -> p x e", e=8), AX.X,
                OP.add)
            bar()
            nc.vector.reciprocal(gt4a[:], gt4a[:])
            bar()
            nc.vector.tensor_tensor(
                gt1a[:].rearrange("p (x e) -> p x e", e=8),
                gt1a[:].rearrange("p (x e) -> p x e", e=8),
                gt4a[:].rearrange("p (x o) -> p x o", o=1)
                .broadcast_to([128, 128, 8]), OP.mult)
            bar()
            for i in range(16):
                nc.tensor.matmul(p6[0:64, i * 128:(i + 1) * 128],
                                 gt1a[:, i * 64:(i + 1) * 64], s_i128[:],
                                 start=True, stop=True)
            bar()
            nc.scalar.copy(gstga[:], p6[0:64, :])
            bar()
            for i in range(16):
                nc.tensor.matmul(p6[0:8, i * 128:(i + 1) * 128],
                                 gt3a[:, i * 8:(i + 1) * 8], s_i128[:],
                                 start=True, stop=True)
            bar()
            nc.vector.tensor_copy(gscfa[0:8, :], p6[0:8, :])
            bar()
            for g in range(NG):
                dma(nc.sync, d_soft8[:, g, :], gstga[g * 8:(g + 1) * 8, :])
                dma(nc.sync, d_scrow[g:g + 1, :], gscfa[g:g + 1, :])
            dwait()
            bar()
            dump("soft8", d_soft8.ap())
            dump("scrow", d_scrow.ap())
        _sc.__exit__(None, None, None)

        if STAGE >= 7:
            _sc = nc.named_scope("P7"); _sc.__enter__()
            with MultiCtx(
                nc.sbuf_tensor("fw0", [128, 16 * 128], BF16),
                nc.sbuf_tensor("fw1", [128, 16 * 128], BF16),
                nc.sbuf_tensor("sgg", [128, 2048], F32),
                nc.sbuf_tensor("sab", [128, 11 * 2048], BF16),
                nc.sbuf_tensor("sSgb", [128, 2048], F32),
                nc.sbuf_tensor("swpt", [8, 128], BF16),
                nc.sbuf_tensor("syd0", [128, 2048], F32),
                nc.sbuf_tensor("syd1", [128, 2048], F32),
                nc.sbuf_tensor("sydp0", [128, 2048], F32),
                nc.sbuf_tensor("sydp1", [128, 2048], F32),
                nc.sbuf_tensor("s7s", [8, 2048], BF16),
                nc.sbuf_tensor("s7r", [1, 2048], F32),
                nc.psum_tensor("f7a", [128, 1024], F32),
                nc.psum_tensor("f7b", [128, 1024], F32),
                nc.psum_tensor("f7t", [8, 128], F32),
            ) as (fw0, fw1, sgg, sab, sSgb, swpt, syd0, syd1,
                  sydp0, sydp1, s7s, s7r, f7a, f7b, f7t):
                fws = (fw0, fw1)
                f7ms = (f7a, f7b)
                sydps = (sydp0, sydp1)
                y2st, y2pl, y2ev = {}, {}, {}
                fwmk = {0: None, 1: None}
                for g in range(NG):
                    dma(nc.sync, s7s[:], d_soft8[:, g, :])
                    dma(nc.sync, s7r[:], d_scrow[g:g + 1, :])
                    dwait()
                    bar()
                    for t in range(4):
                        nc.tensor.matmul(
                            f7ms[t // 2][:, (t % 2) * 512:(t % 2 + 1) * 512],
                            s_ones1[:], s7r[0:1, t * 512:(t + 1) * 512],
                            start=True, stop=True)
                    bar()
                    nc.scalar.copy(sSgb[:, 0:1024], f7a[:])
                    nc.scalar.copy(sSgb[:, 1024:2048], f7b[:])
                    bar()
                    emk = {0: es.n, 1: es.n}
                    cmk = {0: cs.n, 1: cs.n}
                    zmk = zs.n
                    gvw = d_gpT[g].rearrange("(k p) i -> p k i", p=128)
                    uvw = d_upT[g].rearrange("(k p) i -> p k i", p=128)
                    wvs = (gvw, uvw)
                    wmarks = []
                    if fwmk[0] is not None:
                        nc.gpsimd.wait_ge(ps.h, fwmk[0])
                    dma(nc.gpsimd, fw0[:].rearrange("p (k m) -> p k m", m=128),
                        gvw[:, :, 0:128], sem=wsp[0])
                    wmarks.append((wsp[0], wsp[0].n))
                    for step in range(22):
                        m, proj = step // 2, step % 2
                        buf = fws[step % 2]
                        if step + 1 < 22:
                            m2, p2 = (step + 1) // 2, (step + 1) % 2
                            wsm = wsp[(step + 1) % 2]
                            if fwmk[(step + 1) % 2] is not None:
                                nc.gpsimd.wait_ge(ps.h, fwmk[(step + 1) % 2])
                            dma(nc.gpsimd,
                                fws[(step + 1) % 2][:]
                                .rearrange("p (k m) -> p k m", m=128),
                                wvs[p2][:, :, m2 * 128:(m2 + 1) * 128],
                                sem=wsm)
                            wmarks.append((wsm, wsm.n))
                        nc.tensor.wait_ge(wmarks[step][0].h, wmarks[step][1])
                        nc.tensor.wait_ge(zs.h, zmk)
                        for k in range(16):
                            _mm = nc.tensor.matmul(
                                f7t[:], s_mgT[:, k * 64 + g * 8:
                                              k * 64 + g * 8 + 8],
                                buf[:, k * 128:(k + 1) * 128],
                                start=(k == 0), stop=(k == 15))
                        ps.inc(_mm)
                        nc.vector.wait_ge(ps.h, ps.n)
                        zs.inc(nc.vector.tensor_copy(swpt[:], f7t[:]))
                        zmk = zs.n
                        for half in range(2):
                            pm = f7ms[half]
                            nc.tensor.wait_ge(es.h, emk[half])
                            for k in range(16):
                                for t2 in range(2):
                                    nc.tensor.matmul(
                                        pm[:, t2 * 512:(t2 + 1) * 512],
                                        buf[:, k * 128:(k + 1) * 128],
                                        s_xp1[:, k * 2048 + (half * 2 + t2)
                                              * 512:k * 2048 +
                                              (half * 2 + t2 + 1) * 512],
                                        start=(k == 0), stop=False)
                            if half == 0:
                                nc.tensor.wait_ge(zs.h, zs.n)
                            for t2 in range(2):
                                _mm = nc.tensor.matmul(
                                    pm[:, t2 * 512:(t2 + 1) * 512],
                                    swpt[:],
                                    s7s[:, (half * 2 + t2) * 512:
                                        (half * 2 + t2 + 1) * 512],
                                    start=False, stop=True)
                            ps.inc(_mm)
                            if proj == 0:
                                nc.scalar.wait_ge(ps.h, ps.n)
                                nc.scalar.wait_ge(cs.h, cmk[half])
                                es.inc(nc.scalar.activation(
                                    sgg[:, half * 1024:(half + 1) * 1024],
                                    pm[:], AF.Silu))
                                emk[half] = es.n
                            else:
                                nc.vector.wait_ge(ps.h, ps.n)
                                es.inc(nc.vector.tensor_mul(
                                    sgg[:, half * 1024:(half + 1) * 1024],
                                    sgg[:, half * 1024:(half + 1) * 1024],
                                    pm[:]))
                                emk[half] = es.n
                                nc.gpsimd.wait_ge(es.h, es.n)
                                cs.inc(nc.gpsimd.tensor_tensor(
                                    sab[:, m * 2048 + half * 1024:
                                        m * 2048 + (half + 1) * 1024],
                                    sgg[:, half * 1024:(half + 1) * 1024],
                                    sSgb[:, half * 1024:(half + 1) * 1024],
                                    OP.mult))
                                cmk[half] = cs.n
                        fwmk[step % 2] = ps.n
                    nc.tensor.wait_ge(cs.h, cs.n)
                    wv = d_dnT[g].rearrange("(k p) m -> p k m", p=128)
                    wmarks = []
                    if fwmk[0] is not None:
                        nc.gpsimd.wait_ge(ps.h, fwmk[0])
                    dma(nc.gpsimd,
                        fw0[:, 0:11 * 128].rearrange("p (k m) -> p k m", m=128),
                        wv[:, :, 0:128], sem=wsp[0])
                    wmarks.append((wsp[0], wsp[0].n))
                    def ld_y2p(g, m):
                        if g == 0 or m >= 16:
                            return
                        W7 = nc.sync.wait_ge
                        if (g, m - 2) in y2ev:
                            W7(es.h, y2ev[(g, m - 2)])
                        W7(srx.h, y2st[(g - 1, m)])
                        dma(nc.sync, sydps[m % 2][:],
                            d_y2[m * 128:(m + 1) * 128, :], sem=(sh0, sh1)[m % 2])
                        y2pl[(g, m)] = (sh0, sh1)[m % 2].n

                    ld_y2p(g, 0)
                    ld_y2p(g, 1)
                    for m in range(16):
                        buf = fws[m % 2]
                        syd = (syd0, syd1)[m % 2]
                        sydp = sydps[m % 2]
                        if m + 1 < 16:
                            wsm = wsp[(m + 1) % 2]
                            if fwmk[(m + 1) % 2] is not None:
                                nc.gpsimd.wait_ge(ps.h, fwmk[(m + 1) % 2])
                            dma(nc.gpsimd,
                                fws[(m + 1) % 2][:, 0:11 * 128]
                                .rearrange("p (k m) -> p k m", m=128),
                                wv[:, :, (m + 1) * 128:(m + 2) * 128],
                                sem=wsm)
                            wmarks.append((wsm, wsm.n))
                        nc.tensor.wait_ge(wmarks[m][0].h, wmarks[m][1])
                        for half in range(2):
                            pm = f7ms[half]
                            nc.tensor.wait_ge(es.h, emk[half])
                            for k in range(11):
                                for t2 in range(2):
                                    _mm = nc.tensor.matmul(
                                        pm[:, t2 * 512:(t2 + 1) * 512],
                                        buf[:, k * 128:(k + 1) * 128],
                                        sab[:, k * 2048 + (half * 2 + t2)
                                            * 512:k * 2048 +
                                            (half * 2 + t2 + 1) * 512],
                                        start=(k == 0), stop=(k == 10))
                            ps.inc(_mm)
                            if g == 0:
                                nc.scalar.wait_ge(ps.h, ps.n)
                                if (g, m - 2) in y2st:
                                    nc.scalar.wait_ge(srx.h,
                                                      y2st[(g, m - 2)])
                                es.inc(nc.scalar.copy(
                                    syd[:, half * 1024:(half + 1) * 1024],
                                    pm[:]))
                                emk[half] = es.n
                            else:
                                nc.vector.wait_ge(ps.h, ps.n)
                                if half == 0:
                                    nc.vector.wait_ge(
                                        (sh0, sh1)[m % 2].h, y2pl[(g, m)])
                                    if (g, m - 2) in y2st:
                                        nc.vector.wait_ge(
                                            srx.h, y2st[(g, m - 2)])
                                es.inc(nc.vector.tensor_add(
                                    syd[:, half * 1024:(half + 1) * 1024],
                                    pm[:],
                                    sydp[:, half * 1024:(half + 1) * 1024]))
                                emk[half] = es.n
                                if half == 1:
                                    y2ev[(g, m)] = es.n
                        fwmk[m % 2] = ps.n
                        nc.sync.wait_ge(es.h, es.n)
                        dma(nc.sync, d_y2[m * 128:(m + 1) * 128, :],
                            syd[:], sem=srx)
                        y2st[(g, m)] = srx.n
                        ld_y2p(g, m + 2)
                nc.sync.wait_ge(srx.h, srx.n)
                bar()
                dump("y2", d_y2.ap())
            _sc.__exit__(None, None, None)
        _p67.__exit__(None, None, None)

    if STAGE >= 8:
        _sc = nc.named_scope("P8"); _sc.__enter__()
        with MultiCtx(
            nc.sbuf_tensor("sxrA", [128, 2048], F32),
            nc.sbuf_tensor("sxrB", [128, 2048], F32),
            nc.sbuf_tensor("sy2A", [128, 2048], F32),
            nc.sbuf_tensor("sy2B", [128, 2048], F32),
            nc.sbuf_tensor("st2A", [128, 2048], F32),
            nc.sbuf_tensor("st2B", [128, 2048], F32),
            nc.sbuf_tensor("sxbA", [128, 2048], BF16),
            nc.sbuf_tensor("sxbB", [128, 2048], BF16),
            nc.sbuf_tensor("sxqA", [128, 2048], BF16),
            nc.sbuf_tensor("sxqB", [128, 2048], BF16),
            nc.sbuf_tensor("s_m1b", [128, 2048], F32),
            nc.sbuf_tensor("s_r1b", [128, 2048], F32),
            nc.sbuf_tensor("s_m2b", [128, 2048], F32),
            nc.sbuf_tensor("s_r2b", [128, 2048], F32),
            nc.sbuf_tensor("srow8", [1, 2 * 2048], F32),
        ) as (sxrA, sxrB, sy2A, sy2B, st2A, st2B, sxbA, sxbB, sxqA, sxqB,
              s_m1b, s_r1b, s_m2b, s_r2b, srow8):
            sxr_ = (sxrA, sxrB)
            sy2_ = (sy2A, sy2B)
            st2_ = (st2A, st2B)
            sxb_ = (sxbA, sxbB)
            sxq_ = (sxqA, sxqB)
            shq8 = (sh0, sh1)
            uqq8 = (uq0, uq1)
            xl8, yl8 = {}, {}
            m_ml8, m_ad8, m_cp8, m_sum8, m_st8 = {}, {}, {}, {}, {}

            def W8(eng, sem, mark):
                if mark is not None:
                    eng.wait_ge(sem.h, mark)

            def ld8(pp, m):
                p = m % 2
                W8(nc.sync, gsm, m_ad8.get((pp, m - 2)))
                dma(nc.sync, sxr_[p][:],
                    d_xres[m * 128:(m + 1) * 128, :], sem=shq8[p])
                xl8[(pp, m)] = shq8[p].n
                W8(nc.scalar, gsm, m_ad8.get((pp, m - 2)))
                dma(nc.scalar, sy2_[p][:],
                    d_y2[m * 128:(m + 1) * 128, :], sem=uqq8[p])
                yl8[(pp, m)] = uqq8[p].n

            dma(nc.sync, srow8[0:1, 0:2048], d_murs[0:1, :])
            dma(nc.sync, srow8[0:1, 2048:4096], d_murs[1:2, :])
            dwait()
            bar()
            with nc.psum_tensor("p8c", [128, 2048], F32) as p8c:
                for t in range(4):
                    nc.tensor.matmul(p8c[:, t * 512:(t + 1) * 512],
                                     s_ones1[:],
                                     srow8[0:1, t * 512:(t + 1) * 512],
                                     start=True, stop=True)
                bar()
                nc.scalar.copy(s_m1b[:], p8c[:])
                bar()
                for t in range(4):
                    nc.tensor.matmul(p8c[:, t * 512:(t + 1) * 512],
                                     s_ones1[:],
                                     srow8[0:1, 2048 + t * 512:
                                           2048 + (t + 1) * 512],
                                     start=True, stop=True)
                bar()
                nc.scalar.copy(s_r1b[:], p8c[:])
                bar()
            nc.vector.tensor_mul(s_m1b[:], s_m1b[:], s_r1b[:])
            bar()

            with MultiCtx(
                nc.psum_tensor("p8r1", [1, 2048], F32),
                nc.psum_tensor("p8r2", [1, 2048], F32),
            ) as (p8r1, p8r2):
                ld8(0, 0)
                ld8(0, 1)
                for m in range(16):
                    p = m % 2
                    W8(nc.vector, shq8[p], xl8[(0, m)])
                    nc.vector.tensor_mul(sxr_[p][:], sxr_[p][:], s_r1b[:])
                    cs.inc(nc.vector.tensor_sub(sxr_[p][:], sxr_[p][:],
                                                s_m1b[:]))
                    m_ml8[(0, m)] = cs.n
                    W8(nc.gpsimd, cs, m_ml8[(0, m)])
                    W8(nc.gpsimd, uqq8[p], yl8[(0, m)])
                    W8(nc.gpsimd, es, m_cp8.get((0, m - 2)))
                    gsm.inc(nc.gpsimd.tensor_tensor(
                        st2_[p][:], sxr_[p][:], sy2_[p][:], OP.add))
                    m_ad8[(0, m)] = gsm.n
                    W8(nc.scalar, gsm, m_ad8[(0, m)])
                    W8(nc.scalar, ps, m_sum8.get((0, m - 2)))
                    nc.scalar.activation(sxb_[p][:], st2_[p][:], AF.Copy)
                    es.inc(nc.scalar.activation(sxq_[p][:], st2_[p][:],
                                                AF.Square))
                    m_cp8[(0, m)] = es.n
                    if m + 2 < 16:
                        ld8(0, m + 2)
                    W8(nc.tensor, es, m_cp8[(0, m)])
                    for t in range(4):
                        nc.tensor.matmul(
                            p8r1[0:1, t * 512:(t + 1) * 512], s_onesb[:],
                            sxb_[p][:, t * 512:(t + 1) * 512],
                            start=(m == 0), stop=(m == 15))
                        _mm = nc.tensor.matmul(
                            p8r2[0:1, t * 512:(t + 1) * 512], s_onesb[:],
                            sxq_[p][:, t * 512:(t + 1) * 512],
                            start=(m == 0), stop=(m == 15))
                    ps.inc(_mm)
                    m_sum8[(0, m)] = ps.n
                nc.vector.wait_ge(ps.h, ps.n)
                nc.vector.tensor_copy(srow8[0:1, 0:2048], p8r1[0:1, :])
                bar()
                nc.vector.tensor_copy(srow8[0:1, 2048:4096], p8r2[0:1, :])
                bar()
            nc.vector.tensor_scalar_mul(srow8[0:1, 0:2048],
                                        srow8[0:1, 0:2048], 1.0 / DM)
            bar()
            nc.vector.tensor_scalar_mul(srow8[0:1, 2048:4096],
                                        srow8[0:1, 2048:4096], 1.0 / DM)
            bar()
            nc.vector.tensor_tensor(st2A[0:1, 0:2048], srow8[0:1, 0:2048],
                                    srow8[0:1, 0:2048], OP.mult)
            bar()
            nc.vector.tensor_sub(srow8[0:1, 2048:4096],
                                 srow8[0:1, 2048:4096], st2A[0:1, 0:2048])
            bar()
            nc.vector.tensor_scalar(srow8[0:1, 2048:4096],
                                    srow8[0:1, 2048:4096], EPS, None, OP.add)
            bar()
            nc.scalar.activation(srow8[0:1, 2048:4096],
                                 srow8[0:1, 2048:4096], AF.Sqrt)
            bar()
            nc.vector.reciprocal(srow8[0:1, 2048:4096],
                                 srow8[0:1, 2048:4096])
            bar()
            dump("r8", srow8[:])
            with nc.psum_tensor("p8b", [128, 2048], F32) as p8b:
                for t in range(4):
                    nc.tensor.matmul(p8b[:, t * 512:(t + 1) * 512],
                                     s_ones1[:],
                                     srow8[0:1, t * 512:(t + 1) * 512],
                                     start=True, stop=True)
                bar()
                nc.scalar.copy(s_m2b[:], p8b[:])
                bar()
                for t in range(4):
                    nc.tensor.matmul(p8b[:, t * 512:(t + 1) * 512],
                                     s_ones1[:],
                                     srow8[0:1, 2048 + t * 512:
                                           2048 + (t + 1) * 512],
                                     start=True, stop=True)
                bar()
                nc.scalar.copy(s_r2b[:], p8b[:])
                bar()
            nc.vector.tensor_add(s_m2b[:], s_m2b[:], s_m1b[:])
            bar()

            ld8(1, 0)
            ld8(1, 1)
            for m in range(16):
                p = m % 2
                W8(nc.vector, shq8[p], xl8[(1, m)])
                nc.vector.tensor_mul(sxr_[p][:], sxr_[p][:], s_r1b[:])
                cs.inc(nc.vector.tensor_sub(sxr_[p][:], sxr_[p][:],
                                            s_m2b[:]))
                m_ml8[(1, m)] = cs.n
                W8(nc.gpsimd, cs, m_ml8[(1, m)])
                W8(nc.gpsimd, uqq8[p], yl8[(1, m)])
                W8(nc.gpsimd, ss, m_st8.get((1, m - 2)))
                gsm.inc(nc.gpsimd.tensor_tensor(
                    st2_[p][:], sxr_[p][:], sy2_[p][:], OP.add))
                m_ad8[(1, m)] = gsm.n
                W8(nc.vector, gsm, m_ad8[(1, m)])
                cs.inc(nc.vector.tensor_mul(st2_[p][:], st2_[p][:],
                                            s_r2b[:]))
                m_ml8[(1, m)] = cs.n
                W8(nc.sync, cs, m_ml8[(1, m)])
                dma(nc.sync, d_out[m * 128:(m + 1) * 128, :], st2_[p][:],
                    sem=ss)
                m_st8[(1, m)] = ss.n
                if m + 2 < 16:
                    ld8(1, m + 2)
            nc.sync.wait_ge(ss.h, ss.n)
            bar()
        _sc.__exit__(None, None, None)
    if STAGE < 8:
        with nc.sbuf_tensor("sfin", [128, TOK], F32) as sfin:
            nc.vector.memset(sfin[:], 0.0)
            bar()
            for m in range(16):
                dma(nc.sync, d_out[m * 128:(m + 1) * 128, :], sfin[:],
                    sem=ss)
            nc.sync.wait_ge(ss.h, ss.n)
            bar()
    nc.sync.wait_ge(ds.h, ds.n)
    nc.sync.wait_ge(ss.h, ss.n)
    bar()
    return nc



def _prep(inputs):
    ident = np.eye(128, dtype=ml_dtypes.bfloat16)
    i64 = np.eye(64, dtype=np.float32)
    i128 = np.eye(128, dtype=np.float32)
    onesb = np.ones((128, 1), dtype=ml_dtypes.bfloat16)
    ones1 = np.ones((1, 128), dtype=np.float32)
    W_inT = np.ascontiguousarray(np.asarray(inputs["L0_attn_W_in"]).T)
    woutT = np.ascontiguousarray(np.asarray(inputs["L0_attn_W_out"]).T)
    gatewT = np.ascontiguousarray(np.asarray(inputs["L0_ffn_gate_w"]).T)
    mupT = np.ascontiguousarray(
        np.asarray(inputs["L0_ffn_mask_up_w"]).transpose(1, 0, 2)
        .reshape(DM, NG * 8))
    gpT = np.ascontiguousarray(np.asarray(inputs["L0_ffn_gate_proj_w"])
                               .transpose(0, 2, 1))
    upT = np.ascontiguousarray(np.asarray(inputs["L0_ffn_up_proj_w"])
                               .transpose(0, 2, 1))
    dnT = np.ascontiguousarray(np.asarray(inputs["L0_ffn_down_proj_w"])
                               .transpose(0, 2, 1))
    drep = np.ascontiguousarray(np.repeat(np.asarray(inputs["L0_attn_D"]),
                                          PH))
    invmap = np.ascontiguousarray(
        np.asarray(inputs["L0_ffn_inv_map"]).reshape(-1).astype(np.int32))
    u = np.asarray(inputs["L0_attn_u"])
    shared = dict(
        winT=W_inT, convw=np.ascontiguousarray(inputs["L0_attn_conv_w"]),
        convb=np.ascontiguousarray(inputs["L0_attn_conv_b"]),
        dtbias=np.ascontiguousarray(inputs["L0_attn_dt_bias"]),
        alog=np.ascontiguousarray(inputs["L0_attn_A_log"]),
        drep=drep, nw=np.ascontiguousarray(inputs["L0_attn_norm_w"]),
        woutT=woutT, gatewT=gatewT, invmap=invmap, mupT=mupT,
        gpT=gpT, upT=upT, dnT=dnT, ibf=ident, i64=i64, i128=i128,
        onesb=onesb, ones1=ones1)
    in_maps = []
    for core in range(8):
        b, qq = core // 4, core % 4
        t0 = qq * TOK
        uslice = np.zeros((TOKH, DM), dtype=np.float32)
        lo = t0 - 3
        src = u[b, max(lo, 0):t0 + TOK]
        uslice[TOKH - src.shape[0]:] = src
        uT = np.ascontiguousarray(uslice.T)
        mq = np.zeros((5, 4), dtype=np.float32)
        mq[4, :] = NEG
        for pq in range(4):
            if pq < qq:
                mq[4, pq] = 0.0
                for rq in range(4):
                    if pq < rq < qq:
                        mq[rq, pq] = 1.0
        m = dict(shared)
        m["uT"] = uT
        m["mq"] = mq
        in_maps.append(m)
    return in_maps


def kernel(**inputs):
    in_maps = _prep(inputs)
    nc = build_nc()
    res = run_bass_kernel_spmd(nc, in_maps, core_ids=list(range(8)))
    out = np.zeros((2, 8192, DM), dtype=np.float32)
    for core in range(8):
        b, qq = core // 4, core % 4
        out[b, qq * TOK:(qq + 1) * TOK] = res.results[core]["out"].T
    return out
